# version 11
# baseline (speedup 1.0000x reference)
import math
import numpy as np
import ml_dtypes
from contextlib import ExitStack

import concourse.bass as bass
import concourse.mybir as mybir
from concourse.bass_utils import run_bass_kernel_spmd

F32 = mybir.dt.float32
BF16 = mybir.dt.bfloat16
AF = mybir.ActivationFunctionType
ALU = mybir.AluOpType
AX = mybir.AxisListType
bfd = ml_dtypes.bfloat16

N, D, NT, NCH = 4096, 1024, 32, 8
H, HD, FF = 8, 64, 4096
ETA, DT_, GAMMA_FLOOR, EPS = 0.1, 0.01, 0.1, 1e-5
NHEAD, NACT, DA = 16, 4, 64

CP = {"eps": 0, "neg05": 1, "neg1": 2, "negdt": 3, "half": 4, "negln8": 5,
      "two": 6, "eta": 7, "tenth": 8, "qkscale": 9}
CP_VALS = [EPS, -0.5, -1.0, -DT_, 0.5, -math.log(8.0), 2.0, ETA, 0.1, 0.125]


class Eng:
    def __init__(self, sem, serialize=True):
        self.sem = sem
        self.n = 0
        self.maxw = 0
        self.serialize = serialize
        self.eng = None

    def op(self, f, *a, **kw):
        if self.serialize and self.n > 0:
            self.eng.wait_ge(self.sem, self.n)
        instr = f(*a, **kw)
        self.n += 1
        instr.then_inc(self.sem, 1)
        return self.n

    def wait(self, other, v):
        if v > 0:
            other.maxw = max(other.maxw, v)
            self.eng.wait_ge(other.sem, v)


def build_program(debug=False, upto=99):
    nc = bass.Bass("TRN2", target_bir_lowering=False)
    t = {}

    def din(name, shape, dt=F32):
        t[name] = nc.dram_tensor(name, list(shape), dt, kind="ExternalInput").ap()
        return t[name]

    def dint(name, shape, dt=F32):
        t[name] = nc.dram_tensor(name, list(shape), dt, kind="Internal").ap()
        return t[name]

    def dout(name, shape, dt=F32):
        t[name] = nc.dram_tensor(name, list(shape), dt, kind="ExternalOutput").ap()
        return t[name]

    xin = din("xin", [N, D])
    out = dout("out", [N, D])
    din("w_potV", [128, 8]); din("w_potP", [128, 8]); din("b_pot", [1, 2])
    din("ln1g_c", [128, 8]); din("ln1b_c", [128, 8])
    din("w_bk", [2, D]); din("b_bk", [1, D])
    din("w_qkv", [D, 3 * H * HD])
    din("b_qkv", [128, 12]); din("b_qkv_v", [1, 512]); din("b_qkv_k", [1, 512])
    din("w_out", [512, D]); din("b_out", [1, D])
    din("w_ff1", [D, FF]); din("b_ff1", [128, 32])
    din("w_ff2", [FF, D]); din("b_ff2", [1, D])
    for gb in ["ln1g", "ln1b", "ln2g", "ln2b", "ln3g", "ln3b"]:
        din(gb, [1, D])
    din("Wq", [D, D]); din("Wk", [D, D]); din("Wv", [D, D])
    din("bq_c", [128, 8]); din("bk_c", [128, 8]); din("bv_r", [1, D])
    din("Wr", [D, NHEAD]); din("br_c", [16, 1]); din("br_r", [1, 16])
    din("Wr_c", [128, 128]); din("bv_c", [128, 8])
    din("Wo", [NACT * DA, D]); din("bo", [1, D])
    din("IDC", [128, 128], BF16)
    din("MASKETA", [128, 128])
    din("MASKETA4", [128, 512])
    din("U16", [16, 16], BF16)
    din("IOTA4", [16, 4])
    din("E4", [4, 256], BF16)
    din("GH", [16, 1024], BF16)
    din("IKRON", [128, 256], BF16)
    din("ONESC", [128, 1], BF16)
    din("ONESCF", [128, 1])
    din("ONESRF", [1, 128])
    din("ONESR", [1, 512], BF16)
    din("CPOOL", [128, len(CP_VALS)])
    dint("x2s", [N, D])
    dint("rowscr", [1, N])
    dint("rowscr2", [1, N])
    dint("rowb1", [1, N], BF16)
    dint("rowb2", [1, N], BF16)
    dint("wff1bf", [128, 8 * FF], BF16)
    dint("wff2bf", [128, 32 * D], BF16)
    dbgs = {}
    if debug:
        for nm, sh in [("dbg_xa", [N, D]), ("dbg_potV", [1, N]), ("dbg_potP", [1, N]),
                       ("dbg_gam", [128, 32]), ("dbg_G", [2, N]), ("dbg_xa4", [N, D]),
                       ("dbg_xa5", [N, D]), ("dbg_xa6", [N, D]), ("dbg_x2", [N, D]),
                       ("dbg_r16", [16, 1]), ("dbg_S16", [16, 4]),
                       ("dbg_are", [128, 32]), ("dbg_aim", [128, 32]),
                       ("dbg_bre", [128, 32]), ("dbg_bim", [128, 32]),
                       ("dbg_pt", [32, 128]), ("dbg_qselT", [256, N]), ("dbg_oT", [256, N]),
                       ("dbg_stb", [128, 256])]:
            dbgs[nm] = dout(nm, list(sh))

    es = ExitStack()
    with es:
        def sbt(name, shape, dt, scope=es):
            h = scope.enter_context(nc.sbuf_tensor(name, shape, dt))
            t[name] = h
            return h

        XA_scope = ExitStack()
        XA = XA_scope.enter_context(nc.sbuf_tensor("XA", [128, NT * D], F32, side="right"))
        TB = sbt("TB", [128, 4096], BF16)
        SXN = sbt("SXN", [128, 2 * D], BF16)
        SXG = sbt("SXG", [128, 2 * D], BF16)
        SSQ = sbt("SSQ", [128, D], F32)
        STAT = sbt("STAT", [128, 40], F32)
        RSTD2 = sbt("RSTD2", [128, 32], F32); NB2 = sbt("NB2", [128, 32], F32)
        RSTD3 = sbt("RSTD3", [128, 32], F32); NB3 = sbt("NB3", [128, 32], F32)
        CPO = sbt("CPO", [128, len(CP_VALS)], F32)
        IDCS = sbt("IDCS", [128, 128], BF16)
        ONESCS = sbt("ONESCS", [128, 1], BF16)
        ONESCF = sbt("ONESCF_s", [128, 1], F32)
        ONESRF = sbt("ONESRF_s", [1, 128], F32)
        ONESRS = sbt("ONESRS", [1, 512], BF16)
        BCAST = sbt("BCAST", [128, D], F32)
        GGB = sbt("GGB", [128, 2 * D], BF16)
        BQC = sbt("BQC", [128, 12], F32)
        ROWS = SXN[0:1, 0:2048].bitcast(F32)

        sems = {}
        for s_ in ["pe", "ve", "sc", "gp", "ld", "gld", "sld"]:
            sems[s_] = es.enter_context(nc.semaphore("sem_" + s_))
        XSEM = [es.enter_context(nc.semaphore(f"sem_x{k}")) for k in range(8)]
        PE = Eng(sems["pe"], serialize=False)
        VE = Eng(sems["ve"]); SC = Eng(sems["sc"]); GP = Eng(sems["gp"])
        LD = Eng(sems["ld"], serialize=False)
        GLD = Eng(sems["gld"], serialize=False)
        SLD = Eng(sems["sld"], serialize=False)
        global _ENGS
        _ENGS = [PE, VE, SC, GP, LD, GLD]

        def dma(ctr, eng, dst, src, **kw):
            ctr.n += 16
            eng.dma_start(dst, src, **kw).then_inc(ctr.sem, 16)
            return ctr.n

        P47es = ExitStack()
        P = [es.enter_context(nc.psum_tensor(f"pb{i}", [128, 512], F32)) for i in range(4)] + \
            [P47es.enter_context(nc.psum_tensor(f"pb{i}", [128, 512], F32)) for i in range(4, 8)]
        es.callback(P47es.close)

        cp = lambda name: CPO[:, CP[name]:CP[name] + 1]

        def emit_row_broadcast(row_name, dst, dtname="f32"):
            with nc.Block() as blk:
                @blk.sync
                def _(s):
                    dma(LD, s, ROWS, t[row_name][:, :])
                    s.wait_ge(sems["ld"], LD.n)
            with nc.Block() as blk:
                @blk.tensor
                def _(te):
                    PE.eng = te
                    PE.op(te.matmul, P[6][:, 0:512], ONESRF[0:1, :],
                          SXN[0:1, 0:1024].bitcast(F32), start=True, stop=True)
                    PE.op(te.matmul, P[7][:, 0:512], ONESRF[0:1, :],
                          SXN[0:1, 1024:2048].bitcast(F32), start=True, stop=True)

                @blk.vector
                def _(v):
                    VE.eng = v
                    VE.wait(PE, PE.n)
                    VE.op(v.tensor_copy, dst[:, 0:512], P[6][:, 0:512])
                    VE.op(v.tensor_copy, dst[:, 512:1024], P[7][:, 0:512])

        def emit_gb_broadcast(gname, bname):
            with nc.Block() as blk:
                @blk.sync
                def _(s):
                    dma(LD, s, ROWS, t[gname][:, :])
                    s.wait_ge(sems["ld"], LD.n)
            with nc.Block() as blk:
                @blk.tensor
                def _(te):
                    PE.eng = te
                    PE.op(te.matmul, P[6][:, 0:512], ONESRF[0:1, :],
                          SXN[0:1, 0:1024].bitcast(F32), start=True, stop=True)
                    PE.op(te.matmul, P[7][:, 0:512], ONESRF[0:1, :],
                          SXN[0:1, 1024:2048].bitcast(F32), start=True, stop=True)

                @blk.vector
                def _(v):
                    VE.eng = v
                    VE.wait(PE, PE.n)
                    VE.op(v.tensor_copy, GGB[:, 0:512], P[6][:, 0:512])
                    VE.op(v.tensor_copy, GGB[:, 512:1024], P[7][:, 0:512])
            with nc.Block() as blk:
                @blk.sync
                def _(s):
                    dma(LD, s, ROWS, t[bname][:, :])
                    s.wait_ge(sems["ld"], LD.n)
            with nc.Block() as blk:
                @blk.tensor
                def _(te):
                    PE.eng = te
                    PE.op(te.matmul, P[6][:, 0:512], ONESRF[0:1, :],
                          SXN[0:1, 0:1024].bitcast(F32), start=True, stop=True)
                    PE.op(te.matmul, P[7][:, 0:512], ONESRF[0:1, :],
                          SXN[0:1, 1024:2048].bitcast(F32), start=True, stop=True)

                @blk.vector
                def _(v):
                    VE.eng = v
                    VE.wait(PE, PE.n)
                    VE.op(v.tensor_copy, GGB[:, D:D + 512], P[6][:, 0:512])
                    VE.op(v.tensor_copy, GGB[:, D + 512:2 * D], P[7][:, 0:512])

        with nc.Block() as blk:
            @blk.sync
            def _(s):
                for nm, dst in [("CPOOL", CPO), ("IDC", IDCS),
                                ("ONESC", ONESCS), ("ONESCF", ONESCF), ("ONESRF", ONESRF),
                                ("ONESR", ONESRS), ("b_qkv", BQC)]:
                    dma(LD, s, dst[:, :], t[nm][:, :])
                for k in range(8):
                    src = xin[512 * k:512 * (k + 1), :].rearrange("(j p) f -> p j f", p=128)
                    dst = XA[:, 4 * k * D:(4 * k + 4) * D].rearrange("p (j f) -> p j f", j=4)
                    s.dma_start(dst, src).then_inc(XSEM[k], 16)
                s.wait_ge(sems["ld"], 16 * 7)

        esAB = ExitStack()
        with esAB:
            GAM32 = sbt("GAM32", [128, 32], F32, esAB)
            STB = sbt("STB", [128, 256], F32, esAB)
            RPETA = sbt("RPETA", [128, 32], F32, esAB)
            PLB2 = sbt("PLB2", [128, 32], F32, esAB)
            PT32 = sbt("PT32", [32, 128], F32, esAB)

            esA = ExitStack()
            with esA:
                WPV = sbt("WPV", [128, 8], BF16, esA); WPP = sbt("WPP", [128, 8], BF16, esA)
                BPV = sbt("BPV", [1, 2], F32, esA)
                WBK = sbt("WBK", [2, D], BF16, esA)
                ONER128 = sbt("ONER128", [1, 128], BF16, esA)
                BBKR = sbt("BBKR", [1, D], BF16, esA)
                DRE = sbt("DRE", [128, 32], F32, esA); DIM = sbt("DIM", [128, 32], F32, esA)
                G1C = sbt("G1C", [128, 8], F32, esA)
                B1C = sbt("B1C", [128, 8], F32, esA)
                B1CB = sbt("B1CB", [128, 8], BF16, esA)
                WPVg = sbt("WPVg", [128, 8], BF16, esA)
                WPPg = sbt("WPPg", [128, 8], BF16, esA)
                BPV2 = sbt("BPV2", [1, 2], F32, esA)
                STA = sbt("STA", [128, 384], F32, esA)
                esPot = ExitStack()
                POTV = sbt("POTV", [1, N], BF16, esPot)
                POTP = sbt("POTP", [1, N], BF16, esPot)

                with nc.Block() as blk:
                    @blk.gpsimd
                    def _(g):
                        dma(GLD, g, WPV[:, :], t["w_potV"][:, :])
                        dma(GLD, g, WPP[:, :], t["w_potP"][:, :])
                        dma(GLD, g, WBK[:, :], t["w_bk"][:, :])
                        dma(GLD, g, BBKR[0:1, :], t["b_bk"][0:1, :])
                        g.wait_ge(sems["gld"], GLD.n)

                    @blk.sync
                    def _(s):
                        dma(LD, s, BPV[:, :], t["b_pot"][:, :])
                        dma(LD, s, G1C[:, :], t["ln1g_c"][:, :])
                        dma(LD, s, B1C[:, :], t["ln1b_c"][:, :])
                        s.wait_ge(sems["ld"], LD.n)

                with nc.Block() as blk:
                    ve0f, pe0f = VE.n, PE.n

                    @blk.vector
                    def _(v):
                        VE.eng = v
                        VE.op(v.memset, ONER128[:, :], 1.0)
                        VE.op(v.tensor_copy, B1CB[:, :], B1C[:, :])
                        VE.op(v.tensor_tensor, WPVg[:, :], WPV[:, :], G1C[:, :], ALU.mult)
                        VE.op(v.tensor_tensor, WPPg[:, :], WPP[:, :], G1C[:, :], ALU.mult)
                        VE.wait(PE, pe0f + 16)
                        VE.op(v.tensor_tensor, BPV2[0:1, 0:2], BPV[0:1, 0:2],
                              P[6][0:1, 0:2], ALU.add)

                    @blk.tensor
                    def _(te):
                        PE.eng = te
                        PE.wait(VE, ve0f + 2)
                        for c in range(8):
                            PE.op(te.matmul, P[6][0:1, 0:1], WPV[:, c:c + 1], B1CB[:, c:c + 1],
                                  start=(c == 0), stop=(c == 7))
                        for c in range(8):
                            PE.op(te.matmul, P[6][0:1, 1:2], WPP[:, c:c + 1], B1CB[:, c:c + 1],
                                  start=(c == 0), stop=(c == 7))

                A_SUM = STA[:, 0:32]; A_SSA = STA[:, 32:64]; A_MEAN = STA[:, 64:96]
                A_T0 = STA[:, 96:128]; A_VAR = STA[:, 128:160]; A_W = STA[:, 160:192]
                A_T2 = STA[:, 192:224]; A_RSTD0 = STA[:, 224:256]; A_NB0 = STA[:, 256:288]
                A_RSTD1 = STA[:, 288:320]; A_RS01 = STA[:, 320:352]; A_NB01 = STA[:, 352:384]
                XSLOT = [SXN[:, 0:1024], SXN[:, 1024:2048], SXG[:, 0:1024], SXG[:, 1024:2048]]

                with nc.Block() as blk:
                    sc0, ve0, pe0 = SC.n, VE.n, PE.n
                    a1i = lambda tt: 37 if tt == 0 else 3 * tt + 35
                    te1i = lambda tp: 3 * tp + 40 if tp <= 30 else 132

                    @blk.scalar
                    def _(s):
                        SC.eng = s
                        for tt in range(NT):
                            s.wait_ge(XSEM[tt // 4], 16)
                            SC.op(s.activation, SSQ[:, :], XA[:, tt * D:(tt + 1) * D],
                                  AF.Square, accum_out=A_SSA[:, tt:tt + 1])
                        SC.wait(VE, ve0 + 35)
                        SC.op(s.activation, A_T2[:, :], A_VAR[:, :], AF.Ln, bias=cp("eps"))
                        SC.op(s.activation, A_RSTD0[:, :], A_T2[:, :], AF.Exp, scale=cp("neg05"))
                        SC.wait(VE, ve0 + 36)
                        SC.op(s.activation, A_T0[:, :], A_W[:, :], AF.Ln)
                        SC.wait(VE, ve0 + 38)
                        SC.op(s.activation, A_RSTD1[:, :], A_VAR[:, :], AF.Exp, scale=cp("half"))
                        for tt in range(NT + 1):
                            if tt < NT:
                                if tt == 0:
                                    SC.wait(VE, ve0 + 40)
                                if tt >= 4:
                                    SC.wait(PE, pe0 + 24 * (tt - 4) + 8)
                                SC.op(s.activation, XSLOT[tt % 4], XA[:, tt * D:(tt + 1) * D],
                                      AF.Identity, scale=A_RS01[:, tt:tt + 1],
                                      bias=A_NB01[:, tt:tt + 1])
                            if tt >= 1:
                                tp = tt - 1
                                qq = tp % 2
                                SC.wait(PE, pe0 + 24 * tp + 8)
                                SC.op(s.activation, TB[:, qq * 2048:qq * 2048 + 512],
                                      P[2 * qq][:, :], AF.Identity)
                                SC.op(s.activation, TB[:, qq * 2048 + 512:qq * 2048 + 1024],
                                      P[2 * qq + 1][:, :], AF.Identity)

                    @blk.vector
                    def _(v):
                        VE.eng = v
                        for tt in range(NT):
                            v.wait_ge(XSEM[tt // 4], 16)
                            VE.op(v.tensor_scalar, TB[:, 0:1024], XA[:, tt * D:(tt + 1) * D],
                                  1.0, 0.0, ALU.mult, ALU.add, accum_out=A_SUM[:, tt:tt + 1])
                        VE.op(v.tensor_scalar, A_MEAN[:, :], A_SUM[:, :], 1.0 / D, None, ALU.mult)
                        VE.op(v.tensor_tensor, A_T0[:, :], A_MEAN[:, :], A_MEAN[:, :], ALU.mult)
                        VE.wait(SC, sc0 + 32)
                        VE.op(v.scalar_tensor_tensor, A_VAR[:, :], A_SSA[:, :], 1.0 / D,
                              A_T0[:, :], ALU.mult, ALU.subtract)
                        VE.op(v.tensor_scalar, A_W[:, :], A_VAR[:, :], 1.0 + EPS, EPS * EPS,
                              ALU.mult, ALU.add)
                        VE.wait(SC, sc0 + 34)
                        VE.op(v.scalar_tensor_tensor, A_NB0[:, :], A_MEAN[:, :], -1.0,
                              A_RSTD0[:, :], ALU.mult, ALU.mult)
                        VE.wait(SC, sc0 + 35)
                        VE.op(v.tensor_tensor, A_VAR[:, :], A_T2[:, :], A_T0[:, :], ALU.subtract)
                        VE.wait(SC, sc0 + 36)
                        VE.op(v.tensor_tensor, A_RS01[:, :], A_RSTD0[:, :], A_RSTD1[:, :], ALU.mult)
                        VE.op(v.tensor_tensor, A_NB01[:, :], A_NB0[:, :], A_RSTD1[:, :], ALU.mult)
                        for tt in range(NT):
                            qq = tt % 2
                            VE.wait(SC, sc0 + a1i(tt))
                            VE.op(v.tensor_scalar, XA[:, tt * D:(tt + 1) * D],
                                  XA[:, tt * D:(tt + 1) * D], A_RSTD0[:, tt:tt + 1],
                                  A_NB0[:, tt:tt + 1], ALU.mult, ALU.add)
                            VE.wait(PE, pe0 + 24 * tt + 24)
                            VE.op(v.tensor_scalar, POTV[0:1, tt * 128:(tt + 1) * 128],
                                  P[4 + qq][0:1, 0:128], BPV2[0:1, 0:1], None, ALU.add)
                            VE.wait(PE, pe0 + 24 * tt + 24)
                            VE.op(v.tensor_scalar, POTP[0:1, tt * 128:(tt + 1) * 128],
                                  P[4 + qq][0:1, 128:256], BPV2[0:1, 1:2], None, ALU.add)

                    @blk.tensor
                    def _(te):
                        PE.eng = te
                        for tt in range(NT):
                            qq = tt % 2
                            PE.wait(SC, sc0 + a1i(tt))
                            for j in range(8):
                                bank = P[2 * qq + j // 4]
                                PE.op(te.matmul, bank[:, (j % 4) * 128:(j % 4 + 1) * 128],
                                      XSLOT[tt % 4][:, j * 128:(j + 1) * 128],
                                      IDCS[:, :], start=True, stop=True)
                            PE.wait(SC, sc0 + te1i(tt))
                            if tt >= 2:
                                PE.wait(VE, ve0 + 43 + 3 * (tt - 2))
                            for c in range(8):
                                PE.op(te.matmul, P[4 + qq][0:1, 0:128], WPVg[:, c:c + 1],
                                      TB[:, qq * 2048 + c * 128:qq * 2048 + (c + 1) * 128],
                                      start=(c == 0), stop=(c == 7))
                            for c in range(8):
                                PE.op(te.matmul, P[4 + qq][0:1, 128:256], WPPg[:, c:c + 1],
                                      TB[:, qq * 2048 + c * 128:qq * 2048 + (c + 1) * 128],
                                      start=(c == 0), stop=(c == 7))

                if debug:
                    with nc.Block() as blk:
                        @blk.sync
                        def _(s):
                            for k in range(8):
                                dst = dbgs["dbg_xa"][512 * k:512 * (k + 1), :].rearrange(
                                    "(j p) f -> p j f", p=128)
                                src_ = XA[:, 4 * k * D:(4 * k + 4) * D].rearrange(
                                    "p (j f) -> p j f", j=4)
                                dma(LD, s, dst, src_)
                            s.wait_ge(sems["ld"], LD.n)

                        @blk.gpsimd
                        def _(g):
                            dma(GLD, g, dbgs["dbg_potV"][:, :], POTV[:, :])
                            dma(GLD, g, dbgs["dbg_potP"][:, :], POTP[:, :])
                            g.wait_ge(sems["gld"], GLD.n)

                if upto >= 2:
                    rowscr, rowscr2 = t["rowscr"], t["rowscr2"]
                    with nc.Block() as blk:
                        @blk.sync
                        def _(s):
                            dma(LD, s, t["rowb1"][:, :], POTV[:, :])
                            dma(LD, s, t["rowb2"][:, :], POTP[:, :])
                            s.wait_ge(sems["ld"], LD.n)

                        @blk.gpsimd
                        def _(g):
                            g.wait_ge(sems["ld"], LD.n)
                            dma(GLD, g, DRE[:, :],
                                t["rowb1"].rearrange("one (p f) -> (one p) f", p=128))
                            dma(GLD, g, GAM32[:, :],
                                t["rowb2"].rearrange("one (p f) -> (one p) f", p=128))
                            g.wait_ge(sems["gld"], GLD.n)
                    esPot.close()
                    GROW = sbt("GROW", [2, N], BF16, esA)
                    TS_RE = sbt("TS_RE", [128, 128], F32, esA)
                    TS_IM = sbt("TS_IM", [128, 128], F32, esA)
                    APL = sbt("APL", [128, 16], F32, esA)
                    APL2 = sbt("APL2", [128, 16], F32, esA)
                    IDP = sbt("IDP", [128, 16], F32, esA)
                    E10R = sbt("E10R", [128, 2], F32, esA); E10I = sbt("E10I", [128, 2], F32, esA)
                    SC32 = sbt("SC32", [128, 512], F32, esA)
                    GRE = sbt("GRE", [128, 32], F32, esA); GIM = sbt("GIM", [128, 32], F32, esA)
                    CROW32 = sbt("CROW32", [32, 128], F32, esA)
                    ZER32 = sbt("ZER32", [32, 128], F32, esA)
                    CFB = sbt("CFB", [32, 128], BF16, esA)
                    PL1 = sbt("PL1", [1, 32], BF16, esA)
                    PLC = sbt("PLC", [32, 1], BF16, esA)

                    with nc.Block() as blk:
                        sc0 = SC.n

                        @blk.scalar
                        def _(s):
                            SC.eng = s
                            SC.op(s.activation, SC32[:, 0:32], GAM32[:, :], AF.Abs)
                            SC.op(s.activation, SC32[:, 32:64], SC32[:, 0:32], AF.Exp,
                                  scale=cp("neg1"))
                            SC.op(s.activation, SC32[:, 32:64], SC32[:, 32:64], AF.Ln, bias=1.0)

                        @blk.vector
                        def _(v):
                            VE.eng = v
                            VE.wait(SC, sc0 + 3)
                            VE.op(v.scalar_tensor_tensor, SC32[:, 64:96], GAM32[:, :], 0.0,
                                  SC32[:, 32:64], ALU.max, ALU.add)
                            VE.op(v.tensor_scalar, GAM32[:, :], SC32[:, 64:96], 0.1, None, ALU.add)
                            VE.op(v.tensor_scalar, DRE[:, :], DRE[:, :], 2.0, None, ALU.add)
                            VE.op(v.tensor_scalar, DIM[:, :], GAM32[:, :], -1.0, None, ALU.mult)
                            VE.op(v.memset, E10R[:, 0:1], 1.0)
                            VE.op(v.memset, E10R[:, 1:2], 0.0)
                            VE.op(v.memset, E10I[:, :], 0.0)
                            VE.op(v.memset, IDP[:, :], 0.0)
                            VE.op(v.memset, IDP[:, 0:1], 1.0)
                            VE.op(v.memset, IDP[:, 5:6], 1.0)
                            VE.op(v.memset, IDP[:, 8:9], 1.0)
                            VE.op(v.memset, IDP[:, 13:14], 1.0)

                    if debug:
                        with nc.Block() as blk:
                            @blk.sync
                            def _(s):
                                dma(LD, s, dbgs["dbg_gam"][:, :], GAM32[:, :])
                                s.wait_ge(sems["ld"], LD.n)

                    with nc.Block() as blk:
                        @blk.vector
                        def _(v):
                            VE.eng = v
                            sa, sb = SC32[:, 96:98], SC32[:, 98:100]

                            def slot_re(o, i):
                                return TS_RE[:, o + 2 * i:o + 2 * i + 2]

                            def slot_im(o, i):
                                return TS_IM[:, o + 2 * i:o + 2 * i + 2]

                            VE.op(v.tensor_copy, TS_RE[:, 0:1], DRE[:, 0:1])
                            VE.op(v.memset, TS_RE[:, 1:2], -1.0)
                            VE.op(v.tensor_copy, TS_IM[:, 0:1], DIM[:, 0:1])
                            VE.op(v.memset, TS_IM[:, 1:2], 0.0)
                            VE.op(v.tensor_copy, TS_RE[:, 64 + 62:64 + 63], DRE[:, 31:32])
                            VE.op(v.memset, TS_RE[:, 64 + 63:64 + 64], -1.0)
                            VE.op(v.tensor_copy, TS_IM[:, 64 + 62:64 + 63], DIM[:, 31:32])
                            VE.op(v.memset, TS_IM[:, 64 + 63:64 + 64], 0.0)
                            for step in range(1, 32):
                                for dirn in range(2):
                                    if dirn == 0:
                                        o, i, cur, prv = 0, step, step - 1, step - 2
                                        dcol = i
                                    else:
                                        f = 31 - step
                                        o, i, cur, prv = 64, f, f + 1, f + 2
                                        dcol = f
                                    pre = E10R[:, :] if step == 1 else slot_re(o, prv)
                                    pim = E10I[:, :] if step == 1 else slot_im(o, prv)
                                    dre_c = DRE[:, dcol:dcol + 1]
                                    dim_c = DIM[:, dcol:dcol + 1]
                                    VE.op(v.tensor_scalar, sa, slot_im(o, cur), dim_c, None, ALU.mult)
                                    VE.op(v.scalar_tensor_tensor, sb, slot_re(o, cur), dre_c,
                                          sa, ALU.mult, ALU.subtract)
                                    VE.op(v.tensor_tensor, slot_re(o, i), sb, pre, ALU.subtract)
                                    VE.op(v.tensor_scalar, sa, slot_re(o, cur), dim_c, None, ALU.mult)
                                    VE.op(v.scalar_tensor_tensor, sb, slot_im(o, cur), dre_c,
                                          sa, ALU.mult, ALU.add)
                                    VE.op(v.tensor_tensor, slot_im(o, i), sb, pim, ALU.subtract)

                    with nc.Block() as blk:
                        sc0, ve0, ld0 = SC.n, VE.n, LD.n
                        NLVL = 7

                        @blk.vector
                        def _(v):
                            VE.eng = v
                            for o, s1, s2, off in [(0, 31, 30, 0), (64, 0, 1, 8)]:
                                VE.op(v.tensor_copy, APL[:, off + 0:off + 2], TS_RE[:, o + 2 * s1:o + 2 * s1 + 2])
                                VE.op(v.tensor_copy, APL[:, off + 2:off + 4], TS_IM[:, o + 2 * s1:o + 2 * s1 + 2])
                                VE.op(v.tensor_copy, APL[:, off + 4:off + 6], TS_RE[:, o + 2 * s2:o + 2 * s2 + 2])
                                VE.op(v.tensor_copy, APL[:, off + 6:off + 8], TS_IM[:, o + 2 * s2:o + 2 * s2 + 2])
                            VE.wait(SC, sc0 + 1)
                            for off in (0, 8):
                                VE.op(v.tensor_reduce, SC32[:, 16 + off // 8:17 + off // 8],
                                      SC32[:, off:off + 8], AX.X, ALU.add)
                                VE.op(v.reciprocal, SC32[:, 18 + off // 8:19 + off // 8],
                                      SC32[:, 16 + off // 8:17 + off // 8])
                                VE.op(v.tensor_scalar, APL[:, off:off + 8], APL[:, off:off + 8],
                                      SC32[:, 18 + off // 8:19 + off // 8], None, ALU.mult)
                            for lv in range(NLVL):
                                VE.wait(LD, ld0 + 64 * (lv + 1))
                                for off in (0, 8):
                                    A = lambda c0, c1: APL[:, off + c0:off + c1]
                                    B = lambda c0, c1: APL2[:, off + c0:off + c1]
                                    scr = lambda k: SC32[:, 128 + 2 * k:128 + 2 * k + 2]
                                    outs = []
                                    for row in range(2):
                                        a0r = APL[:, off + 4 * row + 0:off + 4 * row + 1]
                                        a1r = APL[:, off + 4 * row + 1:off + 4 * row + 2]
                                        a0i = APL[:, off + 4 * row + 2:off + 4 * row + 3]
                                        a1i = APL[:, off + 4 * row + 3:off + 4 * row + 4]
                                        VE.op(v.tensor_scalar, scr(0), B(2, 4), a0i, None, ALU.mult)
                                        VE.op(v.scalar_tensor_tensor, scr(1), B(0, 2), a0r, scr(0),
                                              ALU.mult, ALU.subtract)
                                        VE.op(v.tensor_scalar, scr(2), B(6, 8), a1i, None, ALU.mult)
                                        VE.op(v.scalar_tensor_tensor, scr(3), B(4, 6), a1r, scr(2),
                                              ALU.mult, ALU.subtract)
                                        VE.op(v.tensor_tensor, scr(4 + 2 * row), scr(1), scr(3), ALU.add)
                                        VE.op(v.tensor_scalar, scr(0), B(0, 2), a0i, None, ALU.mult)
                                        VE.op(v.scalar_tensor_tensor, scr(1), B(2, 4), a0r, scr(0),
                                              ALU.mult, ALU.add)
                                        VE.op(v.tensor_scalar, scr(2), B(4, 6), a1i, None, ALU.mult)
                                        VE.op(v.scalar_tensor_tensor, scr(3), B(6, 8), a1r, scr(2),
                                              ALU.mult, ALU.add)
                                        VE.op(v.tensor_tensor, scr(5 + 2 * row), scr(1), scr(3), ALU.add)
                                    VE.op(v.tensor_copy, A(0, 2), scr(4))
                                    VE.op(v.tensor_copy, A(2, 4), scr(5))
                                    VE.op(v.tensor_copy, A(4, 6), scr(6))
                                    VE.op(v.tensor_copy, A(6, 8), scr(7))
                                    VE.wait(SC, sc0 + 1 + 2 * lv + off // 8 + 1)
                                    VE.op(v.tensor_reduce, SC32[:, 20:21], SC32[:, 24 + off:32 + off],
                                          AX.X, ALU.add)
                                    VE.op(v.reciprocal, SC32[:, 21:22], SC32[:, 20:21])
                                    VE.op(v.tensor_scalar, APL[:, off:off + 8], APL[:, off:off + 8],
                                          SC32[:, 21:22], None, ALU.mult)

                        @blk.scalar
                        def _(s):
                            SC.eng = s
                            SC.wait(VE, ve0 + 8)
                            SC.op(s.activation, SC32[:, 0:16], APL[:, 0:16], AF.Abs)
                            for lv in range(NLVL):
                                for off in (0, 8):
                                    VE_done = ve0 + 14 + lv * 54 + (off // 8 + 1) * 24 + (off // 8) * 3
                                    SC.wait(VE, VE_done)
                                    SC.op(s.activation, SC32[:, 24 + off:32 + off],
                                          APL[:, off:off + 8], AF.Abs)

                        @blk.sync
                        def _(s):
                            for lv in range(NLVL):
                                sft = 1 << lv
                                if lv == 0:
                                    s.wait_ge(sems["ve"], ve0 + 14)
                                else:
                                    s.wait_ge(sems["ve"], ve0 + 14 + lv * 54)
                                dma(LD, s, APL2[sft:128, 0:8], APL[0:128 - sft, 0:8])
                                dma(LD, s, APL2[0:sft, 0:8], IDP[0:sft, 0:8])
                                dma(LD, s, APL2[0:128 - sft, 8:16], APL[sft:128, 8:16])
                                dma(LD, s, APL2[128 - sft:128, 8:16], IDP[0:sft, 8:16])
                            s.wait_ge(sems["ve"], ve0 + 14 + NLVL * 54)
                            dma(LD, s, APL2[1:128, 0:8], APL[0:127, 0:8])
                            dma(LD, s, APL2[0:1, 0:8], IDP[0:1, 0:8])
                            dma(LD, s, APL2[0:127, 8:16], APL[1:128, 8:16])
                            dma(LD, s, APL2[127:128, 8:16], IDP[0:1, 8:16])
                            s.wait_ge(sems["ld"], LD.n)

                    with nc.Block() as blk:
                        @blk.vector
                        def _(v):
                            VE.eng = v
                            reg = lambda k: SC32[:, 32 * k:32 * (k + 1)]
                            for dirn in range(2):
                                o = 64 * dirn
                                off = 8 * dirn
                                Sp = lambda c: APL2[:, off + c:off + c + 1]
                                s0r, s0i = Sp(0), Sp(2)
                                s1r, s1i = Sp(4), Sp(6)
                                T00r = TS_RE[:, o + 0:o + 64:2]
                                T01r = TS_RE[:, o + 1:o + 64:2]
                                T00i = TS_IM[:, o + 0:o + 64:2]
                                T01i = TS_IM[:, o + 1:o + 64:2]

                                def mobius(dst_re, dst_im, t00r, t01r, t00i, t01i, w):
                                    tmp = SC32[:, 0:w]; acc = SC32[:, 32:32 + w]
                                    VE.op(v.tensor_scalar, acc, t00r, s0r, None, ALU.mult)
                                    VE.op(v.tensor_scalar, tmp, t00i, s0i, None, ALU.mult)
                                    VE.op(v.tensor_tensor, acc, acc, tmp, ALU.subtract)
                                    VE.op(v.tensor_scalar, tmp, t01r, s1r, None, ALU.mult)
                                    VE.op(v.tensor_tensor, acc, acc, tmp, ALU.add)
                                    VE.op(v.tensor_scalar, tmp, t01i, s1i, None, ALU.mult)
                                    VE.op(v.tensor_tensor, acc, acc, tmp, ALU.subtract)
                                    VE.op(v.tensor_copy, dst_re, acc)
                                    VE.op(v.tensor_scalar, acc, t00i, s0r, None, ALU.mult)
                                    VE.op(v.tensor_scalar, tmp, t00r, s0i, None, ALU.mult)
                                    VE.op(v.tensor_tensor, acc, acc, tmp, ALU.add)
                                    VE.op(v.tensor_scalar, tmp, t01i, s1r, None, ALU.mult)
                                    VE.op(v.tensor_tensor, acc, acc, tmp, ALU.add)
                                    VE.op(v.tensor_scalar, tmp, t01r, s1i, None, ALU.mult)
                                    VE.op(v.tensor_tensor, acc, acc, tmp, ALU.add)
                                    VE.op(v.tensor_copy, dst_im, acc)

                                mobius(reg(2), reg(3), T00r, T01r, T00i, T01i, 32)
                                if dirn == 0:
                                    mobius(reg(4)[:, 1:32], reg(5)[:, 1:32],
                                           TS_RE[:, o + 0:o + 62:2], TS_RE[:, o + 1:o + 62:2],
                                           TS_IM[:, o + 0:o + 62:2], TS_IM[:, o + 1:o + 62:2], 31)
                                    VE.op(v.tensor_scalar, reg(4)[:, 0:1], ONESCF[:, :], s0r,
                                          None, ALU.mult)
                                    VE.op(v.tensor_scalar, reg(5)[:, 0:1], ONESCF[:, :], s0i,
                                          None, ALU.mult)
                                else:
                                    mobius(reg(4)[:, 0:31], reg(5)[:, 0:31],
                                           TS_RE[:, o + 2:o + 64:2], TS_RE[:, o + 3:o + 64:2],
                                           TS_IM[:, o + 2:o + 64:2], TS_IM[:, o + 3:o + 64:2], 31)
                                    VE.op(v.tensor_scalar, reg(4)[:, 31:32], ONESCF[:, :], s0r,
                                          None, ALU.mult)
                                    VE.op(v.tensor_scalar, reg(5)[:, 31:32], ONESCF[:, :], s0i,
                                          None, ALU.mult)
                                VE.op(v.tensor_tensor, reg(6), reg(4), reg(4), ALU.mult)
                                VE.op(v.tensor_tensor, reg(0), reg(5), reg(5), ALU.mult)
                                VE.op(v.tensor_tensor, reg(6), reg(6), reg(0), ALU.add)
                                VE.op(v.reciprocal, reg(6), reg(6))
                                dst_re, dst_im = (reg(7), reg(8)) if dirn == 0 else (reg(9), reg(10))
                                VE.op(v.tensor_tensor, reg(0), reg(2), reg(4), ALU.mult)
                                VE.op(v.tensor_tensor, reg(1), reg(3), reg(5), ALU.mult)
                                VE.op(v.tensor_tensor, reg(0), reg(0), reg(1), ALU.add)
                                VE.op(v.tensor_tensor, dst_re, reg(0), reg(6), ALU.mult)
                                VE.op(v.tensor_tensor, reg(0), reg(3), reg(4), ALU.mult)
                                VE.op(v.tensor_tensor, reg(1), reg(2), reg(5), ALU.mult)
                                VE.op(v.tensor_tensor, reg(0), reg(0), reg(1), ALU.subtract)
                                VE.op(v.tensor_tensor, dst_im, reg(0), reg(6), ALU.mult)
                            VE.op(v.tensor_tensor, reg(0), reg(7), reg(9), ALU.add)
                            VE.op(v.tensor_tensor, reg(0), reg(0), DRE[:, :], ALU.subtract)
                            VE.op(v.tensor_tensor, reg(1), reg(8), reg(10), ALU.add)
                            VE.op(v.tensor_tensor, reg(1), reg(1), DIM[:, :], ALU.subtract)
                            VE.op(v.tensor_tensor, reg(6), reg(0), reg(0), ALU.mult)
                            VE.op(v.tensor_tensor, reg(2), reg(1), reg(1), ALU.mult)
                            VE.op(v.tensor_tensor, reg(6), reg(6), reg(2), ALU.add)
                            VE.op(v.reciprocal, reg(6), reg(6))
                            VE.op(v.tensor_tensor, GRE[:, :], reg(0), reg(6), ALU.mult)
                            VE.op(v.scalar_tensor_tensor, GIM[:, :], reg(1), -1.0, reg(6),
                                  ALU.mult, ALU.mult)

                    if debug:
                        with nc.Block() as blk:
                            @blk.sync
                            def _(s):
                                reg = lambda k: SC32[:, 32 * k:32 * (k + 1)]
                                dma(LD, s, dbgs["dbg_are"][:, :], reg(7))
                                dma(LD, s, dbgs["dbg_aim"][:, :], reg(8))
                                dma(LD, s, dbgs["dbg_bre"][:, :], reg(9))
                                dma(LD, s, dbgs["dbg_bim"][:, :], reg(10))
                                s.wait_ge(sems["ld"], LD.n)

                    with nc.Block() as blk:
                        sc0, ve0, ld0, gld0, pe0 = SC.n, VE.n, LD.n, GLD.n, PE.n

                        @blk.sync
                        def _(s):
                            dma(LD, s, t["rowscr"].rearrange("one (p f) -> (one p) f", p=128),
                                GRE[:, :])
                            dma(LD, s, t["rowscr2"].rearrange("one (p f) -> (one p) f", p=128),
                                GIM[:, :])
                            s.wait_ge(sems["ld"], LD.n)

                        @blk.scalar
                        def _(s):
                            SC.eng = s
                            SC.op(s.activation, SC32[:, 0:32], GAM32[:, :], AF.Exp, scale=cp("negdt"))

                        @blk.gpsimd
                        def _(g):
                            GP.eng = g
                            g.wait_ge(sems["ld"], ld0 + 32)
                            dma(GLD, g, GROW[0:1, :], t["rowscr"][:, :])
                            dma(GLD, g, GROW[1:2, :], t["rowscr2"][:, :])
                            g.wait_ge(sems["gld"], GLD.n)

                    with nc.Block() as blk:
                        sc0, ve0, ld0, gld0, pe0 = SC.n, VE.n, LD.n, GLD.n, PE.n

                        @blk.sync
                        def _(s):
                            dma(LD, s, t["rowscr"].rearrange("one (p f) -> (one p) f", p=128),
                                SC32[:, 0:32])
                            s.wait_ge(sems["ld"], ld0 + 16)
                            dma(LD, s, CROW32[:, :],
                                t["rowscr"].rearrange("one (p f) -> (one p) f", p=32))
                            s.wait_ge(sems["ld"], ld0 + 32)

                        @blk.vector
                        def _(v):
                            VE.eng = v
                            VE.op(v.memset, ZER32[:, :], 0.0)
                            VE.wait(LD, ld0 + 32)
                            VE.op(v.tensor_tensor_scan, PT32[:, :], CROW32[:, :], ZER32[:, :],
                                  1.0, ALU.mult, ALU.add)
                            VE.op(v.reciprocal, CROW32[:, :], PT32[:, :])
                            VE.op(v.tensor_scalar, ZER32[:, :], CROW32[:, :], ETA, None, ALU.mult)

                        @blk.scalar
                        def _(s):
                            SC.eng = s
                            SC.wait(VE, ve0 + 4)
                            SC.op(s.activation, CFB[:, :], ZER32[:, :], AF.Identity)
                            SC.op(s.activation, PLC[:, :], PT32[:, 127:128], AF.Identity)

                        @blk.tensor
                        def _(te):
                            PE.eng = te
                            PE.wait(SC, sc0 + 1)
                            PE.op(te.matmul, P[0][0:128, 0:32], CFB[:, :], IDCS[0:32, 0:32],
                                  start=True, stop=True)
                            PE.wait(SC, sc0 + 2)
                            PE.op(te.matmul, P[1][0:1, 0:32], PLC[:, :], IDCS[0:32, 0:32],
                                  start=True, stop=True)

                    with nc.Block() as blk:
                        sc0, ve0, ld0, gld0, pe0 = SC.n, VE.n, LD.n, GLD.n, PE.n

                        @blk.vector
                        def _(v):
                            VE.eng = v
                            VE.op(v.tensor_copy, RPETA[:, :], P[0][0:128, 0:32])
                            VE.op(v.tensor_copy, PL1[:, :], P[1][0:1, 0:32])

                        @blk.sync
                        def _(s):
                            dma(LD, s, t["rowscr"].rearrange("one (p f) -> (one p) f", p=32),
                                PT32[:, :])
                            dma(LD, s, t["rowscr2"].rearrange("one (p f) -> (one p) f", p=32),
                                CROW32[:, :])
                            s.wait_ge(sems["ld"], LD.n)


                    with nc.Block() as blk:
                        pe0, ve0 = PE.n, VE.n

                        @blk.tensor
                        def _(te):
                            PE.eng = te
                            PE.op(te.matmul, P[2][0:128, 0:32], ONESRS[0:1, 0:128],
                                  PL1[:, :], start=True, stop=True)

                        @blk.vector
                        def _(v):
                            VE.eng = v
                            VE.wait(PE, pe0 + 1)
                            VE.op(v.tensor_copy, PLB2[:, :], P[2][0:128, 0:32])

                    if debug:
                        with nc.Block() as blk:
                            @blk.gpsimd
                            def _(g):
                                dma(GLD, g, dbgs["dbg_G"][:, :], GROW[:, :])
                                g.wait_ge(sems["gld"], GLD.n)

                            @blk.sync
                            def _(s):
                                dma(LD, s, dbgs["dbg_pt"][:, :], PT32[:, :])
                                s.wait_ge(sems["ld"], LD.n)

                if upto >= 3:
                    with nc.Block() as blk:
                        sc0, ve0, pe0 = SC.n, VE.n, PE.n

                        @blk.tensor
                        def _(te):
                            PE.eng = te
                            for tt in range(NT):
                                qq = tt % 2
                                if tt >= 2:
                                    PE.wait(VE, ve0 + 3 * (tt - 2) + 2)
                                for j in range(2):
                                    PE.op(te.matmul, P[2 * qq + j][:, :],
                                          GROW[:, tt * 128:(tt + 1) * 128],
                                          WBK[:, j * 512:(j + 1) * 512], start=True, stop=False)
                                    PE.op(te.matmul, P[2 * qq + j][:, :],
                                          ONER128[0:1, 0:128],
                                          BBKR[0:1, j * 512:(j + 1) * 512], start=False, stop=True)

                        @blk.vector
                        def _(v):
                            VE.eng = v
                            for tt in range(NT):
                                qq = tt % 2
                                xat = XA[:, tt * D:(tt + 1) * D]
                                VE.wait(PE, pe0 + 4 * tt + 4)
                                VE.op(v.tensor_tensor, xat[:, 0:512], xat[:, 0:512],
                                      P[2 * qq][:, :], ALU.add)
                                VE.op(v.tensor_tensor, xat[:, 512:1024], xat[:, 512:1024],
                                      P[2 * qq + 1][:, :], ALU.add)
                                VE.op(v.tensor_scalar, SXN[:, 0:1024], xat, 1.0, 0.0,
                                      ALU.mult, ALU.add, accum_out=STB[:, tt:tt + 1])
                            VE.op(v.tensor_scalar, STB[:, 64:96], STB[:, 0:32], 1.0 / D,
                                  None, ALU.mult)
                            VE.op(v.tensor_tensor, STB[:, 96:128], STB[:, 64:96],
                                  STB[:, 64:96], ALU.mult)
                            VE.wait(SC, sc0 + 32)
                            VE.op(v.scalar_tensor_tensor, STB[:, 128:160], STB[:, 32:64],
                                  1.0 / D, STB[:, 96:128], ALU.mult, ALU.subtract)
                            VE.wait(SC, sc0 + 34)
                            VE.op(v.scalar_tensor_tensor, NB2[:, :], STB[:, 64:96], -1.0,
                                  RSTD2[:, :], ALU.mult, ALU.mult)

                        @blk.scalar
                        def _(s):
                            SC.eng = s
                            for tt in range(NT):
                                SC.wait(VE, ve0 + 3 * tt + 2)
                                SC.op(s.activation, SSQ[:, :], XA[:, tt * D:(tt + 1) * D],
                                      AF.Square, accum_out=STB[:, 32 + tt:33 + tt])
                            SC.wait(VE, ve0 + 99)
                            SC.op(s.activation, STB[:, 160:192], STB[:, 128:160], AF.Ln,
                                  bias=cp("eps"))
                            SC.op(s.activation, RSTD2[:, :], STB[:, 160:192], AF.Exp,
                                  scale=cp("neg05"))

                    if debug:
                        with nc.Block() as blk:
                            @blk.sync
                            def _(s):
                                for k in range(8):
                                    dst = dbgs["dbg_xa4"][512 * k:512 * (k + 1), :].rearrange(
                                        "(j p) f -> p j f", p=128)
                                    src_ = XA[:, 4 * k * D:(4 * k + 4) * D].rearrange(
                                        "p (j f) -> p j f", j=4)
                                    dma(LD, s, dst, src_)
                                s.wait_ge(sems["ld"], LD.n)

            if upto >= 4:
                esA5 = ExitStack()
                with esA5:
                    WQKV = sbt("WQKV", [128, 12288], BF16, esA5)
                    WOUT = sbt("WOUT", [128, 4096], BF16, esA5)
                    BVB = sbt("BVB", [128, 512], BF16, esA5)
                    BKB = sbt("BKB", [128, 512], BF16, esA5)
                    MSK4 = sbt("MSK4", [128, 512], F32, esA5)
                    PRB = sbt("PRB", [1, 512], BF16, esA5)
                    def QTBs(p0, p1, c0, c1):
                        return SSQ[p0:p1, c0 // 2:c1 // 2].bitcast(BF16)

                    def KTBs(p0, p1, c0, c1):
                        return SSQ[p0:p1, 512 + c0 // 2:512 + c1 // 2].bitcast(BF16)
                    VAB = sbt("VAB", [128, 512], BF16, esA5)
                    KAB = sbt("KAB", [128, 512], BF16, esA5)
                    PBC = sbt("PBC", [128, 128], BF16, esA5)
                    RPBC = sbt("RPBC", [128, 128], BF16, esA5)
                    SMT = sbt("SMT", [128, 1024], BF16, esA5)
                    OTB = sbt("OTB", [128, 512], BF16, esA5)
                    WHEB = sbt("WHEB", [64, 512], F32, esA5)
                    WHEBB = sbt("WHEBB", [64, 512], BF16, esA5)

                    emit_gb_broadcast("ln2g", "ln2b")
                    emit_row_broadcast("b_out", BCAST)
                    with nc.Block() as blk:
                        @blk.gpsimd
                        def _(g):
                            for c in range(8):
                                dma(GLD, g, WQKV[:, c * 1536:(c + 1) * 1536],
                                    t["w_qkv"][128 * c:128 * (c + 1), :])
                            for c in range(4):
                                dma(GLD, g, WOUT[:, c * 1024:(c + 1) * 1024],
                                    t["w_out"][128 * c:128 * (c + 1), :])
                            dma(GLD, g, BVB[0:1, 0:512], t["b_qkv_v"][:, :])
                            dma(GLD, g, BKB[0:1, 0:512], t["b_qkv_k"][:, :])
                            g.wait_ge(sems["gld"], GLD.n)

                        @blk.sync
                        def _(s):
                            dma(LD, s, MSK4[:, :], t["MASKETA4"][:, :])
                            s.wait_ge(sems["ld"], LD.n)

                    with nc.Block() as blk:
                        pe0v = PE.n

                        @blk.tensor
                        def _(te):
                            PE.eng = te
                            PE.op(te.matmul, P[6][:, 0:512], ONESRS[0:1, 0:128], BVB[0:1, 0:512],
                                  start=True, stop=True)
                            PE.op(te.matmul, P[7][:, 0:512], ONESRS[0:1, 0:128], BKB[0:1, 0:512],
                                  start=True, stop=True)

                        @blk.vector
                        def _(v):
                            VE.eng = v
                            VE.wait(PE, pe0v + 2)
                            VE.op(v.tensor_copy, BVB[:, 0:512], P[6][:, 0:512])
                            VE.op(v.tensor_copy, BKB[:, 0:512], P[7][:, 0:512])
                            VE.op(v.memset, WHEB[:, :], 0.0)
                            VE.op(v.memset, WHEBB[:, :], 0.0)

                    import os
                    N5 = int(os.environ.get("K5N", NT))
                    STG = int(os.environ.get("K5STG", 9))
                    with nc.Block() as blk:
                        sc0, ve0, pe0, gld0 = SC.n, VE.n, PE.n, GLD.n
                        SCP, VEP, PEP = 30, 29, 130
                        s_ = lambda c, k: sc0 + SCP * c + k
                        v_ = lambda c, k: ve0 + VEP * c + k
                        TRD = lambda c: pe0 + PEP * c + 8
                        QD = lambda c: pe0 + PEP * c + 40
                        KD = lambda c: pe0 + PEP * c + 72
                        VD = lambda c: pe0 + PEP * c + 80
                        KAD = lambda c: pe0 + PEP * c + 88
                        BCD = lambda c: pe0 + PEP * c + 90
                        SD = lambda c: pe0 + PEP * c + 98
                        OD = lambda c: pe0 + PEP * c + 114
                        DWD = lambda c: pe0 + PEP * c + 122
                        WOD = lambda c: pe0 + PEP * c + 130

                        @blk.gpsimd
                        def _(g):
                            for c in range(N5):
                                if STG < 3:
                                    continue
                                qq = c % 2
                                if c >= 2:
                                    g.wait_ge(sems["pe"], BCD(c - 2))
                                dma(GLD, g, PRB[0:1, 256 * qq:256 * qq + 128],
                                    t["rowscr"][0:1, c * 128:(c + 1) * 128])
                                dma(GLD, g, PRB[0:1, 256 * qq + 128:256 * qq + 256],
                                    t["rowscr2"][0:1, c * 128:(c + 1) * 128])

                        @blk.scalar
                        def _(s):
                            SC.eng = s
                            for c in range(N5):
                                qq = c % 2
                                xat = XA[:, c * D:(c + 1) * D]
                                if c >= 2:
                                    SC.wait(VE, v_(c - 2, 2))
                                SC.op(s.activation, SXN[:, qq * D:(qq + 1) * D], xat, AF.Identity,
                                      scale=RSTD2[:, c:c + 1], bias=NB2[:, c:c + 1])
                                SC.wait(PE, TRD(c))
                                SC.op(s.activation, TB[:, qq * 2048:qq * 2048 + 512],
                                      P[2 * qq][:, :], AF.Identity)
                                SC.op(s.activation, TB[:, qq * 2048 + 512:qq * 2048 + 1024],
                                      P[2 * qq + 1][:, :], AF.Identity)
                                if STG < 2:
                                    continue
                                SC.wait(PE, QD(c))
                                for h in range(8):
                                    SC.op(s.activation,
                                          SXG[0:64, qq * D + 128 * h:qq * D + 128 * (h + 1)],
                                          P[4][64 * (h % 2):64 * (h % 2) + 64,
                                               128 * (h // 2):128 * (h // 2 + 1)], AF.Identity,
                                          bias=BQC[64 * (h % 2):64 * (h % 2) + 64, h // 2:h // 2 + 1])
                                SC.wait(PE, KD(c))
                                for h in range(8):
                                    SC.op(s.activation,
                                          SXG[64:128, qq * D + 128 * h:qq * D + 128 * (h + 1)],
                                          P[5][64 * (h % 2):64 * (h % 2) + 64,
                                               128 * (h // 2):128 * (h // 2 + 1)], AF.Identity,
                                          bias=BQC[64 * (h % 2):64 * (h % 2) + 64, 4 + h // 2:5 + h // 2])
                                if STG < 3:
                                    continue
                                SC.wait(PE, BCD(c))
                                SC.op(s.activation, PBC[:, :], P[2 * qq][:, 0:128], AF.Identity)
                                SC.op(s.activation, RPBC[:, :], P[2 * qq][:, 128:256], AF.Identity)
                                if STG < 6:
                                    continue
                                SC.wait(PE, OD(c))
                                for h in range(8):
                                    pb = P[4] if h < 4 else P[6]
                                    SC.op(s.activation,
                                          OTB[64 * (h % 2):64 * (h % 2) + 64,
                                              128 * (h // 2):128 * (h // 2 + 1)],
                                          pb[0:64, 128 * (h % 4):128 * (h % 4 + 1)], AF.Identity)
                                if STG < 7:
                                    continue
                                SC.wait(VE, v_(c, 25))
                                SC.op(s.activation, WHEBB[:, :], WHEB[:, :], AF.Identity)

                        @blk.vector
                        def _(v):
                            VE.eng = v
                            for c in range(N5):
                                qq = c % 2
                                xat = XA[:, c * D:(c + 1) * D]
                                VE.wait(SC, s_(c, 1))
                                VE.op(v.tensor_tensor, SXG[:, qq * D:(qq + 1) * D],
                                      SXN[:, qq * D:(qq + 1) * D], GGB[:, 0:D], ALU.mult)
                                VE.op(v.tensor_tensor, SXG[:, qq * D:(qq + 1) * D],
                                      SXG[:, qq * D:(qq + 1) * D], GGB[:, D:2 * D], ALU.add)
                                if STG < 2:
                                    continue
                                VE.wait(PE, VD(c))
                                if c >= 1:
                                    VE.wait(PE, DWD(c - 1))
                                VE.op(v.tensor_tensor, VAB[:, :], P[6][:, :], BVB[:, :], ALU.add)
                                VE.wait(PE, KAD(c))
                                VE.op(v.tensor_tensor, KAB[:, :], P[7][:, :], BKB[:, :], ALU.add)
                                VE.op(v.tensor_scalar, KAB[:, :], KAB[:, :], RPETA[:, c:c + 1],
                                      None, ALU.mult)
                                if STG < 4:
                                    continue
                                VE.wait(SC, s_(c, 21))
                                if c >= 1:
                                    VE.wait(PE, OD(c - 1))
                                for h in range(8):
                                    VE.op(v.tensor_tensor, QTBs(0, 64, 128 * h, 128 * (h + 1)),
                                          SXG[0:64, qq * D + 128 * h:qq * D + 128 * (h + 1)],
                                          PBC[0:64, :], ALU.mult)
                                for h in range(8):
                                    VE.op(v.tensor_tensor, KTBs(0, 64, 128 * h, 128 * (h + 1)),
                                          SXG[64:128, qq * D + 128 * h:qq * D + 128 * (h + 1)],
                                          RPBC[64:128, :], ALU.mult)
                                if STG < 5:
                                    continue
                                VE.wait(PE, SD(c))
                                VE.op(v.tensor_tensor, SMT[:, 0:512], P[2 * qq][:, :],
                                      MSK4[:, :], ALU.mult)
                                VE.op(v.tensor_tensor, SMT[:, 512:1024], P[2 * qq + 1][:, :],
                                      MSK4[:, :], ALU.mult)
                                if STG < 7:
                                    continue
                                VE.wait(PE, DWD(c))
                                VE.op(v.tensor_tensor, WHEB[:, :], WHEB[:, :], P[5][0:64, :], ALU.add)
                                VE.op(v.tensor_scalar, WHEB[:, :], WHEB[:, :], PLB2[0:64, c:c + 1],
                                      None, ALU.mult)
                                if STG < 8:
                                    continue
                                VE.wait(PE, WOD(c))
                                VE.op(v.tensor_tensor, xat[:, 0:512], xat[:, 0:512],
                                      P[6][:, :], ALU.add)
                                VE.op(v.tensor_tensor, xat[:, 512:1024], xat[:, 512:1024],
                                      P[7][:, :], ALU.add)
                                VE.op(v.tensor_tensor, xat[:, 0:512], xat[:, 0:512],
                                      BCAST[:, 0:512], ALU.add)
                                VE.op(v.tensor_tensor, xat[:, 512:1024], xat[:, 512:1024],
                                      BCAST[:, 512:1024], ALU.add)

                        @blk.tensor
                        def _(te):
                            PE.eng = te
                            for c in range(N5):
                                qq = c % 2
                                PE.wait(VE, v_(c, 2))
                                if c >= 2:
                                    PE.wait(VE, v_(c - 2, 23))
                                    PE.wait(SC, s_(c - 2, SCP))
                                for j in range(8):
                                    bank = P[2 * qq + j // 4]
                                    PE.op(te.matmul, bank[:, (j % 4) * 128:(j % 4 + 1) * 128],
                                          SXG[:, qq * D + j * 128:qq * D + (j + 1) * 128],
                                          IDCS[:, :], start=True, stop=True)
                                if STG < 2:
                                    continue
                                PE.wait(SC, s_(c, 3))
                                if c >= 1 and STG >= 7:
                                    PE.wait(SC, s_(c - 1, 30))
                                if c >= 1 and STG >= 8:
                                    PE.wait(VE, v_(c - 1, 29))
                                if c >= 1 and STG >= 6:
                                    PE.wait(SC, s_(c - 1, 29))
                                for mt in range(4):
                                    for cc in range(8):
                                        PE.op(te.matmul, P[4][:, 128 * mt:128 * (mt + 1)],
                                              WQKV[:, cc * 1536 + 128 * mt:cc * 1536 + 128 * (mt + 1)],
                                              TB[:, qq * 2048 + cc * 128:qq * 2048 + (cc + 1) * 128],
                                              start=(cc == 0), stop=(cc == 7))
                                for mt in range(4):
                                    for cc in range(8):
                                        PE.op(te.matmul, P[5][:, 128 * mt:128 * (mt + 1)],
                                              WQKV[:, cc * 1536 + 512 + 128 * mt:cc * 1536 + 512 + 128 * (mt + 1)],
                                              TB[:, qq * 2048 + cc * 128:qq * 2048 + (cc + 1) * 128],
                                              start=(cc == 0), stop=(cc == 7))
                                for cc in range(8):
                                    PE.op(te.matmul, P[6][:, :],
                                          TB[:, qq * 2048 + cc * 128:qq * 2048 + (cc + 1) * 128],
                                          WQKV[:, cc * 1536 + 1024:cc * 1536 + 1536],
                                          start=(cc == 0), stop=(cc == 7))
                                for cc in range(8):
                                    PE.op(te.matmul, P[7][:, :],
                                          TB[:, qq * 2048 + cc * 128:qq * 2048 + (cc + 1) * 128],
                                          WQKV[:, cc * 1536 + 512:cc * 1536 + 1024],
                                          start=(cc == 0), stop=(cc == 7))
                                if STG < 3:
                                    continue
                                PE.wait(GLD, gld0 + 32 * min(max(c + 1, 2), N5))
                                PE.op(te.matmul, P[2 * qq][:, 0:128], ONESRS[0:1, 0:128],
                                      PRB[0:1, 256 * qq:256 * qq + 128], start=True, stop=True)
                                PE.op(te.matmul, P[2 * qq][:, 128:256], ONESRS[0:1, 0:128],
                                      PRB[0:1, 256 * qq + 128:256 * qq + 256], start=True, stop=True)
                                if STG < 5:
                                    continue
                                PE.wait(VE, v_(c, 21))
                                PE.wait(SC, s_(c, 21))
                                for h in range(8):
                                    PE.op(te.matmul,
                                          P[2 * qq + h // 4][:, 128 * (h % 4):128 * (h % 4 + 1)],
                                          KTBs(0, 64, 128 * h, 128 * (h + 1)),
                                          QTBs(0, 64, 128 * h, 128 * (h + 1)),
                                          start=True, stop=True)
                                if STG < 6:
                                    continue
                                PE.wait(VE, v_(c, 23))
                                for h in range(8):
                                    pb = P[4] if h < 4 else P[6]
                                    ops = pb[0:64, 128 * (h % 4):128 * (h % 4 + 1)]
                                    PE.op(te.matmul, ops, WHEBB[0:64, 64 * h:64 * (h + 1)],
                                          QTBs(0, 64, 128 * h, 128 * (h + 1)),
                                          start=True, stop=False)
                                    PE.op(te.matmul, ops, VAB[:, 64 * h:64 * (h + 1)],
                                          SMT[:, 128 * h:128 * (h + 1)],
                                          start=False, stop=True)
                                if STG < 7:
                                    continue
                                for h in range(8):
                                    PE.op(te.matmul, P[5][0:64, 64 * h:64 * (h + 1)],
                                          KAB[:, 64 * h:64 * (h + 1)], VAB[:, 64 * h:64 * (h + 1)],
                                          start=True, stop=True)
                                if STG < 8:
                                    continue
                                PE.wait(SC, s_(c, 29))
                                for j in range(2):
                                    for pch in range(4):
                                        PE.op(te.matmul, P[6 + j][:, :],
                                              OTB[:, 128 * pch:128 * (pch + 1)],
                                              WOUT[:, pch * 1024 + j * 512:pch * 1024 + (j + 1) * 512],
                                              start=(pch == 0), stop=(pch == 3))

                    if debug:
                        with nc.Block() as blk:
                            @blk.sync
                            def _(s):
                                for k in range(8):
                                    dst = dbgs["dbg_xa5"][512 * k:512 * (k + 1), :].rearrange(
                                        "(j p) f -> p j f", p=128)
                                    src_ = XA[:, 4 * k * D:(4 * k + 4) * D].rearrange(
                                        "p (j f) -> p j f", j=4)
                                    dma(LD, s, dst, src_)
                                s.wait_ge(sems["ld"], LD.n)


        if upto >= 5:
            with nc.Block() as blk:
                sc0, ve0 = SC.n, VE.n

                @blk.scalar
                def _(s):
                    SC.eng = s
                    for tt in range(NT):
                        SC.op(s.activation, SSQ[:, :], XA[:, tt * D:(tt + 1) * D],
                              AF.Square, accum_out=STB[:, 32 + tt:33 + tt])
                    SC.wait(VE, ve0 + 35)
                    SC.op(s.activation, STB[:, 160:192], STB[:, 128:160], AF.Ln, bias=cp("eps"))
                    SC.op(s.activation, RSTD3[:, :], STB[:, 160:192], AF.Exp, scale=cp("neg05"))

                @blk.vector
                def _(v):
                    VE.eng = v
                    for tt in range(NT):
                        VE.op(v.tensor_scalar, SXN[:, 0:1024], XA[:, tt * D:(tt + 1) * D],
                              1.0, 0.0, ALU.mult, ALU.add, accum_out=STB[:, tt:tt + 1])
                    VE.op(v.tensor_scalar, STB[:, 64:96], STB[:, 0:32], 1.0 / D, None, ALU.mult)
                    VE.op(v.tensor_tensor, STB[:, 96:128], STB[:, 64:96], STB[:, 64:96], ALU.mult)
                    VE.wait(SC, sc0 + 32)
                    VE.op(v.scalar_tensor_tensor, STB[:, 128:160], STB[:, 32:64], 1.0 / D,
                          STB[:, 96:128], ALU.mult, ALU.subtract)
                    VE.wait(SC, sc0 + 34)
                    VE.op(v.scalar_tensor_tensor, NB3[:, :], STB[:, 64:96], -1.0,
                          RSTD3[:, :], ALU.mult, ALU.mult)

        if upto >= 5:
            emit_gb_broadcast("ln3g", "ln3b")
            emit_row_broadcast("b_ff2", BCAST)
            esA6 = ExitStack()
            with esA6:
                WF1 = sbt("WF1", [128, 2 * 2048], BF16, esA6)
                WF2 = sbt("WF2", [128, 2 * 2048], BF16, esA6)
                H1B = sbt("H1B", [128, 32 * 512], BF16, esA6)
                BF1C = sbt("BF1C", [128, 32], F32, esA6)
                SF1 = SSQ
                SF3H = lambda m: SXG[:, (m % 2) * 512:(m % 2 + 1) * 512]
                SF1H = lambda m: SSQ[:, (m % 2) * 512:(m % 2 + 1) * 512]

                with nc.Block() as blk:
                    gld0, ld0 = GLD.n, LD.n

                    @blk.sync
                    def _(s):
                        dma(LD, s, BF1C[:, :], t["b_ff1"][:, :])
                        for i in range(16):
                            c, half = i // 2, i % 2
                            s.wait_ge(sems["gld"], gld0 + 16 * (i + 1))
                            dma(LD, s, t["wff1bf"][:, c * 4096 + half * 2048:c * 4096 + (half + 1) * 2048],
                                WF1[:, half * 2048:(half + 1) * 2048])
                        for k in range(32):
                            half = k % 2
                            s.wait_ge(sems["gld"], gld0 + 16 * (16 + k + 1))
                            dma(LD, s, t["wff2bf"][:, k * 1024:(k + 1) * 1024],
                                WF2[:, half * 2048:half * 2048 + 1024])

                    @blk.gpsimd
                    def _(g):
                        for i in range(16):
                            c, half = i // 2, i % 2
                            if i >= 2:
                                g.wait_ge(sems["ld"], ld0 + 16 * i)
                            dma(GLD, g, WF1[:, half * 2048:(half + 1) * 2048],
                                t["w_ff1"][128 * c:128 * (c + 1), half * 2048:(half + 1) * 2048])
                        for k in range(32):
                            half = k % 2
                            if k >= 2:
                                g.wait_ge(sems["ld"], ld0 + 16 * (16 + k))
                            dma(GLD, g, WF2[:, half * 2048:half * 2048 + 1024],
                                t["w_ff2"][128 * k:128 * (k + 1), :])
                        g.wait_ge(sems["ld"], ld0 + 16 * 49)

                GA = 0.044715 * 1.5957691216057308
                GB = 1.5957691216057308
                TB3 = TB[:, :].rearrange("p (c f) -> p c f", c=8)
                W1BF = t["wff1bf"].rearrange("p (c f) -> p c f", c=8)

                with nc.Block() as blk:
                    sc0, ve0, pe0, ld0 = SC.n, VE.n, PE.n, LD.n
                    SCP6, VEP6, PEP6 = 108, 120, 544
                    f1d = lambda b, mg: pe0 + PEP6 * b + 32 + 16 * (mg + 1)
                    f1dm = lambda b, m: pe0 + PEP6 * b + 32 + 8 * (m + 1)
                    f2d = lambda b, g: pe0 + PEP6 * b + 288 + 16 * (g + 1)
                    trd = lambda b, tb: pe0 + PEP6 * b + 8 * (tb + 1)
                    s_ap = lambda b, tb: sc0 + SCP6 * b + tb + 1
                    s_tev = lambda b, tb: sc0 + SCP6 * b + 4 + 2 * (tb + 1)
                    s_gel = lambda b, m, j: sc0 + SCP6 * b + 12 + 3 * m + j
                    v_gb = lambda b, tb: ve0 + VEP6 * b + 2 * (tb + 1)
                    v_gel = lambda b, m, j: ve0 + VEP6 * b + 8 + 3 * m + j
                    v_ev = lambda b, k: ve0 + VEP6 * b + 104 + k

                    @blk.sync
                    def _(s):
                        for b in range(8):
                            for mg in range(16):
                                if mg >= 2:
                                    s.wait_ge(sems["pe"], f1d(b, mg - 2))
                                elif b >= 1:
                                    s.wait_ge(sems["pe"], pe0 + PEP6 * b)
                                dst = WF1[:, (mg % 2) * 2048:(mg % 2 + 1) * 2048].rearrange(
                                    "p (c f) -> p c f", c=8)
                                dma(LD, s, dst, W1BF[:, :, mg * 256:(mg + 1) * 256])
                            for g in range(16):
                                if g >= 2:
                                    s.wait_ge(sems["pe"], f2d(b, g - 2))
                                dma(LD, s, WF2[:, (g % 2) * 2048:(g % 2 + 1) * 2048],
                                    t["wff2bf"][:, g * 2048:(g + 1) * 2048])

                    @blk.scalar
                    def _(s):
                        SC.eng = s
                        for b in range(8):
                            for tb in range(4):
                                tt = 4 * b + tb
                                qq = tb % 2
                                if tb >= 2:
                                    SC.wait(VE, v_gb(b, tb - 2))
                                elif b >= 1:
                                    SC.wait(VE, ve0 + VEP6 * b)
                                SC.op(s.activation, SXN[:, qq * D:(qq + 1) * D],
                                      XA[:, tt * D:(tt + 1) * D], AF.Identity,
                                      scale=RSTD3[:, tt:tt + 1], bias=NB3[:, tt:tt + 1])
                            for tb in range(4):
                                SC.wait(PE, trd(b, tb))
                                SC.op(s.activation, TB3[:, 0:4, 128 * tb:128 * (tb + 1)],
                                      P[2 * (tb % 2)][:, :], AF.Identity)
                                SC.op(s.activation, TB3[:, 4:8, 128 * tb:128 * (tb + 1)],
                                      P[2 * (tb % 2) + 1][:, :], AF.Identity)
                            for m in range(32):
                                bank = P[m % 4]
                                SC.wait(PE, f1dm(b, m))
                                SC.op(s.activation, SF3H(m), bank[:, :], AF.Identity,
                                      bias=BF1C[:, m:m + 1])
                                SC.op(s.activation, SF1H(m), bank[:, :], AF.Square,
                                      bias=BF1C[:, m:m + 1])
                                SC.wait(VE, v_gel(b, m, 2))
                                SC.op(s.activation, SF1H(m), SF1H(m), AF.Sigmoid)

                    @blk.vector
                    def _(v):
                        VE.eng = v
                        for b in range(8):
                            for tb in range(4):
                                qq = tb % 2
                                VE.wait(SC, s_ap(b, tb))
                                if b >= 1 and tb == 0:
                                    VE.wait(PE, pe0 + PEP6 * b)
                                if tb >= 2:
                                    VE.wait(PE, trd(b, tb - 2))
                                VE.op(v.tensor_tensor, SXG[:, qq * D:(qq + 1) * D],
                                      SXN[:, qq * D:(qq + 1) * D], GGB[:, 0:D], ALU.mult)
                                VE.op(v.tensor_tensor, SXG[:, qq * D:(qq + 1) * D],
                                      SXG[:, qq * D:(qq + 1) * D], GGB[:, D:2 * D], ALU.add)
                            for m in range(32):
                                VE.wait(SC, s_gel(b, m, 2))
                                VE.op(v.tensor_scalar, SF1H(m), SF1H(m), GA, GB, ALU.mult, ALU.add)
                                VE.op(v.tensor_tensor, SF1H(m), SF3H(m), SF1H(m), ALU.mult)
                                VE.wait(SC, s_gel(b, m, 3))
                                VE.op(v.tensor_tensor, H1B[:, m * 512:(m + 1) * 512],
                                      SF3H(m), SF1H(m), ALU.mult)
                            VE.wait(PE, pe0 + PEP6 * (b + 1))
                            k = 0
                            for tb in range(4):
                                tt = 4 * b + tb
                                for j in range(2):
                                    xh = XA[:, tt * D + j * 512:tt * D + (j + 1) * 512]
                                    k += 1
                                    VE.op(v.tensor_tensor, xh, xh, P[2 * tb + j][:, :], ALU.add)
                                    k += 1
                                    VE.op(v.tensor_tensor, xh, xh,
                                          BCAST[:, j * 512:(j + 1) * 512], ALU.add)

                    @blk.tensor
                    def _(te):
                        PE.eng = te
                        for b in range(8):
                            for tb in range(4):
                                qq = tb % 2
                                PE.wait(VE, v_gb(b, tb))
                                if b >= 1 and tb <= 1:
                                    PE.wait(VE, ve0 + VEP6 * b)
                                if tb >= 2:
                                    PE.wait(SC, s_tev(b, tb - 2))
                                for j in range(8):
                                    bank = P[2 * qq + j // 4]
                                    PE.op(te.matmul, bank[:, (j % 4) * 128:(j % 4 + 1) * 128],
                                          SXG[:, qq * D + j * 128:qq * D + (j + 1) * 128],
                                          IDCS[:, :], start=True, stop=True)
                            for m in range(32):
                                mg = m // 2
                                bank = P[m % 4]
                                PE.wait(SC, sc0 + SCP6 * b + 12)
                                PE.wait(LD, ld0 + 16 * (32 * b + mg + 1))
                                if m >= 4:
                                    PE.wait(SC, s_gel(b, m - 4, 2))
                                for cc in range(8):
                                    lh = WF1[:, (mg % 2) * 2048 + cc * 256 + (m % 2) * 128:
                                             (mg % 2) * 2048 + cc * 256 + (m % 2) * 128 + 128]
                                    PE.op(te.matmul, bank[:, :], lh,
                                          TB[:, cc * 512:(cc + 1) * 512],
                                          start=(cc == 0), stop=(cc == 7))
                            for g in range(16):
                                PE.wait(LD, ld0 + 16 * (32 * b + 16 + g + 1))
                                PE.wait(VE, v_gel(b, 4 * g + 3 if 4 * g + 3 < 32 else 31, 3))
                                if g == 0:
                                    PE.wait(SC, s_gel(b, 31, 2))
                                for tb in range(4):
                                    for j in range(2):
                                        pb = P[2 * tb + j]
                                        for ck in range(2):
                                            kk = 2 * g + ck
                                            PE.op(te.matmul, pb[:, :],
                                                  H1B[:, kk * 512 + 128 * tb:kk * 512 + 128 * (tb + 1)],
                                                  WF2[:, (g % 2) * 2048 + ck * 1024 + j * 512:
                                                      (g % 2) * 2048 + ck * 1024 + (j + 1) * 512],
                                                  start=(g == 0 and ck == 0),
                                                  stop=(g == 15 and ck == 1))

            if debug:
                with nc.Block() as blk:
                    @blk.sync
                    def _(s):
                        for k in range(8):
                            dst = dbgs["dbg_xa6"][512 * k:512 * (k + 1), :].rearrange(
                                "(j p) f -> p j f", p=128)
                            src_ = XA[:, 4 * k * D:(4 * k + 4) * D].rearrange(
                                "p (j f) -> p j f", j=4)
                            dma(LD, s, dst, src_)
                        s.wait_ge(sems["ld"], LD.n)

        if upto >= 6:
            XINB = TB[:, :].bitcast(F32)
            B0RSTD = STB[:, 192:224]
            B0NB = STB[:, 224:256]
            with nc.Block() as blk:
                sc0, ve0, ld0 = SC.n, VE.n, LD.n

                @blk.scalar
                def _(s):
                    SC.eng = s
                    for tt in range(NT):
                        SC.op(s.activation, SSQ[:, :], XA[:, tt * D:(tt + 1) * D],
                              AF.Square, accum_out=STB[:, 32 + tt:33 + tt])
                    SC.wait(VE, ve0 + 35)
                    SC.op(s.activation, STB[:, 160:192], STB[:, 128:160], AF.Ln, bias=cp("eps"))
                    SC.op(s.activation, B0RSTD[:, :], STB[:, 160:192], AF.Exp, scale=cp("neg05"))
                    for tt in range(NT):
                        if tt == 0:
                            SC.wait(VE, ve0 + 36)
                        SC.op(s.activation, XA[:, tt * D:(tt + 1) * D],
                              XA[:, tt * D:(tt + 1) * D], AF.Identity,
                              scale=B0RSTD[:, tt:tt + 1], bias=B0NB[:, tt:tt + 1])

                @blk.vector
                def _(v):
                    VE.eng = v
                    for tt in range(NT):
                        VE.op(v.tensor_scalar, SXN[:, 0:1024], XA[:, tt * D:(tt + 1) * D],
                              1.0, 0.0, ALU.mult, ALU.add, accum_out=STB[:, tt:tt + 1])
                    VE.op(v.tensor_scalar, STB[:, 64:96], STB[:, 0:32], 1.0 / D, None, ALU.mult)
                    VE.op(v.tensor_tensor, STB[:, 96:128], STB[:, 64:96], STB[:, 64:96], ALU.mult)
                    VE.wait(SC, sc0 + 32)
                    VE.op(v.scalar_tensor_tensor, STB[:, 128:160], STB[:, 32:64], 1.0 / D,
                          STB[:, 96:128], ALU.mult, ALU.subtract)
                    VE.wait(SC, sc0 + 34)
                    VE.op(v.scalar_tensor_tensor, B0NB[:, :], STB[:, 64:96], -1.0,
                          B0RSTD[:, :], ALU.mult, ALU.mult)
                    for tt in range(NT):
                        VE.wait(SC, sc0 + 35 + tt)
                        VE.wait(LD, ld0 + 32 * tt + 16)
                        VE.op(v.tensor_tensor, XA[:, tt * D:(tt + 1) * D],
                              XA[:, tt * D:(tt + 1) * D],
                              XINB[:, (tt % 2) * D:(tt % 2 + 1) * D], ALU.add)

                @blk.sync
                def _(s):
                    for tt in range(NT):
                        if tt >= 2:
                            s.wait_ge(sems["ve"], ve0 + 37 + (tt - 2))
                        dma(LD, s, XINB[:, (tt % 2) * D:(tt % 2 + 1) * D],
                            xin[128 * tt:128 * (tt + 1), :])
                        s.wait_ge(sems["ve"], ve0 + 37 + tt)
                        dma(LD, s, t["x2s"][128 * tt:128 * (tt + 1), :],
                            XA[:, tt * D:(tt + 1) * D])

            if debug:
                with nc.Block() as blk:
                    @blk.sync
                    def _(s):
                        dma(LD, s, dbgs["dbg_stb"][:, :], STB[:, :])
                        for k in range(8):
                            dst = dbgs["dbg_x2"][512 * k:512 * (k + 1), :].rearrange(
                                "(j p) f -> p j f", p=128)
                            src_ = XA[:, 4 * k * D:(4 * k + 4) * D].rearrange(
                                "p (j f) -> p j f", j=4)
                            dma(LD, s, dst, src_)
                        s.wait_ge(sems["ld"], LD.n)

        if upto >= 7:
            XA_scope.close()
            OT = es.enter_context(nc.sbuf_tensor("OT", [128, 8192], BF16, side="right"))
            t["OT"] = OT
            esB = ExitStack()
            QselT = esB.enter_context(nc.sbuf_tensor("QselT", [128, 8192], BF16, side="right"))
            KselT = esB.enter_context(nc.sbuf_tensor("KselT", [128, 8192], BF16, side="right"))
            VB = esB.enter_context(nc.sbuf_tensor("VB", [128, 8320], BF16, side="right"))
            esXB = ExitStack()
            xbT = esXB.enter_context(nc.sbuf_tensor("xbT", [128, 32768], BF16, side="right"))
            X2SC = sbt("X2SC", [128, 2048], F32, esB)
            TACC = sbt("TACC", [128, 16], F32, esB)
            TACCB = sbt("TACCB", [128, 16], BF16, esB)
            WRS = sbt("WRS", [128, 128], BF16, esB)
            BRC = sbt("BRC", [16, 1], F32, esB)
            BRR = sbt("BRR", [1, 16], F32, esB)
            R16C = sbt("R16C", [16, 1], F32, esB)
            R16R = sbt("R16R", [1, 16], F32, esB)
            R16RB = sbt("R16RB", [1, 16], BF16, esB)
            MB16 = sbt("MB16", [16, 16], F32, esB)
            M16 = sbt("M16", [16, 1], F32, esB)
            M16B = sbt("M16B", [16, 1], BF16, esB)
            RANK = sbt("RANK", [16, 1], F32, esB)
            S16F = sbt("S16F", [16, 4], F32, esB)
            S16B = sbt("S16B", [16, 4], BF16, esB)
            U16S = sbt("U16S", [16, 16], BF16, esB)
            IOTA4S = sbt("IOTA4S", [16, 4], F32, esB)
            xbT3 = xbT[:, :].rearrange("p (c f) -> p c f", c=8)

            with nc.Block() as blk:
                @blk.gpsimd
                def _(g):
                    dma(GLD, g, WRS[:, :], t["Wr_c"][:, :])
                    g.wait_ge(sems["gld"], GLD.n)

                @blk.sync
                def _(s):
                    dma(LD, s, BRC[:, :], t["br_c"][:, :])
                    dma(LD, s, BRR[:, :], t["br_r"][:, :])
                    dma(LD, s, U16S[:, :], t["U16"][:, :])
                    dma(LD, s, IOTA4S[:, :], t["IOTA4"][:, :])
                    s.wait_ge(sems["ld"], LD.n)

                @blk.vector
                def _(v):
                    VE.eng = v
                    VE.op(v.memset, TACC[:, :], 0.0)

            with nc.Block() as blk:
                sc0, ve0, pe0, ld0 = SC.n, VE.n, PE.n, LD.n

                @blk.sync
                def _(s):
                    for tt in range(NT):
                        qq = tt % 2
                        if tt >= 2:
                            s.wait_ge(sems["sc"], sc0 + 6 * (tt - 2) + 4)
                        dma(LD, s, X2SC[:, qq * D:(qq + 1) * D],
                            t["x2s"][128 * tt:128 * (tt + 1), :])

                @blk.scalar
                def _(s):
                    SC.eng = s
                    for tt in range(NT):
                        qq = tt % 2
                        x2t = X2SC[:, qq * D:(qq + 1) * D]
                        SC.wait(LD, ld0 + 16 * (tt + 1))
                        if tt >= 2:
                            SC.wait(VE, ve0 + 6 * (tt - 2) + 1)
                        SC.op(s.activation, SSQ[:, :], x2t, AF.Square, accum_out=STAT[:, 24:25])
                        SC.wait(VE, ve0 + 6 * tt + 4)
                        SC.op(s.activation, STAT[:, 25:26], STAT[:, 23:24], AF.Ln, bias=cp("eps"))
                        SC.op(s.activation, STAT[:, 26:27], STAT[:, 25:26], AF.Exp, scale=cp("neg05"))
                        SC.wait(VE, ve0 + 6 * tt + 5)
                        if tt >= 2:
                            SC.wait(PE, pe0 + 16 * (tt - 2) + 8)
                        SC.op(s.activation, SXN[:, qq * D:(qq + 1) * D], x2t, AF.Identity,
                              scale=STAT[:, 26:27], bias=STAT[:, 27:28])
                        SC.wait(PE, pe0 + 16 * tt + 8)
                        SC.op(s.activation, xbT3[:, 0:4, tt * 128:(tt + 1) * 128],
                              P[2 * qq][:, :], AF.Identity)
                        SC.op(s.activation, xbT3[:, 4:8, tt * 128:(tt + 1) * 128],
                              P[2 * qq + 1][:, :], AF.Identity)

                @blk.vector
                def _(v):
                    VE.eng = v
                    for tt in range(NT):
                        qq = tt % 2
                        x2t = X2SC[:, qq * D:(qq + 1) * D]
                        VE.wait(SC, sc0 + 6 * tt + 1)
                        VE.op(v.tensor_scalar, SXG[:, qq * D:(qq + 1) * D], x2t,
                              1.0, 0.0, ALU.mult, ALU.add, accum_out=STAT[:, 20:21])
                        VE.op(v.tensor_scalar, STAT[:, 21:22], STAT[:, 20:21], 1.0 / D, None, ALU.mult)
                        VE.op(v.tensor_tensor, STAT[:, 22:23], STAT[:, 21:22], STAT[:, 21:22], ALU.mult)
                        VE.op(v.scalar_tensor_tensor, STAT[:, 23:24], STAT[:, 24:25], 1.0 / D,
                              STAT[:, 22:23], ALU.mult, ALU.subtract)
                        VE.wait(SC, sc0 + 6 * tt + 3)
                        VE.op(v.scalar_tensor_tensor, STAT[:, 27:28], STAT[:, 21:22], -1.0,
                              STAT[:, 26:27], ALU.mult, ALU.mult)
                        VE.wait(PE, pe0 + 16 * tt + 16)
                        VE.op(v.tensor_tensor, TACC[:, :], TACC[:, :], P[4 + qq][0:128, 0:16], ALU.add)

                @blk.tensor
                def _(te):
                    PE.eng = te
                    for tt in range(NT):
                        qq = tt % 2
                        PE.wait(SC, sc0 + 6 * tt + 4)
                        if tt >= 2:
                            PE.wait(SC, sc0 + 6 * (tt - 1))
                            PE.wait(VE, ve0 + 6 * (tt - 1))
                        for j in range(8):
                            bank = P[2 * qq + j // 4]
                            PE.op(te.matmul, bank[:, (j % 4) * 128:(j % 4 + 1) * 128],
                                  SXN[:, qq * D + j * 128:qq * D + (j + 1) * 128],
                                  IDCS[:, :], start=True, stop=True)
                        PE.wait(SC, sc0 + 6 * (tt + 1))
                        for c in range(8):
                            PE.op(te.matmul, P[4 + qq][0:128, 0:16],
                                  xbT3[:, c, tt * 128:(tt + 1) * 128],
                                  WRS[:, c * 16:(c + 1) * 16], start=(c == 0), stop=(c == 7))

            with nc.Block() as blk:
                sc0, ve0, pe0 = SC.n, VE.n, PE.n

                @blk.scalar
                def _(s):
                    SC.eng = s
                    SC.op(s.activation, TACCB[:, :], TACC[:, :], AF.Identity)
                    SC.wait(VE, ve0 + 7)
                    SC.op(s.activation, M16B[:, :], M16[:, :], AF.Identity)
                    SC.wait(VE, ve0 + 10)
                    SC.op(s.activation, S16B[:, :], S16F[:, :], AF.Identity)

                @blk.tensor
                def _(te):
                    PE.eng = te
                    PE.wait(SC, sc0 + 1)
                    PE.op(te.matmul, P[0][0:16, 0:1], TACCB[:, :], ONESCS[:, :],
                          start=True, stop=True)
                    PE.op(te.matmul, P[1][0:1, 0:16], ONESCS[:, :], TACCB[:, :],
                          start=True, stop=True)
                    PE.wait(VE, ve0 + 4)
                    PE.op(te.matmul, P[2][0:16, 0:16], ONESRF[0:1, 0:16], R16R[:, :],
                          start=True, stop=True)
                    PE.wait(SC, sc0 + 2)
                    PE.op(te.matmul, P[3][0:16, 0:1], U16S[:, :], M16B[:, :],
                          start=True, stop=True)

                @blk.vector
                def _(v):
                    VE.eng = v
                    VE.wait(PE, pe0 + 2)
                    VE.op(v.tensor_scalar, R16C[:, :], P[0][0:16, 0:1], 1.0 / N, None, ALU.mult)
                    VE.op(v.tensor_tensor, R16C[:, :], R16C[:, :], BRC[:, :], ALU.add)
                    VE.op(v.tensor_scalar, R16R[:, :], P[1][0:1, 0:16], 1.0 / N, None, ALU.mult)
                    VE.op(v.tensor_tensor, R16R[:, :], R16R[:, :], BRR[:, :], ALU.add)
                    VE.wait(PE, pe0 + 3)
                    VE.op(v.tensor_scalar, MB16[:, :], P[2][0:16, 0:16], R16C[:, 0:1], None,
                          ALU.is_gt)
                    VE.op(v.tensor_reduce, M16[:, :], MB16[:, :], AX.X, ALU.add)
                    VE.op(v.tensor_scalar, M16[:, :], M16[:, :], 4.0, None, ALU.is_lt)
                    VE.wait(PE, pe0 + 4)
                    VE.op(v.tensor_copy, RANK[:, :], P[3][0:16, 0:1])
                    VE.op(v.tensor_scalar, S16F[:, :], IOTA4S[:, :], RANK[:, 0:1], None,
                          ALU.is_equal)
                    VE.op(v.tensor_scalar, S16F[:, :], S16F[:, :], M16[:, 0:1], None, ALU.mult)

            if debug:
                with nc.Block() as blk:
                    @blk.sync
                    def _(s):
                        dma(LD, s, dbgs["dbg_r16"][:, :], R16C[:, :])
                        dma(LD, s, dbgs["dbg_S16"][:, :], S16F[:, :])
                        s.wait_ge(sems["ld"], LD.n)
            if upto >= 8:
                SBLK = sbt("SBLK", [128, 2048], BF16, esB)
                WSELQ = sbt("WSELQ", [128, 2048], BF16, esB)
                WSELK = sbt("WSELK", [128, 2048], BF16, esB)
                WSELV = sbt("WSELV", [128, 2048], BF16, esB)
                BQSEL = sbt("BQSEL", [128, 4], F32, esB)
                BVSEL = sbt("BVSEL", [128, 256], F32, esB)
                esC = ExitStack()
                S16TB = sbt("S16TB", [4, 16], BF16, esC)
                H16B = sbt("H16B", [16, 256], BF16, esC)
                E4S = sbt("E4S", [4, 256], BF16, esC)
                GHS = sbt("GHS", [16, 1024], BF16, esC)
                IKRONS = sbt("IKRONS", [128, 256], BF16, esC)
                BQH = sbt("BQH", [128, 8], BF16, esC)
                BKH = sbt("BKH", [128, 8], BF16, esC)
                BVH = sbt("BVH", [128, 8], BF16, esC)
                BSROW = sbt("BSROW", [1, 512], F32, esC)
                BVRB = sbt("BVRB", [1, 256], BF16, esC)
                VB3 = VB[:, :].rearrange("p (tl a f) -> p tl a f", tl=32, a=4)

                with nc.Block() as blk:
                    @blk.sync
                    def _(s):
                        dma(LD, s, E4S[:, :], t["E4"][:, :])
                        dma(LD, s, GHS[:, :], t["GH"][:, :])
                        dma(LD, s, IKRONS[:, :], t["IKRON"][:, :])
                        s.wait_ge(sems["ld"], LD.n)

                    @blk.gpsimd
                    def _(g):
                        dma(GLD, g, BQH[:, :], t["bq_c"][:, :])
                        dma(GLD, g, BKH[:, :], t["bk_c"][:, :])
                        dma(GLD, g, BVH[:, :], t["bv_c"][:, :])
                        g.wait_ge(sems["gld"], GLD.n)

                    @blk.vector
                    def _(v):
                        VE.eng = v
                        VE.op(v.memset, VB[:, 64:8320:65], 1.0)

                with nc.Block() as blk:
                    sc0, ve0, pe0 = SC.n, VE.n, PE.n

                    @blk.tensor
                    def _(te):
                        PE.eng = te
                        PE.op(te.matmul, P[4][0:4, 0:16], S16B[:, :], IDCS[0:16, 0:16],
                              start=True, stop=True)
                        PE.wait(SC, sc0 + 1)
                        PE.op(te.matmul, P[5][0:16, 0:256], S16TB[:, :], E4S[:, :],
                              start=True, stop=True)
                        PE.wait(SC, sc0 + 2)
                        for tch in range(8):
                            if tch >= 2:
                                PE.wait(VE, ve0 + tch - 1)
                            PE.op(te.matmul, P[tch % 2][0:128, 0:256],
                                  GHS[:, tch * 128:(tch + 1) * 128], H16B[:, :],
                                  start=True, stop=True)
                        PE.wait(VE, ve0 + 8)
                        for c in range(8):
                            PE.op(te.matmul, P[6][0:1, 0:256], BQH[:, c:c + 1],
                                  SBLK[:, c * 256:(c + 1) * 256], start=(c == 0), stop=(c == 7))
                        for c in range(8):
                            PE.op(te.matmul, P[6][0:1, 256:512], BKH[:, c:c + 1],
                                  SBLK[:, c * 256:(c + 1) * 256], start=(c == 0), stop=(c == 7))
                        for c in range(8):
                            PE.op(te.matmul, P[7][0:1, 0:256], BVH[:, c:c + 1],
                                  SBLK[:, c * 256:(c + 1) * 256], start=(c == 0), stop=(c == 7))
                        PE.wait(VE, ve0 + 11)
                        PE.op(te.matmul, P[7][0:128, 256:512], ONESRS[0:1, 0:128], BVRB[:, :],
                              start=True, stop=True)

                    @blk.scalar
                    def _(s):
                        SC.eng = s
                        SC.wait(PE, pe0 + 1)
                        SC.op(s.activation, S16TB[:, :], P[4][0:4, 0:16], AF.Identity)
                        SC.wait(PE, pe0 + 2)
                        SC.op(s.activation, H16B[:, :], P[5][0:16, 0:256], AF.Identity)


                    @blk.vector
                    def _(v):
                        VE.eng = v
                        for tch in range(8):
                            VE.wait(PE, pe0 + 2 + tch + 1)
                            VE.op(v.tensor_tensor, SBLK[:, tch * 256:(tch + 1) * 256],
                                  P[tch % 2][0:128, 0:256], IKRONS[:, :], ALU.mult)
                        VE.wait(PE, pe0 + 34)
                        VE.op(v.tensor_copy, BSROW[0:1, 0:256], P[6][0:1, 0:256])
                        VE.op(v.tensor_copy, BSROW[0:1, 256:512], P[6][0:1, 256:512])
                        VE.op(v.tensor_copy, BVRB[:, :], P[7][0:1, 0:256])
                        VE.wait(PE, pe0 + 35)
                        VE.op(v.tensor_copy, BVSEL[:, :], P[7][0:128, 256:512])

                with nc.Block() as blk:
                    ld0 = LD.n

                    @blk.sync
                    def _(s):
                        dma(LD, s, t["rowscr"][0:1, 0:512], BSROW[:, :])
                        s.wait_ge(sems["ld"], ld0 + 16)
                        LD.n += 16
                        s.dma_start(BQSEL[:, :],
                                    t["rowscr"][0:1, 0:512].rearrange("one (c p) -> (one p) c", p=128),
                                    allow_slow_non_contiguous=True).then_inc(LD.sem, 16)
                        s.wait_ge(sems["ld"], LD.n)

                esC.close()
                esW = ExitStack()
                WTMP = esW.enter_context(nc.sbuf_tensor("WTMP", [128, 2048], BF16))
                WTT = esW.enter_context(nc.sbuf_tensor("WTT", [128, 8192], BF16))
                WTT3 = WTT[:, :].rearrange("p (e f) -> p e f", e=8)

                with nc.Block() as blk:
                    sc0, ve0, pe0, gld0 = SC.n, VE.n, PE.n, GLD.n
                    WSRC = [t["Wq"], t["Wk"], t["Wv"]]
                    WDST = [WSELQ, WSELK, WSELV]
                    SCW, PEW = 24, 128

                    @blk.gpsimd
                    def _(g):
                        for wi in range(3):
                            for c in range(8):
                                if 8 * wi + c >= 2:
                                    g.wait_ge(sems["pe"], pe0 + PEW * wi + 8 * (c - 1)
                                              if c >= 2 else pe0 + PEW * (wi - 1) + 8 * (c + 7))
                                dma(GLD, g, WTMP[:, (c % 2) * 1024:(c % 2 + 1) * 1024],
                                    WSRC[wi][128 * c:128 * (c + 1), :])

                    @blk.tensor
                    def _(te):
                        PE.eng = te
                        for wi in range(3):
                            if wi >= 1:
                                PE.wait(SC, sc0 + SCW * (wi - 1) + 24)
                            for c in range(8):
                                PE.wait(GLD, gld0 + 16 * (8 * wi + c + 1))
                                if c >= 2:
                                    PE.wait(SC, sc0 + SCW * wi + 2 * (c - 1))
                                for e in range(8):
                                    bank = P[2 * (c % 2) + e // 4]
                                    PE.op(te.matmul, bank[:, (e % 4) * 128:(e % 4 + 1) * 128],
                                          WTMP[:, (c % 2) * 1024 + e * 128:(c % 2) * 1024 + (e + 1) * 128],
                                          IDCS[:, :], start=True, stop=True)
                            PE.wait(SC, sc0 + SCW * wi + 16)
                            for m in range(8):
                                if m >= 2:
                                    PE.wait(SC, sc0 + SCW * wi + 16 + (m - 1))
                                for e in range(8):
                                    PE.op(te.matmul, P[4 + m % 2][0:128, 0:256],
                                          WTT[:, e * 1024 + m * 128:e * 1024 + (m + 1) * 128],
                                          SBLK[:, e * 256:(e + 1) * 256],
                                          start=(e == 0), stop=(e == 7))

                    @blk.scalar
                    def _(s):
                        SC.eng = s
                        for wi in range(3):
                            for c in range(8):
                                SC.wait(PE, pe0 + PEW * wi + 8 * (c + 1))
                                SC.op(s.activation, WTT3[:, 0:4, c * 128:(c + 1) * 128],
                                      P[2 * (c % 2)][:, :], AF.Identity)
                                SC.op(s.activation, WTT3[:, 4:8, c * 128:(c + 1) * 128],
                                      P[2 * (c % 2) + 1][:, :], AF.Identity)
                            for m in range(8):
                                SC.wait(PE, pe0 + PEW * wi + 64 + 8 * (m + 1))
                                SC.op(s.activation, WDST[wi][:, m * 256:(m + 1) * 256],
                                      P[4 + m % 2][0:128, 0:256], AF.Identity)

                esW.close()

                with nc.Block() as blk:
                    sc0, ve0, pe0 = SC.n, VE.n, PE.n
                    PEB, SCB, VEB = 64, 4, 4

                    @blk.tensor
                    def _(te):
                        PE.eng = te
                        for T in range(8):
                            if T >= 1:
                                PE.wait(SC, sc0 + SCB * (T - 1) + 4)
                                PE.wait(VE, ve0 + VEB * (T - 1) + 4)
                            for rc in range(2):
                                for c in range(8):
                                    PE.op(te.matmul, P[rc][:, :],
                                          WSELQ[:, c * 256 + rc * 128:c * 256 + (rc + 1) * 128],
                                          xbT[:, c * 4096 + T * 512:c * 4096 + (T + 1) * 512],
                                          start=(c == 0), stop=(c == 7))
                            for rc in range(2):
                                for c in range(8):
                                    PE.op(te.matmul, P[2 + rc][:, :],
                                          WSELK[:, c * 256 + rc * 128:c * 256 + (rc + 1) * 128],
                                          xbT[:, c * 4096 + T * 512:c * 4096 + (T + 1) * 512],
                                          start=(c == 0), stop=(c == 7))
                            for tl in range(4):
                                for c in range(8):
                                    PE.op(te.matmul,
                                          P[4 + tl // 2][:, (tl % 2) * 256:(tl % 2) * 256 + 256],
                                          xbT[:, c * 4096 + (T * 4 + tl) * 128:
                                              c * 4096 + (T * 4 + tl + 1) * 128],
                                          WSELV[:, c * 256:(c + 1) * 256],
                                          start=(c == 0), stop=(c == 7))

                    @blk.scalar
                    def _(s):
                        SC.eng = s
                        for T in range(8):
                            SC.wait(PE, pe0 + PEB * T + 16)
                            for rc in range(2):
                                SC.op(s.activation, QselT[:, rc * 4096 + T * 512:rc * 4096 + (T + 1) * 512],
                                      P[rc][:, :], AF.Identity, bias=BQSEL[:, rc:rc + 1])
                            SC.wait(PE, pe0 + PEB * T + 32)
                            for rc in range(2):
                                SC.op(s.activation, KselT[:, rc * 4096 + T * 512:rc * 4096 + (T + 1) * 512],
                                      P[2 + rc][:, :], AF.Identity, bias=BQSEL[:, 2 + rc:3 + rc])

                    @blk.vector
                    def _(v):
                        VE.eng = v
                        for T in range(8):
                            VE.wait(PE, pe0 + PEB * T + 64)
                            for tl in range(4):
                                ps = P[4 + tl // 2][:, (tl % 2) * 256:(tl % 2) * 256 + 256]
                                VE.op(v.tensor_tensor,
                                      VB3[:, T * 4 + tl, :, 0:64],
                                      ps.rearrange("p (a f) -> p a f", a=4),
                                      BVSEL[:, :].rearrange("p (a f) -> p a f", a=4), ALU.add)

                if debug:
                    with nc.Block() as blk:
                        @blk.gpsimd
                        def _(g):
                            dma(GLD, g, dbgs["dbg_qselT"][0:128, :], QselT[:, 0:4096])
                            dma(GLD, g, dbgs["dbg_qselT"][128:256, :], QselT[:, 4096:8192])
                            g.wait_ge(sems["gld"], GLD.n)

            esXB.close()
            P47es.close()
            if upto >= 9:
                PQB = esB.enter_context(nc.psum_tensor("pqb", [128, 2048], F32))
                KHC = sbt("KHC", [64, 4096], BF16, esB)
                QHC = sbt("QHC", [64, 4096], BF16, esB)
                PSLOT = sbt("PSLOT", [128, 4096], BF16, esB)
                NEGM = sbt("NEGM", [128, 4], F32, esB)
                MCT = sbt("MCT", [128, 1], F32, esB)
                MXC = sbt("MXC", [128, 1], BF16, esB)
                MM1 = sbt("MM1", [1, 8], F32, esB)
                RRF = sbt("RRF", [1, 512], F32, esB)
                RRB = sbt("RRB", [1, 512], BF16, esB)
                RBC = sbt("RBC", [64, 512], BF16, esB)
                WOS = sbt("WOS", [128, 2048], BF16, esB)
                YSC = sbt("YSC", [128, 1024], F32, esB)

                with nc.Block() as blk:
                    @blk.gpsimd
                    def _(g):
                        for rc in range(2):
                            dma(GLD, g, WOS[:, rc * 1024:(rc + 1) * 1024],
                                t["Wo"][128 * rc:128 * (rc + 1), :])
                        g.wait_ge(sems["gld"], GLD.n)

                with nc.Block() as blk:
                    sc0, ve0, pe0 = SC.n, VE.n, PE.n
                    SCH, VEH, PEH = 156, 19, 587
                    peA = lambda a: pe0 + PEH * a
                    qbase = lambda a, qb: pe0 + PEH * a + 67 + 65 * qb

                    @blk.vector
                    def _(v):
                        VE.eng = v
                        for a in range(4):
                            rc, poff = a // 2, 64 * (a % 2)
                            vb = ve0 + VEH * a
                            sb0 = sc0 + SCH * a
                            if a >= 1:
                                VE.wait(PE, peA(a))
                            VE.op(v.tensor_copy, QHC[:, :],
                                  QselT[poff:poff + 64, rc * 4096:(rc + 1) * 4096])
                            VE.op(v.tensor_copy, KHC[:, :],
                                  KselT[poff:poff + 64, rc * 4096:(rc + 1) * 4096])
                            VE.wait(PE, peA(a) + 32)
                            VE.op(v.tensor_reduce, MCT[:, :], P[0][0:128, 0:32], AX.X, ALU.max)
                            VE.op(v.tensor_copy, MXC[:, :], MCT[:, :])
                            VE.wait(PE, peA(a) + 33)
                            VE.op(v.tensor_reduce, MM1[0:1, 0:1], P[1][0:1, 0:128], AX.X,
                                  ALU.max)
                            VE.wait(PE, peA(a) + 65)
                            VE.op(v.tensor_reduce, MCT[:, :], P[0][0:128, 0:32], AX.X, ALU.max)
                            VE.op(v.tensor_copy, MXC[:, :], MCT[:, :])
                            VE.wait(PE, peA(a) + 66)
                            VE.op(v.tensor_reduce, MM1[0:1, 1:2], P[1][0:1, 0:128], AX.X,
                                  ALU.max)
                            VE.op(v.tensor_tensor, MM1[0:1, 2:3], MM1[0:1, 0:1], MM1[0:1, 1:2],
                                  ALU.mult)
                            VE.wait(SC, sb0 + 4)
                            VE.op(v.tensor_scalar, MM1[0:1, 5:6], MM1[0:1, 4:5], -1.0, None,
                                  ALU.mult)
                            VE.wait(PE, peA(a) + 67)
                            VE.op(v.tensor_copy, NEGM[:, a:a + 1], P[2][0:128, 0:1])
                            for qb in range(8):
                                VE.wait(SC, sb0 + 4 + 19 * qb + 19)
                                VE.op(v.tensor_tensor,
                                      OT[poff:poff + 64,
                                         rc * 4096 + qb * 512:rc * 4096 + (qb + 1) * 512],
                                      P[3][0:64, 0:512], RBC[:, :], ALU.mult)

                    @blk.scalar
                    def _(s):
                        SC.eng = s
                        for a in range(4):
                            vb = ve0 + VEH * a
                            sb0 = sc0 + SCH * a
                            SC.wait(VE, vb + 1)
                            SC.op(s.activation, TB[0:64, :], QHC[:, :], AF.Square)
                            SC.wait(VE, vb + 2)
                            SC.wait(PE, peA(a) + 32)
                            SC.op(s.activation, TB[0:64, :], KHC[:, :], AF.Square)
                            SC.wait(VE, vb + 9)
                            SC.op(s.activation, MM1[0:1, 3:4], MM1[0:1, 2:3], AF.Ln)
                            SC.op(s.activation, MM1[0:1, 4:5], MM1[0:1, 3:4], AF.Exp,
                                  scale=CPO[0:1, CP["half"]:CP["half"] + 1],
                                  bias=CPO[0:1, CP["negln8"]:CP["negln8"] + 1])
                            SC.wait(VE, vb + 11)
                            for qb in range(8):
                                base = qbase(a, qb)
                                for p in range(16):
                                    SC.wait(PE, base + (2 if p == 0 else 4 if p == 1 else 4 * p))
                                    SC.op(s.activation,
                                          PSLOT[:, (p % 4) * 1024:(p % 4) * 1024 + 1024],
                                          PQB.ap()[:, (p % 2) * 1024:(p % 2) * 1024 + 1024],
                                          AF.Exp, bias=NEGM[:, a:a + 1],
                                          scale=cp("qkscale"))
                                SC.wait(PE, base + 64)
                                SC.op(s.activation, RRF[:, :], P[3][64:65, 0:512], AF.Ln)
                                SC.op(s.activation, RRB[:, :], RRF[:, :], AF.Exp,
                                      scale=CPO[0:1, CP["neg1"]:CP["neg1"] + 1])
                                SC.wait(PE, base + 65)
                                SC.op(s.activation, RBC[:, :], P[2][0:64, 0:512], AF.Identity)

                    @blk.tensor
                    def _(te):
                        PE.eng = te
                        for a in range(4):
                            vb = ve0 + VEH * a
                            sb0 = sc0 + SCH * a
                            PE.wait(SC, sb0 + 1)
                            for tl in range(32):
                                PE.op(te.matmul, P[0][0:128, tl:tl + 1],
                                      TB[0:64, tl * 128:(tl + 1) * 128], ONESCS[0:64, 0:1],
                                      start=True, stop=True)
                            PE.wait(VE, vb + 4)
                            PE.op(te.matmul, P[1][0:1, 0:128], MXC[:, :], IDCS[:, :],
                                  start=True, stop=True)
                            PE.wait(SC, sb0 + 2)
                            for tl in range(32):
                                PE.op(te.matmul, P[0][0:128, tl:tl + 1],
                                      TB[0:64, tl * 128:(tl + 1) * 128], ONESCS[0:64, 0:1],
                                      start=True, stop=True)
                            PE.wait(VE, vb + 7)
                            PE.op(te.matmul, P[1][0:1, 0:128], MXC[:, :], IDCS[:, :],
                                  start=True, stop=True)
                            PE.wait(VE, vb + 10)
                            PE.op(te.matmul, P[2][0:128, 0:1], ONESRF[0:1, :], MM1[0:1, 5:6],
                                  start=True, stop=True)
                            for qb in range(8):
                                base = qbase(a, qb)
                                sqb = sb0 + 4 + 19 * qb

                                def smm(kt):
                                    PE.op(te.matmul,
                                          PQB.ap()[:, (kt % 4) * 512:(kt % 4 + 1) * 512],
                                          KHC[0:64, kt * 128:(kt + 1) * 128],
                                          QHC[0:64, qb * 512:(qb + 1) * 512],
                                          start=True, stop=True)

                                def omm(kt):
                                    sl = ((kt // 2) % 4) * 1024 + (kt % 2) * 512
                                    PE.op(te.matmul, P[3][0:65, 0:512],
                                          VB[:, kt * 260 + 65 * a:kt * 260 + 65 * (a + 1)],
                                          PSLOT[:, sl:sl + 512],
                                          start=(kt == 0), stop=(kt == 31))

                                smm(0); smm(1); smm(2); smm(3)
                                for p in range(14):
                                    PE.wait(SC, sqb + p + 1)
                                    if p == 0 and qb >= 1:
                                        PE.wait(VE, vb + 11 + qb)
                                    omm(2 * p); omm(2 * p + 1)
                                    smm(2 * p + 4); smm(2 * p + 5)
                                PE.wait(SC, sqb + 15)
                                omm(28); omm(29)
                                PE.wait(SC, sqb + 16)
                                omm(30); omm(31)
                                PE.wait(SC, sqb + 18)
                                PE.op(te.matmul, P[2][0:64, 0:512], ONESRS[0:1, 0:64], RRB[:, :],
                                      start=True, stop=True)

            if upto >= 9:
                emit_row_broadcast("bo", BCAST)
                YG = sbt("YG", [128, 8192], BF16, esB)
                YSC2 = sbt("YSC2", [128, 2048], F32, esB)
                STB6 = sbt("STB6", [128, 288], F32, esB)
                B6SUM0 = STB6[:, 0:32]
                B6SUM1 = STB6[:, 32:64]
                B6SSA = STB6[:, 64:96]
                B6MEAN = STB6[:, 96:128]
                B6T = STB6[:, 128:160]
                B6RSTD = STB6[:, 160:192]
                B6NB = STB6[:, 192:224]
                B6T2 = STB6[:, 224:256]
                B6VAR = STB6[:, 256:288]
                LNY = [SSQ[:, 0:1024], SXG[:, 0:2048].bitcast(F32)]
                with nc.Block() as blk:
                    sc0, ve0, pe0, ld0 = SC.n, VE.n, PE.n, LD.n
                    NG = NT // 8

                    @blk.sync
                    def _(s):
                        for g in range(NG):
                            vg, lg = ve0 + 45 * g, ld0 + 256 * g
                            for i in range(8):
                                tt = 8 * g + i
                                if i >= 2:
                                    s.wait_ge(sems["ve"], vg + 36 + i)
                                elif g >= 1:
                                    s.wait_ge(sems["ve"], vg + i)
                                dma(LD, s, X2SC[:, (i % 2) * D:(i % 2 + 1) * D],
                                    t["x2s"][128 * tt:128 * (tt + 1), :])
                                s.wait_ge(sems["ve"], vg + 38 + i)
                                dma(LD, s, out[128 * tt:128 * (tt + 1), :],
                                    YSC2[:, (i % 2) * D:(i % 2 + 1) * D])
                        s.wait_ge(sems["ld"], LD.n)

                    @blk.tensor
                    def _(te):
                        PE.eng = te
                        for g in range(NG):
                            vg = ve0 + 45 * g
                            for i in range(8):
                                tt = 8 * g + i
                                if i >= 2:
                                    PE.wait(VE, vg + 4 * (i - 2) + 3)
                                elif g >= 1:
                                    PE.wait(VE, vg - 45 + 27 + 4 * i)
                                for j in range(2):
                                    for rc in range(2):
                                        PE.op(te.matmul, P[2 * (i % 2) + j][:, :],
                                              OT[:, rc * 4096 + tt * 128:rc * 4096 + (tt + 1) * 128],
                                              WOS[:, rc * 1024 + j * 512:rc * 1024 + (j + 1) * 512],
                                              start=(rc == 0), stop=(rc == 1))

                    @blk.scalar
                    def _(s):
                        SC.eng = s
                        for g in range(NG):
                            sg, vg = sc0 + 18 * g, ve0 + 45 * g
                            for i in range(8):
                                SC.wait(VE, vg + 4 * i + 4)
                                SC.op(s.activation, YSC[:, :], YG[:, i * 1024:(i + 1) * 1024],
                                      AF.Square, accum_out=B6SSA[:, i:i + 1])
                            SC.wait(VE, vg + 36)
                            SC.op(s.activation, B6T2[:, :], B6VAR[:, :], AF.Ln, bias=cp("eps"))
                            SC.op(s.activation, B6RSTD[:, :], B6T2[:, :], AF.Exp, scale=cp("neg05"))
                            for i in range(8):
                                SC.wait(VE, vg + 37 if i <= 1 else vg + 36 + i)
                                SC.op(s.activation, LNY[i % 2], YG[:, i * 1024:(i + 1) * 1024],
                                      AF.Identity, scale=B6RSTD[:, i:i + 1], bias=B6NB[:, i:i + 1])

                    @blk.vector
                    def _(v):
                        VE.eng = v
                        for g in range(NG):
                            sg, vg, pg, lg = sc0 + 18 * g, ve0 + 45 * g, pe0 + 32 * g, ld0 + 256 * g
                            for i in range(8):
                                VE.wait(PE, pg + 4 * i + 4)
                                if g >= 1:
                                    VE.wait(SC, sg - 18 + 11 + i)
                                for j in range(2):
                                    VE.op(v.tensor_tensor,
                                          YG[:, i * 1024 + j * 512:i * 1024 + (j + 1) * 512],
                                          P[2 * (i % 2) + j][:, :],
                                          BCAST[:, j * 512:(j + 1) * 512], ALU.add)
                                    VE.op(v.tensor_scalar, SXN[:, 0:512],
                                          YG[:, i * 1024 + j * 512:i * 1024 + (j + 1) * 512],
                                          1.0, 0.0, ALU.mult, ALU.add,
                                          accum_out=(B6SUM0 if j == 0 else B6SUM1)[:, i:i + 1])
                            VE.op(v.tensor_tensor, B6SUM0[:, :], B6SUM0[:, :], B6SUM1[:, :], ALU.add)
                            VE.op(v.tensor_scalar, B6MEAN[:, :], B6SUM0[:, :], 1.0 / D, None, ALU.mult)
                            VE.op(v.tensor_tensor, B6T[:, :], B6MEAN[:, :], B6MEAN[:, :], ALU.mult)
                            VE.wait(SC, sg + 8)
                            VE.op(v.scalar_tensor_tensor, B6VAR[:, :], B6SSA[:, :], 1.0 / D,
                                  B6T[:, :], ALU.mult, ALU.subtract)
                            VE.wait(SC, sg + 10)
                            VE.op(v.scalar_tensor_tensor, B6NB[:, :], B6MEAN[:, :], -1.0,
                                  B6RSTD[:, :], ALU.mult, ALU.mult)
                            for i in range(8):
                                VE.wait(SC, sg + 11 + i)
                                VE.wait(LD, lg + 32 * i + 16)
                                if i >= 2:
                                    VE.wait(LD, lg + 32 * (i - 2) + 32)
                                VE.op(v.tensor_tensor, YSC2[:, (i % 2) * D:(i % 2 + 1) * D],
                                      X2SC[:, (i % 2) * D:(i % 2 + 1) * D],
                                      LNY[i % 2], ALU.add)

            if debug and upto >= 9:
                with nc.Block() as blk:
                    @blk.gpsimd
                    def _(g):
                        dma(GLD, g, dbgs["dbg_oT"][0:128, :], OT[:, 0:4096])
                        dma(GLD, g, dbgs["dbg_oT"][128:256, :], OT[:, 4096:8192])
                        g.wait_ge(sems["gld"], GLD.n)

            esB.close()

        if upto < 9:
            with nc.Block() as blk:
                @blk.sync
                def _(s):
                    if upto >= 7:
                        dma(LD, s, out[:, :], t["x2s"][:, :])
                    else:
                        for k in range(8):
                            dst = out[512 * k:512 * (k + 1), :].rearrange("(j p) f -> p j f", p=128)
                            src_ = XA[:, 4 * k * D:(4 * k + 4) * D].rearrange("p (j f) -> p j f", j=4)
                            dma(LD, s, dst, src_)
                    s.wait_ge(sems["ld"], LD.n)
        if upto < 7:
            XA_scope.close()

    for nm, e in [("PE", _ENGS[0]), ("VE", _ENGS[1]), ("SC", _ENGS[2]), ("GP", _ENGS[3]),
                  ("LD", _ENGS[4]), ("GLD", _ENGS[5])]:
        if e.maxw > e.n:
            raise RuntimeError(f"DEADLOCK: {nm} waited {e.maxw} > final {e.n}")
    return nc, t, sems


def host_consts():
    c = {}
    c["IDC"] = np.eye(128).astype(bfd)
    c["MASKETA"] = (np.triu(np.ones((128, 128), np.float32)) * ETA)
    c["MASKETA4"] = np.tile(c["MASKETA"], (1, 4))
    c["U16"] = np.triu(np.ones((16, 16), np.float32), 1).astype(bfd)
    c["IOTA4"] = np.broadcast_to(np.arange(4, dtype=np.float32), (16, 4)).copy()
    e4 = np.zeros((4, 256), np.float32)
    for a in range(4):
        e4[a, 64 * a:64 * (a + 1)] = 1.0
    c["E4"] = e4.astype(bfd)
    gh = np.zeros((16, 1024), np.float32)
    for m_ in range(1024):
        gh[2 * (m_ // 128) + (m_ % 128) // 64, m_] = 1.0
    c["GH"] = gh.astype(bfd)
    ik = np.zeros((128, 256), np.float32)
    for r in range(128):
        for a in range(4):
            ik[r, 64 * a + (r % 64)] = 1.0
    c["IKRON"] = ik.astype(bfd)
    c["ONESC"] = np.ones((128, 1), np.float32).astype(bfd)
    c["ONESCF"] = np.ones((128, 1), np.float32)
    c["ONESRF"] = np.ones((1, 128), np.float32)
    c["ONESR"] = np.ones((1, 512), np.float32).astype(bfd)
    c["CPOOL"] = np.broadcast_to(np.array(CP_VALS, np.float32), (128, len(CP_VALS))).copy()
    return c


def host_weights(i):
    w = {}
    for nm, key in [("ln1g", "L0_attn_ln1_g"), ("ln1b", "L0_attn_ln1_b"),
                    ("ln2g", "L0_attn_ln2_g"), ("ln2b", "L0_attn_ln2_b"),
                    ("ln3g", "L0_attn_ln3_g"), ("ln3b", "L0_attn_ln3_b")]:
        w[nm] = np.ascontiguousarray(np.asarray(i[key], np.float32).reshape(1, D))
    wp = np.asarray(i["L0_attn_w_pot"], np.float32)
    w["w_potV"] = np.ascontiguousarray(wp[:, 0].reshape(8, 128).T)
    w["ln1g_c"] = np.ascontiguousarray(np.asarray(i["L0_attn_ln1_g"], np.float32).reshape(8, 128).T)
    w["ln1b_c"] = np.ascontiguousarray(np.asarray(i["L0_attn_ln1_b"], np.float32).reshape(8, 128).T)
    w["w_potP"] = np.ascontiguousarray(wp[:, 1].reshape(8, 128).T)
    w["b_pot"] = np.asarray(i["L0_attn_b_pot"], np.float32).reshape(1, 2)
    w["w_bk"] = np.asarray(i["L0_attn_w_bk"], np.float32)
    w["b_bk"] = np.asarray(i["L0_attn_b_bk"], np.float32).reshape(1, D)
    w["w_qkv"] = np.asarray(i["L0_attn_w_qkv"], np.float32)
    bq = np.asarray(i["L0_attn_b_qkv"], np.float32)
    w["b_qkv"] = np.ascontiguousarray(bq.reshape(12, 128).T)
    w["b_qkv_v"] = np.ascontiguousarray(bq[1024:1536].reshape(1, 512))
    w["b_qkv_k"] = np.ascontiguousarray(bq[512:1024].reshape(1, 512))
    w["w_out"] = np.asarray(i["L0_attn_w_out"], np.float32)
    w["b_out"] = np.asarray(i["L0_attn_b_out"], np.float32).reshape(1, D)
    w["w_ff1"] = np.asarray(i["L0_attn_w_ff1"], np.float32)
    w["b_ff1"] = np.ascontiguousarray(np.asarray(i["L0_attn_b_ff1"], np.float32).reshape(32, 128).T)
    w["w_ff2"] = np.asarray(i["L0_attn_w_ff2"], np.float32)
    w["b_ff2"] = np.asarray(i["L0_attn_b_ff2"], np.float32).reshape(1, D)
    w["Wq"] = np.asarray(i["L0_ffn_Wq"], np.float32)
    w["Wk"] = np.asarray(i["L0_ffn_Wk"], np.float32)
    w["Wv"] = np.asarray(i["L0_ffn_Wv"], np.float32)
    w["bq_c"] = np.ascontiguousarray(np.asarray(i["L0_ffn_bq"], np.float32).reshape(8, 128).T)
    w["bk_c"] = np.ascontiguousarray(np.asarray(i["L0_ffn_bk"], np.float32).reshape(8, 128).T)
    w["bv_r"] = np.asarray(i["L0_ffn_bv"], np.float32).reshape(1, D)
    w["Wr"] = np.asarray(i["L0_ffn_Wr"], np.float32)
    w["br_c"] = np.asarray(i["L0_ffn_br"], np.float32).reshape(16, 1)
    w["br_r"] = np.asarray(i["L0_ffn_br"], np.float32).reshape(1, 16)
    w["Wr_c"] = np.ascontiguousarray(
        np.asarray(i["L0_ffn_Wr"], np.float32).reshape(8, 128, 16).transpose(1, 0, 2).reshape(128, 128))
    w["bv_c"] = np.ascontiguousarray(np.asarray(i["L0_ffn_bv"], np.float32).reshape(8, 128).T)
    w["Wo"] = np.asarray(i["L0_ffn_Wo"], np.float32)
    w["bo"] = np.asarray(i["L0_ffn_bo"], np.float32).reshape(1, D)
    return w


def kernel(**inputs):
    nc, t, sems = build_program(debug=False)
    consts = host_consts()
    w = host_weights(inputs)
    x = np.ascontiguousarray(np.asarray(inputs["L0_attn_x"], np.float32))
    in_maps = []
    for core in range(8):
        m = {"xin": np.ascontiguousarray(x[core])}
        m.update(consts)
        m.update(w)
        in_maps.append(m)
    res = run_bass_kernel_spmd(nc, in_maps, core_ids=list(range(8)))
    return np.stack([np.asarray(r["out"], np.float32) for r in res.results])


# version 12 (final)
# speedup vs baseline: 1.1796x; 1.1796x over previous
import math
import numpy as np
import ml_dtypes
from contextlib import ExitStack

import concourse.bass as bass
import concourse.mybir as mybir
from concourse.bass_utils import run_bass_kernel_spmd

F32 = mybir.dt.float32
BF16 = mybir.dt.bfloat16
AF = mybir.ActivationFunctionType
ALU = mybir.AluOpType
AX = mybir.AxisListType
bfd = ml_dtypes.bfloat16

N, D, NT, NCH = 4096, 1024, 32, 8
H, HD, FF = 8, 64, 4096
ETA, DT_, GAMMA_FLOOR, EPS = 0.1, 0.01, 0.1, 1e-5
NHEAD, NACT, DA = 16, 4, 64

CP = {"eps": 0, "neg05": 1, "neg1": 2, "negdt": 3, "half": 4, "negln8": 5,
      "two": 6, "eta": 7, "tenth": 8, "qkscale": 9}
CP_VALS = [EPS, -0.5, -1.0, -DT_, 0.5, -math.log(8.0), 2.0, ETA, 0.1, 0.125]


class Eng:
    def __init__(self, sem, serialize=True):
        self.sem = sem
        self.n = 0
        self.maxw = 0
        self.serialize = serialize
        self.eng = None

    def op(self, f, *a, **kw):
        if self.serialize and self.n > 0:
            self.eng.wait_ge(self.sem, self.n)
        instr = f(*a, **kw)
        self.n += 1
        instr.then_inc(self.sem, 1)
        return self.n

    def wait(self, other, v):
        if v > 0:
            other.maxw = max(other.maxw, v)
            self.eng.wait_ge(other.sem, v)


def build_program(debug=False, upto=99):
    nc = bass.Bass("TRN2", target_bir_lowering=False)
    t = {}

    def din(name, shape, dt=F32):
        t[name] = nc.dram_tensor(name, list(shape), dt, kind="ExternalInput").ap()
        return t[name]

    def dint(name, shape, dt=F32):
        t[name] = nc.dram_tensor(name, list(shape), dt, kind="Internal").ap()
        return t[name]

    def dout(name, shape, dt=F32):
        t[name] = nc.dram_tensor(name, list(shape), dt, kind="ExternalOutput").ap()
        return t[name]

    xin = din("xin", [N, D])
    out = dout("out", [N, D])
    din("w_potV", [128, 8]); din("w_potP", [128, 8]); din("b_pot", [1, 2])
    din("ln1g_c", [128, 8]); din("ln1b_c", [128, 8])
    din("w_bk", [2, D]); din("b_bk", [1, D])
    din("w_qkv", [D, 3 * H * HD])
    din("b_qkv", [128, 12]); din("b_qkv_v", [1, 512]); din("b_qkv_k", [1, 512])
    din("w_out", [512, D]); din("b_out", [1, D])
    din("w_ff1", [D, FF]); din("b_ff1", [128, 32])
    din("w_ff2", [FF, D]); din("b_ff2", [1, D])
    for gb in ["ln1g", "ln1b", "ln2g", "ln2b", "ln3g", "ln3b"]:
        din(gb, [1, D])
    din("Wq", [D, D]); din("Wk", [D, D]); din("Wv", [D, D])
    din("bq_c", [128, 8]); din("bk_c", [128, 8]); din("bv_r", [1, D])
    din("Wr", [D, NHEAD]); din("br_c", [16, 1]); din("br_r", [1, 16])
    din("Wr_c", [128, 128]); din("bv_c", [128, 8])
    din("Wo", [NACT * DA, D]); din("bo", [1, D])
    din("IDC", [128, 128], BF16)
    din("MASKETA", [128, 128])
    din("MASKETA4", [128, 512])
    din("U16", [16, 16], BF16)
    din("IOTA4", [16, 4])
    din("E4", [4, 256], BF16)
    din("GH", [16, 1024], BF16)
    din("IKRON", [128, 256], BF16)
    din("ONESC", [128, 1], BF16)
    din("ONESCF", [128, 1])
    din("ONESRF", [1, 128])
    din("ONESR", [1, 512], BF16)
    din("CPOOL", [128, len(CP_VALS)])
    dint("x2s", [N, D])
    dint("rowscr", [1, N])
    dint("rowscr2", [1, N])
    dint("rowb1", [1, N], BF16)
    dint("rowb2", [1, N], BF16)
    dint("wff1bf", [128, 8 * FF], BF16)
    dint("wff2bf", [128, 32 * D], BF16)
    dbgs = {}
    if debug:
        for nm, sh in [("dbg_xa", [N, D]), ("dbg_potV", [1, N]), ("dbg_potP", [1, N]),
                       ("dbg_gam", [128, 32]), ("dbg_G", [2, N]), ("dbg_xa4", [N, D]),
                       ("dbg_xa5", [N, D]), ("dbg_xa6", [N, D]), ("dbg_x2", [N, D]),
                       ("dbg_r16", [16, 1]), ("dbg_S16", [16, 4]),
                       ("dbg_are", [128, 32]), ("dbg_aim", [128, 32]),
                       ("dbg_bre", [128, 32]), ("dbg_bim", [128, 32]),
                       ("dbg_pt", [32, 128]), ("dbg_qselT", [256, N]), ("dbg_oT", [256, N]),
                       ("dbg_stb", [128, 256])]:
            dbgs[nm] = dout(nm, list(sh))

    es = ExitStack()
    with es:
        def sbt(name, shape, dt, scope=es):
            h = scope.enter_context(nc.sbuf_tensor(name, shape, dt))
            t[name] = h
            return h

        XA_scope = ExitStack()
        XA = XA_scope.enter_context(nc.sbuf_tensor("XA", [128, NT * D], F32, side="right"))
        TB = sbt("TB", [128, 4096], BF16)
        SXN = sbt("SXN", [128, 2 * D], BF16)
        SXG = sbt("SXG", [128, 2 * D], BF16)
        SSQ = sbt("SSQ", [128, D], F32)
        STAT = sbt("STAT", [128, 40], F32)
        RSTD2 = sbt("RSTD2", [128, 32], F32); NB2 = sbt("NB2", [128, 32], F32)
        RSTD3 = sbt("RSTD3", [128, 32], F32); NB3 = sbt("NB3", [128, 32], F32)
        CPO = sbt("CPO", [128, len(CP_VALS)], F32)
        IDCS = sbt("IDCS", [128, 128], BF16)
        ONESCS = sbt("ONESCS", [128, 1], BF16)
        ONESCF = sbt("ONESCF_s", [128, 1], F32)
        ONESRF = sbt("ONESRF_s", [1, 128], F32)
        ONESRS = sbt("ONESRS", [1, 512], BF16)
        BCAST = sbt("BCAST", [128, D], F32)
        GGB = sbt("GGB", [128, 2 * D], BF16)
        BQC = sbt("BQC", [128, 12], F32)
        ROWS = SXN[0:1, 0:2048].bitcast(F32)

        sems = {}
        for s_ in ["pe", "ve", "sc", "gp", "ld", "gld", "sld"]:
            sems[s_] = es.enter_context(nc.semaphore("sem_" + s_))
        XSEM = [es.enter_context(nc.semaphore(f"sem_x{k}")) for k in range(8)]
        PE = Eng(sems["pe"], serialize=False)
        VE = Eng(sems["ve"]); SC = Eng(sems["sc"]); GP = Eng(sems["gp"])
        LD = Eng(sems["ld"], serialize=False)
        GLD = Eng(sems["gld"], serialize=False)
        SLD = Eng(sems["sld"], serialize=False)
        global _ENGS
        _ENGS = [PE, VE, SC, GP, LD, GLD]

        def dma(ctr, eng, dst, src, **kw):
            ctr.n += 16
            eng.dma_start(dst, src, **kw).then_inc(ctr.sem, 16)
            return ctr.n

        P47es = ExitStack()
        P = [es.enter_context(nc.psum_tensor(f"pb{i}", [128, 512], F32)) for i in range(4)] + \
            [P47es.enter_context(nc.psum_tensor(f"pb{i}", [128, 512], F32)) for i in range(4, 8)]
        es.callback(P47es.close)

        cp = lambda name: CPO[:, CP[name]:CP[name] + 1]

        def emit_row_broadcast(row_name, dst, dtname="f32"):
            with nc.Block() as blk:
                @blk.sync
                def _(s):
                    dma(LD, s, ROWS, t[row_name][:, :])
                    s.wait_ge(sems["ld"], LD.n)
            with nc.Block() as blk:
                @blk.tensor
                def _(te):
                    PE.eng = te
                    PE.op(te.matmul, P[6][:, 0:512], ONESRF[0:1, :],
                          SXN[0:1, 0:1024].bitcast(F32), start=True, stop=True)
                    PE.op(te.matmul, P[7][:, 0:512], ONESRF[0:1, :],
                          SXN[0:1, 1024:2048].bitcast(F32), start=True, stop=True)

                @blk.vector
                def _(v):
                    VE.eng = v
                    VE.wait(PE, PE.n)
                    VE.op(v.tensor_copy, dst[:, 0:512], P[6][:, 0:512])
                    VE.op(v.tensor_copy, dst[:, 512:1024], P[7][:, 0:512])

        def emit_gb_broadcast(gname, bname):
            with nc.Block() as blk:
                @blk.sync
                def _(s):
                    dma(LD, s, ROWS, t[gname][:, :])
                    s.wait_ge(sems["ld"], LD.n)
            with nc.Block() as blk:
                @blk.tensor
                def _(te):
                    PE.eng = te
                    PE.op(te.matmul, P[6][:, 0:512], ONESRF[0:1, :],
                          SXN[0:1, 0:1024].bitcast(F32), start=True, stop=True)
                    PE.op(te.matmul, P[7][:, 0:512], ONESRF[0:1, :],
                          SXN[0:1, 1024:2048].bitcast(F32), start=True, stop=True)

                @blk.vector
                def _(v):
                    VE.eng = v
                    VE.wait(PE, PE.n)
                    VE.op(v.tensor_copy, GGB[:, 0:512], P[6][:, 0:512])
                    VE.op(v.tensor_copy, GGB[:, 512:1024], P[7][:, 0:512])
            with nc.Block() as blk:
                @blk.sync
                def _(s):
                    dma(LD, s, ROWS, t[bname][:, :])
                    s.wait_ge(sems["ld"], LD.n)
            with nc.Block() as blk:
                @blk.tensor
                def _(te):
                    PE.eng = te
                    PE.op(te.matmul, P[6][:, 0:512], ONESRF[0:1, :],
                          SXN[0:1, 0:1024].bitcast(F32), start=True, stop=True)
                    PE.op(te.matmul, P[7][:, 0:512], ONESRF[0:1, :],
                          SXN[0:1, 1024:2048].bitcast(F32), start=True, stop=True)

                @blk.vector
                def _(v):
                    VE.eng = v
                    VE.wait(PE, PE.n)
                    VE.op(v.tensor_copy, GGB[:, D:D + 512], P[6][:, 0:512])
                    VE.op(v.tensor_copy, GGB[:, D + 512:2 * D], P[7][:, 0:512])

        with nc.Block() as blk:
            @blk.sync
            def _(s):
                for nm, dst in [("CPOOL", CPO), ("IDC", IDCS),
                                ("ONESC", ONESCS), ("ONESCF", ONESCF), ("ONESRF", ONESRF),
                                ("ONESR", ONESRS), ("b_qkv", BQC)]:
                    dma(LD, s, dst[:, :], t[nm][:, :])
                for k in range(8):
                    src = xin[512 * k:512 * (k + 1), :].rearrange("(j p) f -> p j f", p=128)
                    dst = XA[:, 4 * k * D:(4 * k + 4) * D].rearrange("p (j f) -> p j f", j=4)
                    s.dma_start(dst, src).then_inc(XSEM[k], 16)
                s.wait_ge(sems["ld"], 16 * 7)

        esAB = ExitStack()
        with esAB:
            GAM32 = sbt("GAM32", [128, 32], F32, esAB)
            STB = sbt("STB", [128, 256], F32, esAB)
            RPETA = sbt("RPETA", [128, 32], F32, esAB)
            PLB2 = sbt("PLB2", [128, 32], F32, esAB)
            PT32 = sbt("PT32", [32, 128], F32, esAB)

            esA = ExitStack()
            with esA:
                WPV = sbt("WPV", [128, 8], BF16, esA); WPP = sbt("WPP", [128, 8], BF16, esA)
                BPV = sbt("BPV", [1, 2], F32, esA)
                WBK = sbt("WBK", [2, D], BF16, esA)
                ONER128 = sbt("ONER128", [1, 128], BF16, esA)
                BBKR = sbt("BBKR", [1, D], BF16, esA)
                DRE = sbt("DRE", [128, 32], F32, esA); DIM = sbt("DIM", [128, 32], F32, esA)
                G1C = sbt("G1C", [128, 8], F32, esA)
                B1C = sbt("B1C", [128, 8], F32, esA)
                B1CB = sbt("B1CB", [128, 8], BF16, esA)
                WPVg = sbt("WPVg", [128, 8], BF16, esA)
                WPPg = sbt("WPPg", [128, 8], BF16, esA)
                BPV2 = sbt("BPV2", [1, 2], F32, esA)
                STA = sbt("STA", [128, 384], F32, esA)
                esPot = ExitStack()
                POTV = sbt("POTV", [1, N], BF16, esPot)
                POTP = sbt("POTP", [1, N], BF16, esPot)

                with nc.Block() as blk:
                    @blk.gpsimd
                    def _(g):
                        dma(GLD, g, WPV[:, :], t["w_potV"][:, :])
                        dma(GLD, g, WPP[:, :], t["w_potP"][:, :])
                        dma(GLD, g, WBK[:, :], t["w_bk"][:, :])
                        dma(GLD, g, BBKR[0:1, :], t["b_bk"][0:1, :])
                        g.wait_ge(sems["gld"], GLD.n)

                    @blk.sync
                    def _(s):
                        dma(LD, s, BPV[:, :], t["b_pot"][:, :])
                        dma(LD, s, G1C[:, :], t["ln1g_c"][:, :])
                        dma(LD, s, B1C[:, :], t["ln1b_c"][:, :])
                        s.wait_ge(sems["ld"], LD.n)

                with nc.Block() as blk:
                    ve0f, pe0f = VE.n, PE.n

                    @blk.vector
                    def _(v):
                        VE.eng = v
                        VE.op(v.memset, ONER128[:, :], 1.0)
                        VE.op(v.tensor_copy, B1CB[:, :], B1C[:, :])
                        VE.op(v.tensor_tensor, WPVg[:, :], WPV[:, :], G1C[:, :], ALU.mult)
                        VE.op(v.tensor_tensor, WPPg[:, :], WPP[:, :], G1C[:, :], ALU.mult)
                        VE.wait(PE, pe0f + 16)
                        VE.op(v.tensor_tensor, BPV2[0:1, 0:2], BPV[0:1, 0:2],
                              P[6][0:1, 0:2], ALU.add)

                    @blk.tensor
                    def _(te):
                        PE.eng = te
                        PE.wait(VE, ve0f + 2)
                        for c in range(8):
                            PE.op(te.matmul, P[6][0:1, 0:1], WPV[:, c:c + 1], B1CB[:, c:c + 1],
                                  start=(c == 0), stop=(c == 7))
                        for c in range(8):
                            PE.op(te.matmul, P[6][0:1, 1:2], WPP[:, c:c + 1], B1CB[:, c:c + 1],
                                  start=(c == 0), stop=(c == 7))

                A_SUM = STA[:, 0:32]; A_SSA = STA[:, 32:64]; A_MEAN = STA[:, 64:96]
                A_T0 = STA[:, 96:128]; A_VAR = STA[:, 128:160]; A_W = STA[:, 160:192]
                A_T2 = STA[:, 192:224]; A_RSTD0 = STA[:, 224:256]; A_NB0 = STA[:, 256:288]
                A_RSTD1 = STA[:, 288:320]; A_RS01 = STA[:, 320:352]; A_NB01 = STA[:, 352:384]
                XSLOT = [SXN[:, 0:1024], SXN[:, 1024:2048], SXG[:, 0:1024], SXG[:, 1024:2048]]

                with nc.Block() as blk:
                    sc0, ve0, pe0 = SC.n, VE.n, PE.n
                    a1i = lambda tt: 37 if tt == 0 else 3 * tt + 35
                    te1i = lambda tp: 3 * tp + 40 if tp <= 30 else 132

                    @blk.scalar
                    def _(s):
                        SC.eng = s
                        for tt in range(NT):
                            s.wait_ge(XSEM[tt // 4], 16)
                            SC.op(s.activation, SSQ[:, :], XA[:, tt * D:(tt + 1) * D],
                                  AF.Square, accum_out=A_SSA[:, tt:tt + 1])
                        SC.wait(VE, ve0 + 35)
                        SC.op(s.activation, A_T2[:, :], A_VAR[:, :], AF.Ln, bias=cp("eps"))
                        SC.op(s.activation, A_RSTD0[:, :], A_T2[:, :], AF.Exp, scale=cp("neg05"))
                        SC.wait(VE, ve0 + 36)
                        SC.op(s.activation, A_T0[:, :], A_W[:, :], AF.Ln)
                        SC.wait(VE, ve0 + 38)
                        SC.op(s.activation, A_RSTD1[:, :], A_VAR[:, :], AF.Exp, scale=cp("half"))
                        for tt in range(NT + 1):
                            if tt < NT:
                                if tt == 0:
                                    SC.wait(VE, ve0 + 40)
                                if tt >= 4:
                                    SC.wait(PE, pe0 + 24 * (tt - 4) + 8)
                                SC.op(s.activation, XSLOT[tt % 4], XA[:, tt * D:(tt + 1) * D],
                                      AF.Identity, scale=A_RS01[:, tt:tt + 1],
                                      bias=A_NB01[:, tt:tt + 1])
                            if tt >= 1:
                                tp = tt - 1
                                qq = tp % 2
                                SC.wait(PE, pe0 + 24 * tp + 8)
                                SC.op(s.activation, TB[:, qq * 2048:qq * 2048 + 512],
                                      P[2 * qq][:, :], AF.Identity)
                                SC.op(s.activation, TB[:, qq * 2048 + 512:qq * 2048 + 1024],
                                      P[2 * qq + 1][:, :], AF.Identity)

                    @blk.vector
                    def _(v):
                        VE.eng = v
                        for tt in range(NT):
                            v.wait_ge(XSEM[tt // 4], 16)
                            VE.op(v.tensor_scalar, TB[:, 0:1024], XA[:, tt * D:(tt + 1) * D],
                                  1.0, 0.0, ALU.mult, ALU.add, accum_out=A_SUM[:, tt:tt + 1])
                        VE.op(v.tensor_scalar, A_MEAN[:, :], A_SUM[:, :], 1.0 / D, None, ALU.mult)
                        VE.op(v.tensor_tensor, A_T0[:, :], A_MEAN[:, :], A_MEAN[:, :], ALU.mult)
                        VE.wait(SC, sc0 + 32)
                        VE.op(v.scalar_tensor_tensor, A_VAR[:, :], A_SSA[:, :], 1.0 / D,
                              A_T0[:, :], ALU.mult, ALU.subtract)
                        VE.op(v.tensor_scalar, A_W[:, :], A_VAR[:, :], 1.0 + EPS, EPS * EPS,
                              ALU.mult, ALU.add)
                        VE.wait(SC, sc0 + 34)
                        VE.op(v.scalar_tensor_tensor, A_NB0[:, :], A_MEAN[:, :], -1.0,
                              A_RSTD0[:, :], ALU.mult, ALU.mult)
                        VE.wait(SC, sc0 + 35)
                        VE.op(v.tensor_tensor, A_VAR[:, :], A_T2[:, :], A_T0[:, :], ALU.subtract)
                        VE.wait(SC, sc0 + 36)
                        VE.op(v.tensor_tensor, A_RS01[:, :], A_RSTD0[:, :], A_RSTD1[:, :], ALU.mult)
                        VE.op(v.tensor_tensor, A_NB01[:, :], A_NB0[:, :], A_RSTD1[:, :], ALU.mult)
                        for tt in range(NT):
                            qq = tt % 2
                            VE.wait(SC, sc0 + a1i(tt))
                            VE.op(v.tensor_scalar, XA[:, tt * D:(tt + 1) * D],
                                  XA[:, tt * D:(tt + 1) * D], A_RSTD0[:, tt:tt + 1],
                                  A_NB0[:, tt:tt + 1], ALU.mult, ALU.add)
                            VE.wait(PE, pe0 + 24 * tt + 24)
                            VE.op(v.tensor_scalar, POTV[0:1, tt * 128:(tt + 1) * 128],
                                  P[4 + qq][0:1, 0:128], BPV2[0:1, 0:1], None, ALU.add)
                            VE.wait(PE, pe0 + 24 * tt + 24)
                            VE.op(v.tensor_scalar, POTP[0:1, tt * 128:(tt + 1) * 128],
                                  P[4 + qq][0:1, 128:256], BPV2[0:1, 1:2], None, ALU.add)

                    @blk.tensor
                    def _(te):
                        PE.eng = te
                        for tt in range(NT):
                            qq = tt % 2
                            PE.wait(SC, sc0 + a1i(tt))
                            for j in range(8):
                                bank = P[2 * qq + j // 4]
                                PE.op(te.matmul, bank[:, (j % 4) * 128:(j % 4 + 1) * 128],
                                      XSLOT[tt % 4][:, j * 128:(j + 1) * 128],
                                      IDCS[:, :], start=True, stop=True)
                            PE.wait(SC, sc0 + te1i(tt))
                            if tt >= 2:
                                PE.wait(VE, ve0 + 43 + 3 * (tt - 2))
                            for c in range(8):
                                PE.op(te.matmul, P[4 + qq][0:1, 0:128], WPVg[:, c:c + 1],
                                      TB[:, qq * 2048 + c * 128:qq * 2048 + (c + 1) * 128],
                                      start=(c == 0), stop=(c == 7))
                            for c in range(8):
                                PE.op(te.matmul, P[4 + qq][0:1, 128:256], WPPg[:, c:c + 1],
                                      TB[:, qq * 2048 + c * 128:qq * 2048 + (c + 1) * 128],
                                      start=(c == 0), stop=(c == 7))

                if debug:
                    with nc.Block() as blk:
                        @blk.sync
                        def _(s):
                            for k in range(8):
                                dst = dbgs["dbg_xa"][512 * k:512 * (k + 1), :].rearrange(
                                    "(j p) f -> p j f", p=128)
                                src_ = XA[:, 4 * k * D:(4 * k + 4) * D].rearrange(
                                    "p (j f) -> p j f", j=4)
                                dma(LD, s, dst, src_)
                            s.wait_ge(sems["ld"], LD.n)

                        @blk.gpsimd
                        def _(g):
                            dma(GLD, g, dbgs["dbg_potV"][:, :], POTV[:, :])
                            dma(GLD, g, dbgs["dbg_potP"][:, :], POTP[:, :])
                            g.wait_ge(sems["gld"], GLD.n)

                if upto >= 2:
                    rowscr, rowscr2 = t["rowscr"], t["rowscr2"]
                    with nc.Block() as blk:
                        @blk.sync
                        def _(s):
                            dma(LD, s, t["rowb1"][:, :], POTV[:, :])
                            dma(LD, s, t["rowb2"][:, :], POTP[:, :])
                            s.wait_ge(sems["ld"], LD.n)

                        @blk.gpsimd
                        def _(g):
                            g.wait_ge(sems["ld"], LD.n)
                            dma(GLD, g, DRE[:, :],
                                t["rowb1"].rearrange("one (p f) -> (one p) f", p=128))
                            dma(GLD, g, GAM32[:, :],
                                t["rowb2"].rearrange("one (p f) -> (one p) f", p=128))
                            g.wait_ge(sems["gld"], GLD.n)
                    esPot.close()
                    GROW = sbt("GROW", [2, N], BF16, esA)
                    TS_RE = sbt("TS_RE", [128, 128], F32, esA)
                    TS_IM = sbt("TS_IM", [128, 128], F32, esA)
                    APL = sbt("APL", [128, 16], F32, esA)
                    APL2 = sbt("APL2", [128, 16], F32, esA)
                    IDP = sbt("IDP", [128, 16], F32, esA)
                    E10R = sbt("E10R", [128, 2], F32, esA); E10I = sbt("E10I", [128, 2], F32, esA)
                    SC32 = sbt("SC32", [128, 512], F32, esA)
                    GRE = sbt("GRE", [128, 32], F32, esA); GIM = sbt("GIM", [128, 32], F32, esA)
                    CROW32 = sbt("CROW32", [32, 128], F32, esA)
                    ZER32 = sbt("ZER32", [32, 128], F32, esA)
                    CFB = sbt("CFB", [32, 128], BF16, esA)
                    PL1 = sbt("PL1", [1, 32], BF16, esA)
                    PLC = sbt("PLC", [32, 1], BF16, esA)

                    with nc.Block() as blk:
                        sc0 = SC.n

                        @blk.scalar
                        def _(s):
                            SC.eng = s
                            SC.op(s.activation, SC32[:, 0:32], GAM32[:, :], AF.Abs)
                            SC.op(s.activation, SC32[:, 32:64], SC32[:, 0:32], AF.Exp,
                                  scale=cp("neg1"))
                            SC.op(s.activation, SC32[:, 32:64], SC32[:, 32:64], AF.Ln, bias=1.0)

                        @blk.vector
                        def _(v):
                            VE.eng = v
                            VE.wait(SC, sc0 + 3)
                            VE.op(v.scalar_tensor_tensor, SC32[:, 64:96], GAM32[:, :], 0.0,
                                  SC32[:, 32:64], ALU.max, ALU.add)
                            VE.op(v.tensor_scalar, GAM32[:, :], SC32[:, 64:96], 0.1, None, ALU.add)
                            VE.op(v.tensor_scalar, DRE[:, :], DRE[:, :], 2.0, None, ALU.add)
                            VE.op(v.tensor_scalar, DIM[:, :], GAM32[:, :], -1.0, None, ALU.mult)
                            VE.op(v.memset, E10R[:, 0:1], 1.0)
                            VE.op(v.memset, E10R[:, 1:2], 0.0)
                            VE.op(v.memset, E10I[:, :], 0.0)
                            VE.op(v.memset, IDP[:, :], 0.0)
                            VE.op(v.memset, IDP[:, 0:1], 1.0)
                            VE.op(v.memset, IDP[:, 5:6], 1.0)
                            VE.op(v.memset, IDP[:, 8:9], 1.0)
                            VE.op(v.memset, IDP[:, 13:14], 1.0)

                    if debug:
                        with nc.Block() as blk:
                            @blk.sync
                            def _(s):
                                dma(LD, s, dbgs["dbg_gam"][:, :], GAM32[:, :])
                                s.wait_ge(sems["ld"], LD.n)

                    with nc.Block() as blk:
                        @blk.vector
                        def _(v):
                            VE.eng = v
                            sa, sb = SC32[:, 96:98], SC32[:, 98:100]

                            def slot_re(o, i):
                                return TS_RE[:, o + 2 * i:o + 2 * i + 2]

                            def slot_im(o, i):
                                return TS_IM[:, o + 2 * i:o + 2 * i + 2]

                            VE.op(v.tensor_copy, TS_RE[:, 0:1], DRE[:, 0:1])
                            VE.op(v.memset, TS_RE[:, 1:2], -1.0)
                            VE.op(v.tensor_copy, TS_IM[:, 0:1], DIM[:, 0:1])
                            VE.op(v.memset, TS_IM[:, 1:2], 0.0)
                            VE.op(v.tensor_copy, TS_RE[:, 64 + 62:64 + 63], DRE[:, 31:32])
                            VE.op(v.memset, TS_RE[:, 64 + 63:64 + 64], -1.0)
                            VE.op(v.tensor_copy, TS_IM[:, 64 + 62:64 + 63], DIM[:, 31:32])
                            VE.op(v.memset, TS_IM[:, 64 + 63:64 + 64], 0.0)
                            for step in range(1, 32):
                                for dirn in range(2):
                                    if dirn == 0:
                                        o, i, cur, prv = 0, step, step - 1, step - 2
                                        dcol = i
                                    else:
                                        f = 31 - step
                                        o, i, cur, prv = 64, f, f + 1, f + 2
                                        dcol = f
                                    pre = E10R[:, :] if step == 1 else slot_re(o, prv)
                                    pim = E10I[:, :] if step == 1 else slot_im(o, prv)
                                    dre_c = DRE[:, dcol:dcol + 1]
                                    dim_c = DIM[:, dcol:dcol + 1]
                                    VE.op(v.tensor_scalar, sa, slot_im(o, cur), dim_c, None, ALU.mult)
                                    VE.op(v.scalar_tensor_tensor, sb, slot_re(o, cur), dre_c,
                                          sa, ALU.mult, ALU.subtract)
                                    VE.op(v.tensor_tensor, slot_re(o, i), sb, pre, ALU.subtract)
                                    VE.op(v.tensor_scalar, sa, slot_re(o, cur), dim_c, None, ALU.mult)
                                    VE.op(v.scalar_tensor_tensor, sb, slot_im(o, cur), dre_c,
                                          sa, ALU.mult, ALU.add)
                                    VE.op(v.tensor_tensor, slot_im(o, i), sb, pim, ALU.subtract)

                    with nc.Block() as blk:
                        sc0, ve0, ld0 = SC.n, VE.n, LD.n
                        NLVL = 7

                        @blk.vector
                        def _(v):
                            VE.eng = v
                            for o, s1, s2, off in [(0, 31, 30, 0), (64, 0, 1, 8)]:
                                VE.op(v.tensor_copy, APL[:, off + 0:off + 2], TS_RE[:, o + 2 * s1:o + 2 * s1 + 2])
                                VE.op(v.tensor_copy, APL[:, off + 2:off + 4], TS_IM[:, o + 2 * s1:o + 2 * s1 + 2])
                                VE.op(v.tensor_copy, APL[:, off + 4:off + 6], TS_RE[:, o + 2 * s2:o + 2 * s2 + 2])
                                VE.op(v.tensor_copy, APL[:, off + 6:off + 8], TS_IM[:, o + 2 * s2:o + 2 * s2 + 2])
                            VE.wait(SC, sc0 + 1)
                            for off in (0, 8):
                                VE.op(v.tensor_reduce, SC32[:, 16 + off // 8:17 + off // 8],
                                      SC32[:, off:off + 8], AX.X, ALU.add)
                                VE.op(v.reciprocal, SC32[:, 18 + off // 8:19 + off // 8],
                                      SC32[:, 16 + off // 8:17 + off // 8])
                                VE.op(v.tensor_scalar, APL[:, off:off + 8], APL[:, off:off + 8],
                                      SC32[:, 18 + off // 8:19 + off // 8], None, ALU.mult)
                            for lv in range(NLVL):
                                VE.wait(LD, ld0 + 64 * (lv + 1))
                                for off in (0, 8):
                                    A = lambda c0, c1: APL[:, off + c0:off + c1]
                                    B = lambda c0, c1: APL2[:, off + c0:off + c1]
                                    scr = lambda k: SC32[:, 128 + 2 * k:128 + 2 * k + 2]
                                    outs = []
                                    for row in range(2):
                                        a0r = APL[:, off + 4 * row + 0:off + 4 * row + 1]
                                        a1r = APL[:, off + 4 * row + 1:off + 4 * row + 2]
                                        a0i = APL[:, off + 4 * row + 2:off + 4 * row + 3]
                                        a1i = APL[:, off + 4 * row + 3:off + 4 * row + 4]
                                        VE.op(v.tensor_scalar, scr(0), B(2, 4), a0i, None, ALU.mult)
                                        VE.op(v.scalar_tensor_tensor, scr(1), B(0, 2), a0r, scr(0),
                                              ALU.mult, ALU.subtract)
                                        VE.op(v.tensor_scalar, scr(2), B(6, 8), a1i, None, ALU.mult)
                                        VE.op(v.scalar_tensor_tensor, scr(3), B(4, 6), a1r, scr(2),
                                              ALU.mult, ALU.subtract)
                                        VE.op(v.tensor_tensor, scr(4 + 2 * row), scr(1), scr(3), ALU.add)
                                        VE.op(v.tensor_scalar, scr(0), B(0, 2), a0i, None, ALU.mult)
                                        VE.op(v.scalar_tensor_tensor, scr(1), B(2, 4), a0r, scr(0),
                                              ALU.mult, ALU.add)
                                        VE.op(v.tensor_scalar, scr(2), B(4, 6), a1i, None, ALU.mult)
                                        VE.op(v.scalar_tensor_tensor, scr(3), B(6, 8), a1r, scr(2),
                                              ALU.mult, ALU.add)
                                        VE.op(v.tensor_tensor, scr(5 + 2 * row), scr(1), scr(3), ALU.add)
                                    VE.op(v.tensor_copy, A(0, 2), scr(4))
                                    VE.op(v.tensor_copy, A(2, 4), scr(5))
                                    VE.op(v.tensor_copy, A(4, 6), scr(6))
                                    VE.op(v.tensor_copy, A(6, 8), scr(7))
                                    VE.wait(SC, sc0 + 1 + 2 * lv + off // 8 + 1)
                                    VE.op(v.tensor_reduce, SC32[:, 20:21], SC32[:, 24 + off:32 + off],
                                          AX.X, ALU.add)
                                    VE.op(v.reciprocal, SC32[:, 21:22], SC32[:, 20:21])
                                    VE.op(v.tensor_scalar, APL[:, off:off + 8], APL[:, off:off + 8],
                                          SC32[:, 21:22], None, ALU.mult)

                        @blk.scalar
                        def _(s):
                            SC.eng = s
                            SC.wait(VE, ve0 + 8)
                            SC.op(s.activation, SC32[:, 0:16], APL[:, 0:16], AF.Abs)
                            for lv in range(NLVL):
                                for off in (0, 8):
                                    VE_done = ve0 + 14 + lv * 54 + (off // 8 + 1) * 24 + (off // 8) * 3
                                    SC.wait(VE, VE_done)
                                    SC.op(s.activation, SC32[:, 24 + off:32 + off],
                                          APL[:, off:off + 8], AF.Abs)

                        @blk.sync
                        def _(s):
                            for lv in range(NLVL):
                                sft = 1 << lv
                                if lv == 0:
                                    s.wait_ge(sems["ve"], ve0 + 14)
                                else:
                                    s.wait_ge(sems["ve"], ve0 + 14 + lv * 54)
                                dma(LD, s, APL2[sft:128, 0:8], APL[0:128 - sft, 0:8])
                                dma(LD, s, APL2[0:sft, 0:8], IDP[0:sft, 0:8])
                                dma(LD, s, APL2[0:128 - sft, 8:16], APL[sft:128, 8:16])
                                dma(LD, s, APL2[128 - sft:128, 8:16], IDP[0:sft, 8:16])
                            s.wait_ge(sems["ve"], ve0 + 14 + NLVL * 54)
                            dma(LD, s, APL2[1:128, 0:8], APL[0:127, 0:8])
                            dma(LD, s, APL2[0:1, 0:8], IDP[0:1, 0:8])
                            dma(LD, s, APL2[0:127, 8:16], APL[1:128, 8:16])
                            dma(LD, s, APL2[127:128, 8:16], IDP[0:1, 8:16])
                            s.wait_ge(sems["ld"], LD.n)

                    with nc.Block() as blk:
                        @blk.vector
                        def _(v):
                            VE.eng = v
                            reg = lambda k: SC32[:, 32 * k:32 * (k + 1)]
                            for dirn in range(2):
                                o = 64 * dirn
                                off = 8 * dirn
                                Sp = lambda c: APL2[:, off + c:off + c + 1]
                                s0r, s0i = Sp(0), Sp(2)
                                s1r, s1i = Sp(4), Sp(6)
                                T00r = TS_RE[:, o + 0:o + 64:2]
                                T01r = TS_RE[:, o + 1:o + 64:2]
                                T00i = TS_IM[:, o + 0:o + 64:2]
                                T01i = TS_IM[:, o + 1:o + 64:2]

                                def mobius(dst_re, dst_im, t00r, t01r, t00i, t01i, w):
                                    tmp = SC32[:, 0:w]; acc = SC32[:, 32:32 + w]
                                    VE.op(v.tensor_scalar, acc, t00r, s0r, None, ALU.mult)
                                    VE.op(v.tensor_scalar, tmp, t00i, s0i, None, ALU.mult)
                                    VE.op(v.tensor_tensor, acc, acc, tmp, ALU.subtract)
                                    VE.op(v.tensor_scalar, tmp, t01r, s1r, None, ALU.mult)
                                    VE.op(v.tensor_tensor, acc, acc, tmp, ALU.add)
                                    VE.op(v.tensor_scalar, tmp, t01i, s1i, None, ALU.mult)
                                    VE.op(v.tensor_tensor, acc, acc, tmp, ALU.subtract)
                                    VE.op(v.tensor_copy, dst_re, acc)
                                    VE.op(v.tensor_scalar, acc, t00i, s0r, None, ALU.mult)
                                    VE.op(v.tensor_scalar, tmp, t00r, s0i, None, ALU.mult)
                                    VE.op(v.tensor_tensor, acc, acc, tmp, ALU.add)
                                    VE.op(v.tensor_scalar, tmp, t01i, s1r, None, ALU.mult)
                                    VE.op(v.tensor_tensor, acc, acc, tmp, ALU.add)
                                    VE.op(v.tensor_scalar, tmp, t01r, s1i, None, ALU.mult)
                                    VE.op(v.tensor_tensor, acc, acc, tmp, ALU.add)
                                    VE.op(v.tensor_copy, dst_im, acc)

                                mobius(reg(2), reg(3), T00r, T01r, T00i, T01i, 32)
                                if dirn == 0:
                                    mobius(reg(4)[:, 1:32], reg(5)[:, 1:32],
                                           TS_RE[:, o + 0:o + 62:2], TS_RE[:, o + 1:o + 62:2],
                                           TS_IM[:, o + 0:o + 62:2], TS_IM[:, o + 1:o + 62:2], 31)
                                    VE.op(v.tensor_scalar, reg(4)[:, 0:1], ONESCF[:, :], s0r,
                                          None, ALU.mult)
                                    VE.op(v.tensor_scalar, reg(5)[:, 0:1], ONESCF[:, :], s0i,
                                          None, ALU.mult)
                                else:
                                    mobius(reg(4)[:, 0:31], reg(5)[:, 0:31],
                                           TS_RE[:, o + 2:o + 64:2], TS_RE[:, o + 3:o + 64:2],
                                           TS_IM[:, o + 2:o + 64:2], TS_IM[:, o + 3:o + 64:2], 31)
                                    VE.op(v.tensor_scalar, reg(4)[:, 31:32], ONESCF[:, :], s0r,
                                          None, ALU.mult)
                                    VE.op(v.tensor_scalar, reg(5)[:, 31:32], ONESCF[:, :], s0i,
                                          None, ALU.mult)
                                VE.op(v.tensor_tensor, reg(6), reg(4), reg(4), ALU.mult)
                                VE.op(v.tensor_tensor, reg(0), reg(5), reg(5), ALU.mult)
                                VE.op(v.tensor_tensor, reg(6), reg(6), reg(0), ALU.add)
                                VE.op(v.reciprocal, reg(6), reg(6))
                                dst_re, dst_im = (reg(7), reg(8)) if dirn == 0 else (reg(9), reg(10))
                                VE.op(v.tensor_tensor, reg(0), reg(2), reg(4), ALU.mult)
                                VE.op(v.tensor_tensor, reg(1), reg(3), reg(5), ALU.mult)
                                VE.op(v.tensor_tensor, reg(0), reg(0), reg(1), ALU.add)
                                VE.op(v.tensor_tensor, dst_re, reg(0), reg(6), ALU.mult)
                                VE.op(v.tensor_tensor, reg(0), reg(3), reg(4), ALU.mult)
                                VE.op(v.tensor_tensor, reg(1), reg(2), reg(5), ALU.mult)
                                VE.op(v.tensor_tensor, reg(0), reg(0), reg(1), ALU.subtract)
                                VE.op(v.tensor_tensor, dst_im, reg(0), reg(6), ALU.mult)
                            VE.op(v.tensor_tensor, reg(0), reg(7), reg(9), ALU.add)
                            VE.op(v.tensor_tensor, reg(0), reg(0), DRE[:, :], ALU.subtract)
                            VE.op(v.tensor_tensor, reg(1), reg(8), reg(10), ALU.add)
                            VE.op(v.tensor_tensor, reg(1), reg(1), DIM[:, :], ALU.subtract)
                            VE.op(v.tensor_tensor, reg(6), reg(0), reg(0), ALU.mult)
                            VE.op(v.tensor_tensor, reg(2), reg(1), reg(1), ALU.mult)
                            VE.op(v.tensor_tensor, reg(6), reg(6), reg(2), ALU.add)
                            VE.op(v.reciprocal, reg(6), reg(6))
                            VE.op(v.tensor_tensor, GRE[:, :], reg(0), reg(6), ALU.mult)
                            VE.op(v.scalar_tensor_tensor, GIM[:, :], reg(1), -1.0, reg(6),
                                  ALU.mult, ALU.mult)

                    if debug:
                        with nc.Block() as blk:
                            @blk.sync
                            def _(s):
                                reg = lambda k: SC32[:, 32 * k:32 * (k + 1)]
                                dma(LD, s, dbgs["dbg_are"][:, :], reg(7))
                                dma(LD, s, dbgs["dbg_aim"][:, :], reg(8))
                                dma(LD, s, dbgs["dbg_bre"][:, :], reg(9))
                                dma(LD, s, dbgs["dbg_bim"][:, :], reg(10))
                                s.wait_ge(sems["ld"], LD.n)

                    with nc.Block() as blk:
                        sc0, ve0, ld0, gld0, pe0 = SC.n, VE.n, LD.n, GLD.n, PE.n

                        @blk.sync
                        def _(s):
                            dma(LD, s, t["rowscr"].rearrange("one (p f) -> (one p) f", p=128),
                                GRE[:, :])
                            dma(LD, s, t["rowscr2"].rearrange("one (p f) -> (one p) f", p=128),
                                GIM[:, :])
                            s.wait_ge(sems["ld"], LD.n)

                        @blk.scalar
                        def _(s):
                            SC.eng = s
                            SC.op(s.activation, SC32[:, 0:32], GAM32[:, :], AF.Exp, scale=cp("negdt"))

                        @blk.gpsimd
                        def _(g):
                            GP.eng = g
                            g.wait_ge(sems["ld"], ld0 + 32)
                            dma(GLD, g, GROW[0:1, :], t["rowscr"][:, :])
                            dma(GLD, g, GROW[1:2, :], t["rowscr2"][:, :])
                            g.wait_ge(sems["gld"], GLD.n)

                    with nc.Block() as blk:
                        sc0, ve0, ld0, gld0, pe0 = SC.n, VE.n, LD.n, GLD.n, PE.n

                        @blk.sync
                        def _(s):
                            dma(LD, s, t["rowscr"].rearrange("one (p f) -> (one p) f", p=128),
                                SC32[:, 0:32])
                            s.wait_ge(sems["ld"], ld0 + 16)
                            dma(LD, s, CROW32[:, :],
                                t["rowscr"].rearrange("one (p f) -> (one p) f", p=32))
                            s.wait_ge(sems["ld"], ld0 + 32)

                        @blk.vector
                        def _(v):
                            VE.eng = v
                            VE.op(v.memset, ZER32[:, :], 0.0)
                            VE.wait(LD, ld0 + 32)
                            VE.op(v.tensor_tensor_scan, PT32[:, :], CROW32[:, :], ZER32[:, :],
                                  1.0, ALU.mult, ALU.add)
                            VE.op(v.reciprocal, CROW32[:, :], PT32[:, :])
                            VE.op(v.tensor_scalar, ZER32[:, :], CROW32[:, :], ETA, None, ALU.mult)

                        @blk.scalar
                        def _(s):
                            SC.eng = s
                            SC.wait(VE, ve0 + 4)
                            SC.op(s.activation, CFB[:, :], ZER32[:, :], AF.Identity)
                            SC.op(s.activation, PLC[:, :], PT32[:, 127:128], AF.Identity)

                        @blk.tensor
                        def _(te):
                            PE.eng = te
                            PE.wait(SC, sc0 + 1)
                            PE.op(te.matmul, P[0][0:128, 0:32], CFB[:, :], IDCS[0:32, 0:32],
                                  start=True, stop=True)
                            PE.wait(SC, sc0 + 2)
                            PE.op(te.matmul, P[1][0:1, 0:32], PLC[:, :], IDCS[0:32, 0:32],
                                  start=True, stop=True)

                    with nc.Block() as blk:
                        sc0, ve0, ld0, gld0, pe0 = SC.n, VE.n, LD.n, GLD.n, PE.n

                        @blk.vector
                        def _(v):
                            VE.eng = v
                            VE.op(v.tensor_copy, RPETA[:, :], P[0][0:128, 0:32])
                            VE.op(v.tensor_copy, PL1[:, :], P[1][0:1, 0:32])

                        @blk.sync
                        def _(s):
                            dma(LD, s, t["rowscr"].rearrange("one (p f) -> (one p) f", p=32),
                                PT32[:, :])
                            dma(LD, s, t["rowscr2"].rearrange("one (p f) -> (one p) f", p=32),
                                CROW32[:, :])
                            s.wait_ge(sems["ld"], LD.n)


                    with nc.Block() as blk:
                        pe0, ve0 = PE.n, VE.n

                        @blk.tensor
                        def _(te):
                            PE.eng = te
                            PE.op(te.matmul, P[2][0:128, 0:32], ONESRS[0:1, 0:128],
                                  PL1[:, :], start=True, stop=True)

                        @blk.vector
                        def _(v):
                            VE.eng = v
                            VE.wait(PE, pe0 + 1)
                            VE.op(v.tensor_copy, PLB2[:, :], P[2][0:128, 0:32])

                    if debug:
                        with nc.Block() as blk:
                            @blk.gpsimd
                            def _(g):
                                dma(GLD, g, dbgs["dbg_G"][:, :], GROW[:, :])
                                g.wait_ge(sems["gld"], GLD.n)

                            @blk.sync
                            def _(s):
                                dma(LD, s, dbgs["dbg_pt"][:, :], PT32[:, :])
                                s.wait_ge(sems["ld"], LD.n)

                if upto >= 3:
                    with nc.Block() as blk:
                        sc0, ve0, pe0 = SC.n, VE.n, PE.n

                        @blk.tensor
                        def _(te):
                            PE.eng = te
                            for tt in range(NT):
                                qq = tt % 2
                                if tt >= 2:
                                    PE.wait(VE, ve0 + 3 * (tt - 2) + 2)
                                for j in range(2):
                                    PE.op(te.matmul, P[2 * qq + j][:, :],
                                          GROW[:, tt * 128:(tt + 1) * 128],
                                          WBK[:, j * 512:(j + 1) * 512], start=True, stop=False)
                                    PE.op(te.matmul, P[2 * qq + j][:, :],
                                          ONER128[0:1, 0:128],
                                          BBKR[0:1, j * 512:(j + 1) * 512], start=False, stop=True)

                        @blk.vector
                        def _(v):
                            VE.eng = v
                            for tt in range(NT):
                                qq = tt % 2
                                xat = XA[:, tt * D:(tt + 1) * D]
                                VE.wait(PE, pe0 + 4 * tt + 4)
                                VE.op(v.tensor_tensor, xat[:, 0:512], xat[:, 0:512],
                                      P[2 * qq][:, :], ALU.add)
                                VE.op(v.tensor_tensor, xat[:, 512:1024], xat[:, 512:1024],
                                      P[2 * qq + 1][:, :], ALU.add)
                                VE.op(v.tensor_scalar, SXN[:, 0:1024], xat, 1.0, 0.0,
                                      ALU.mult, ALU.add, accum_out=STB[:, tt:tt + 1])
                            VE.op(v.tensor_scalar, STB[:, 64:96], STB[:, 0:32], 1.0 / D,
                                  None, ALU.mult)
                            VE.op(v.tensor_tensor, STB[:, 96:128], STB[:, 64:96],
                                  STB[:, 64:96], ALU.mult)
                            VE.wait(SC, sc0 + 32)
                            VE.op(v.scalar_tensor_tensor, STB[:, 128:160], STB[:, 32:64],
                                  1.0 / D, STB[:, 96:128], ALU.mult, ALU.subtract)
                            VE.wait(SC, sc0 + 34)
                            VE.op(v.scalar_tensor_tensor, NB2[:, :], STB[:, 64:96], -1.0,
                                  RSTD2[:, :], ALU.mult, ALU.mult)

                        @blk.scalar
                        def _(s):
                            SC.eng = s
                            for tt in range(NT):
                                SC.wait(VE, ve0 + 3 * tt + 2)
                                SC.op(s.activation, SSQ[:, :], XA[:, tt * D:(tt + 1) * D],
                                      AF.Square, accum_out=STB[:, 32 + tt:33 + tt])
                            SC.wait(VE, ve0 + 99)
                            SC.op(s.activation, STB[:, 160:192], STB[:, 128:160], AF.Ln,
                                  bias=cp("eps"))
                            SC.op(s.activation, RSTD2[:, :], STB[:, 160:192], AF.Exp,
                                  scale=cp("neg05"))

                    if debug:
                        with nc.Block() as blk:
                            @blk.sync
                            def _(s):
                                for k in range(8):
                                    dst = dbgs["dbg_xa4"][512 * k:512 * (k + 1), :].rearrange(
                                        "(j p) f -> p j f", p=128)
                                    src_ = XA[:, 4 * k * D:(4 * k + 4) * D].rearrange(
                                        "p (j f) -> p j f", j=4)
                                    dma(LD, s, dst, src_)
                                s.wait_ge(sems["ld"], LD.n)

            if upto >= 4:
                esA5 = ExitStack()
                with esA5:
                    WQKV = sbt("WQKV", [128, 12288], BF16, esA5)
                    WOUT = sbt("WOUT", [128, 4096], BF16, esA5)
                    BVB = sbt("BVB", [128, 512], BF16, esA5)
                    BKB = sbt("BKB", [128, 512], BF16, esA5)
                    MSK4 = sbt("MSK4", [128, 512], F32, esA5)
                    PRB = sbt("PRB", [1, 512], BF16, esA5)
                    def QTBs(p0, p1, c0, c1):
                        return SSQ[p0:p1, c0 // 2:c1 // 2].bitcast(BF16)

                    def KTBs(p0, p1, c0, c1):
                        return SSQ[p0:p1, 512 + c0 // 2:512 + c1 // 2].bitcast(BF16)
                    VAB = sbt("VAB", [128, 512], BF16, esA5)
                    KAB = sbt("KAB", [128, 512], BF16, esA5)
                    PBC = sbt("PBC", [128, 128], BF16, esA5)
                    RPBC = sbt("RPBC", [128, 128], BF16, esA5)
                    SMT = sbt("SMT", [128, 1024], BF16, esA5)
                    OTB = sbt("OTB", [128, 512], BF16, esA5)
                    WHEB = sbt("WHEB", [64, 512], F32, esA5)
                    WHEBB = sbt("WHEBB", [64, 512], BF16, esA5)

                    emit_gb_broadcast("ln2g", "ln2b")
                    emit_row_broadcast("b_out", BCAST)
                    with nc.Block() as blk:
                        @blk.gpsimd
                        def _(g):
                            for c in range(8):
                                dma(GLD, g, WQKV[:, c * 1536:(c + 1) * 1536],
                                    t["w_qkv"][128 * c:128 * (c + 1), :])
                            for c in range(4):
                                dma(GLD, g, WOUT[:, c * 1024:(c + 1) * 1024],
                                    t["w_out"][128 * c:128 * (c + 1), :])
                            dma(GLD, g, BVB[0:1, 0:512], t["b_qkv_v"][:, :])
                            dma(GLD, g, BKB[0:1, 0:512], t["b_qkv_k"][:, :])
                            g.wait_ge(sems["gld"], GLD.n)

                        @blk.sync
                        def _(s):
                            dma(LD, s, MSK4[:, :], t["MASKETA4"][:, :])
                            s.wait_ge(sems["ld"], LD.n)

                    with nc.Block() as blk:
                        pe0v = PE.n

                        @blk.tensor
                        def _(te):
                            PE.eng = te
                            PE.op(te.matmul, P[6][:, 0:512], ONESRS[0:1, 0:128], BVB[0:1, 0:512],
                                  start=True, stop=True)
                            PE.op(te.matmul, P[7][:, 0:512], ONESRS[0:1, 0:128], BKB[0:1, 0:512],
                                  start=True, stop=True)

                        @blk.vector
                        def _(v):
                            VE.eng = v
                            VE.wait(PE, pe0v + 2)
                            VE.op(v.tensor_copy, BVB[:, 0:512], P[6][:, 0:512])
                            VE.op(v.tensor_copy, BKB[:, 0:512], P[7][:, 0:512])
                            VE.op(v.memset, WHEB[:, :], 0.0)
                            VE.op(v.memset, WHEBB[:, :], 0.0)

                    import os
                    N5 = int(os.environ.get("K5N", NT))
                    STG = int(os.environ.get("K5STG", 9))
                    with nc.Block() as blk:
                        sc0, ve0, pe0, gld0 = SC.n, VE.n, PE.n, GLD.n
                        SCP, VEP, PEP = 30, 29, 130
                        s_ = lambda c, k: sc0 + SCP * c + k
                        v_ = lambda c, k: ve0 + VEP * c + k
                        TRD = lambda c: pe0 + PEP * c + 8
                        QD = lambda c: pe0 + PEP * c + 40
                        KD = lambda c: pe0 + PEP * c + 72
                        VD = lambda c: pe0 + PEP * c + 80
                        KAD = lambda c: pe0 + PEP * c + 88
                        BCD = lambda c: pe0 + PEP * c + 90
                        SD = lambda c: pe0 + PEP * c + 98
                        OD = lambda c: pe0 + PEP * c + 114
                        DWD = lambda c: pe0 + PEP * c + 122
                        WOD = lambda c: pe0 + PEP * c + 130

                        @blk.gpsimd
                        def _(g):
                            for c in range(N5):
                                if STG < 3:
                                    continue
                                qq = c % 2
                                if c >= 2:
                                    g.wait_ge(sems["pe"], BCD(c - 2))
                                dma(GLD, g, PRB[0:1, 256 * qq:256 * qq + 128],
                                    t["rowscr"][0:1, c * 128:(c + 1) * 128])
                                dma(GLD, g, PRB[0:1, 256 * qq + 128:256 * qq + 256],
                                    t["rowscr2"][0:1, c * 128:(c + 1) * 128])

                        @blk.scalar
                        def _(s):
                            SC.eng = s
                            for c in range(N5):
                                qq = c % 2
                                xat = XA[:, c * D:(c + 1) * D]
                                if c >= 2:
                                    SC.wait(VE, v_(c - 2, 2))
                                SC.op(s.activation, SXN[:, qq * D:(qq + 1) * D], xat, AF.Identity,
                                      scale=RSTD2[:, c:c + 1], bias=NB2[:, c:c + 1])
                                SC.wait(PE, TRD(c))
                                SC.op(s.activation, TB[:, qq * 2048:qq * 2048 + 512],
                                      P[2 * qq][:, :], AF.Identity)
                                SC.op(s.activation, TB[:, qq * 2048 + 512:qq * 2048 + 1024],
                                      P[2 * qq + 1][:, :], AF.Identity)
                                if STG < 2:
                                    continue
                                SC.wait(PE, QD(c))
                                for h in range(8):
                                    SC.op(s.activation,
                                          SXG[0:64, qq * D + 128 * h:qq * D + 128 * (h + 1)],
                                          P[4][64 * (h % 2):64 * (h % 2) + 64,
                                               128 * (h // 2):128 * (h // 2 + 1)], AF.Identity,
                                          bias=BQC[64 * (h % 2):64 * (h % 2) + 64, h // 2:h // 2 + 1])
                                SC.wait(PE, KD(c))
                                for h in range(8):
                                    SC.op(s.activation,
                                          SXG[64:128, qq * D + 128 * h:qq * D + 128 * (h + 1)],
                                          P[5][64 * (h % 2):64 * (h % 2) + 64,
                                               128 * (h // 2):128 * (h // 2 + 1)], AF.Identity,
                                          bias=BQC[64 * (h % 2):64 * (h % 2) + 64, 4 + h // 2:5 + h // 2])
                                if STG < 3:
                                    continue
                                SC.wait(PE, BCD(c))
                                SC.op(s.activation, PBC[:, :], P[2 * qq][:, 0:128], AF.Identity)
                                SC.op(s.activation, RPBC[:, :], P[2 * qq][:, 128:256], AF.Identity)
                                if STG < 6:
                                    continue
                                SC.wait(PE, OD(c))
                                for h in range(8):
                                    pb = P[4] if h < 4 else P[6]
                                    SC.op(s.activation,
                                          OTB[64 * (h % 2):64 * (h % 2) + 64,
                                              128 * (h // 2):128 * (h // 2 + 1)],
                                          pb[0:64, 128 * (h % 4):128 * (h % 4 + 1)], AF.Identity)
                                if STG < 7:
                                    continue
                                SC.wait(VE, v_(c, 25))
                                SC.op(s.activation, WHEBB[:, :], WHEB[:, :], AF.Identity)

                        @blk.vector
                        def _(v):
                            VE.eng = v
                            for c in range(N5):
                                qq = c % 2
                                xat = XA[:, c * D:(c + 1) * D]
                                VE.wait(SC, s_(c, 1))
                                VE.op(v.tensor_tensor, SXG[:, qq * D:(qq + 1) * D],
                                      SXN[:, qq * D:(qq + 1) * D], GGB[:, 0:D], ALU.mult)
                                VE.op(v.tensor_tensor, SXG[:, qq * D:(qq + 1) * D],
                                      SXG[:, qq * D:(qq + 1) * D], GGB[:, D:2 * D], ALU.add)
                                if STG < 2:
                                    continue
                                VE.wait(PE, VD(c))
                                if c >= 1:
                                    VE.wait(PE, DWD(c - 1))
                                VE.op(v.tensor_tensor, VAB[:, :], P[6][:, :], BVB[:, :], ALU.add)
                                VE.wait(PE, KAD(c))
                                VE.op(v.tensor_tensor, KAB[:, :], P[7][:, :], BKB[:, :], ALU.add)
                                VE.op(v.tensor_scalar, KAB[:, :], KAB[:, :], RPETA[:, c:c + 1],
                                      None, ALU.mult)
                                if STG < 4:
                                    continue
                                VE.wait(SC, s_(c, 21))
                                if c >= 1:
                                    VE.wait(PE, OD(c - 1))
                                for h in range(8):
                                    VE.op(v.tensor_tensor, QTBs(0, 64, 128 * h, 128 * (h + 1)),
                                          SXG[0:64, qq * D + 128 * h:qq * D + 128 * (h + 1)],
                                          PBC[0:64, :], ALU.mult)
                                for h in range(8):
                                    VE.op(v.tensor_tensor, KTBs(0, 64, 128 * h, 128 * (h + 1)),
                                          SXG[64:128, qq * D + 128 * h:qq * D + 128 * (h + 1)],
                                          RPBC[64:128, :], ALU.mult)
                                if STG < 5:
                                    continue
                                VE.wait(PE, SD(c))
                                VE.op(v.tensor_tensor, SMT[:, 0:512], P[2 * qq][:, :],
                                      MSK4[:, :], ALU.mult)
                                VE.op(v.tensor_tensor, SMT[:, 512:1024], P[2 * qq + 1][:, :],
                                      MSK4[:, :], ALU.mult)
                                if STG < 7:
                                    continue
                                VE.wait(PE, DWD(c))
                                VE.op(v.tensor_tensor, WHEB[:, :], WHEB[:, :], P[5][0:64, :], ALU.add)
                                VE.op(v.tensor_scalar, WHEB[:, :], WHEB[:, :], PLB2[0:64, c:c + 1],
                                      None, ALU.mult)
                                if STG < 8:
                                    continue
                                VE.wait(PE, WOD(c))
                                VE.op(v.tensor_tensor, xat[:, 0:512], xat[:, 0:512],
                                      P[6][:, :], ALU.add)
                                VE.op(v.tensor_tensor, xat[:, 512:1024], xat[:, 512:1024],
                                      P[7][:, :], ALU.add)
                                VE.op(v.tensor_tensor, xat[:, 0:512], xat[:, 0:512],
                                      BCAST[:, 0:512], ALU.add)
                                VE.op(v.tensor_tensor, xat[:, 512:1024], xat[:, 512:1024],
                                      BCAST[:, 512:1024], ALU.add)

                        @blk.tensor
                        def _(te):
                            PE.eng = te
                            for c in range(N5):
                                qq = c % 2
                                PE.wait(VE, v_(c, 2))
                                if c >= 2:
                                    PE.wait(VE, v_(c - 2, 23))
                                    PE.wait(SC, s_(c - 2, SCP))
                                for j in range(8):
                                    bank = P[2 * qq + j // 4]
                                    PE.op(te.matmul, bank[:, (j % 4) * 128:(j % 4 + 1) * 128],
                                          SXG[:, qq * D + j * 128:qq * D + (j + 1) * 128],
                                          IDCS[:, :], start=True, stop=True)
                                if STG < 2:
                                    continue
                                PE.wait(SC, s_(c, 3))
                                if c >= 1 and STG >= 7:
                                    PE.wait(SC, s_(c - 1, 30))
                                if c >= 1 and STG >= 8:
                                    PE.wait(VE, v_(c - 1, 29))
                                if c >= 1 and STG >= 6:
                                    PE.wait(SC, s_(c - 1, 29))
                                for mt in range(4):
                                    for cc in range(8):
                                        PE.op(te.matmul, P[4][:, 128 * mt:128 * (mt + 1)],
                                              WQKV[:, cc * 1536 + 128 * mt:cc * 1536 + 128 * (mt + 1)],
                                              TB[:, qq * 2048 + cc * 128:qq * 2048 + (cc + 1) * 128],
                                              start=(cc == 0), stop=(cc == 7))
                                for mt in range(4):
                                    for cc in range(8):
                                        PE.op(te.matmul, P[5][:, 128 * mt:128 * (mt + 1)],
                                              WQKV[:, cc * 1536 + 512 + 128 * mt:cc * 1536 + 512 + 128 * (mt + 1)],
                                              TB[:, qq * 2048 + cc * 128:qq * 2048 + (cc + 1) * 128],
                                              start=(cc == 0), stop=(cc == 7))
                                for cc in range(8):
                                    PE.op(te.matmul, P[6][:, :],
                                          TB[:, qq * 2048 + cc * 128:qq * 2048 + (cc + 1) * 128],
                                          WQKV[:, cc * 1536 + 1024:cc * 1536 + 1536],
                                          start=(cc == 0), stop=(cc == 7))
                                for cc in range(8):
                                    PE.op(te.matmul, P[7][:, :],
                                          TB[:, qq * 2048 + cc * 128:qq * 2048 + (cc + 1) * 128],
                                          WQKV[:, cc * 1536 + 512:cc * 1536 + 1024],
                                          start=(cc == 0), stop=(cc == 7))
                                if STG < 3:
                                    continue
                                PE.wait(GLD, gld0 + 32 * min(2 * (c // 2 + 1), N5))
                                PE.op(te.matmul, P[2 * qq][:, 0:128], ONESRS[0:1, 0:128],
                                      PRB[0:1, 256 * qq:256 * qq + 128], start=True, stop=True)
                                PE.op(te.matmul, P[2 * qq][:, 128:256], ONESRS[0:1, 0:128],
                                      PRB[0:1, 256 * qq + 128:256 * qq + 256], start=True, stop=True)
                                if STG < 5:
                                    continue
                                PE.wait(VE, v_(c, 21))
                                PE.wait(SC, s_(c, 21))
                                for h in range(8):
                                    PE.op(te.matmul,
                                          P[2 * qq + h // 4][:, 128 * (h % 4):128 * (h % 4 + 1)],
                                          KTBs(0, 64, 128 * h, 128 * (h + 1)),
                                          QTBs(0, 64, 128 * h, 128 * (h + 1)),
                                          start=True, stop=True)
                                if STG < 6:
                                    continue
                                PE.wait(VE, v_(c, 23))
                                for h in range(8):
                                    pb = P[4] if h < 4 else P[6]
                                    ops = pb[0:64, 128 * (h % 4):128 * (h % 4 + 1)]
                                    PE.op(te.matmul, ops, WHEBB[0:64, 64 * h:64 * (h + 1)],
                                          QTBs(0, 64, 128 * h, 128 * (h + 1)),
                                          start=True, stop=False)
                                    PE.op(te.matmul, ops, VAB[:, 64 * h:64 * (h + 1)],
                                          SMT[:, 128 * h:128 * (h + 1)],
                                          start=False, stop=True)
                                if STG < 7:
                                    continue
                                for h in range(8):
                                    PE.op(te.matmul, P[5][0:64, 64 * h:64 * (h + 1)],
                                          KAB[:, 64 * h:64 * (h + 1)], VAB[:, 64 * h:64 * (h + 1)],
                                          start=True, stop=True)
                                if STG < 8:
                                    continue
                                PE.wait(SC, s_(c, 29))
                                for j in range(2):
                                    for pch in range(4):
                                        PE.op(te.matmul, P[6 + j][:, :],
                                              OTB[:, 128 * pch:128 * (pch + 1)],
                                              WOUT[:, pch * 1024 + j * 512:pch * 1024 + (j + 1) * 512],
                                              start=(pch == 0), stop=(pch == 3))

                    if debug:
                        with nc.Block() as blk:
                            @blk.sync
                            def _(s):
                                for k in range(8):
                                    dst = dbgs["dbg_xa5"][512 * k:512 * (k + 1), :].rearrange(
                                        "(j p) f -> p j f", p=128)
                                    src_ = XA[:, 4 * k * D:(4 * k + 4) * D].rearrange(
                                        "p (j f) -> p j f", j=4)
                                    dma(LD, s, dst, src_)
                                s.wait_ge(sems["ld"], LD.n)


        if upto >= 5:
            with nc.Block() as blk:
                sc0, ve0 = SC.n, VE.n

                @blk.scalar
                def _(s):
                    SC.eng = s
                    for tt in range(NT):
                        SC.op(s.activation, SSQ[:, :], XA[:, tt * D:(tt + 1) * D],
                              AF.Square, accum_out=STB[:, 32 + tt:33 + tt])
                    SC.wait(VE, ve0 + 35)
                    SC.op(s.activation, STB[:, 160:192], STB[:, 128:160], AF.Ln, bias=cp("eps"))
                    SC.op(s.activation, RSTD3[:, :], STB[:, 160:192], AF.Exp, scale=cp("neg05"))

                @blk.vector
                def _(v):
                    VE.eng = v
                    for tt in range(NT):
                        VE.op(v.tensor_scalar, SXN[:, 0:1024], XA[:, tt * D:(tt + 1) * D],
                              1.0, 0.0, ALU.mult, ALU.add, accum_out=STB[:, tt:tt + 1])
                    VE.op(v.tensor_scalar, STB[:, 64:96], STB[:, 0:32], 1.0 / D, None, ALU.mult)
                    VE.op(v.tensor_tensor, STB[:, 96:128], STB[:, 64:96], STB[:, 64:96], ALU.mult)
                    VE.wait(SC, sc0 + 32)
                    VE.op(v.scalar_tensor_tensor, STB[:, 128:160], STB[:, 32:64], 1.0 / D,
                          STB[:, 96:128], ALU.mult, ALU.subtract)
                    VE.wait(SC, sc0 + 34)
                    VE.op(v.scalar_tensor_tensor, NB3[:, :], STB[:, 64:96], -1.0,
                          RSTD3[:, :], ALU.mult, ALU.mult)

        if upto >= 5:
            emit_gb_broadcast("ln3g", "ln3b")
            emit_row_broadcast("b_ff2", BCAST)
            esA6 = ExitStack()
            with esA6:
                WF1 = sbt("WF1", [128, 2 * 2048], BF16, esA6)
                WF2 = sbt("WF2", [128, 2 * 2048], BF16, esA6)
                H1B = sbt("H1B", [128, 32 * 512], BF16, esA6)
                BF1C = sbt("BF1C", [128, 32], F32, esA6)
                SF1 = SSQ
                SF3H = lambda m: SXG[:, (m % 2) * 512:(m % 2 + 1) * 512]
                SF1H = lambda m: SSQ[:, (m % 2) * 512:(m % 2 + 1) * 512]

                with nc.Block() as blk:
                    gld0, ld0 = GLD.n, LD.n

                    @blk.sync
                    def _(s):
                        dma(LD, s, BF1C[:, :], t["b_ff1"][:, :])
                        for i in range(16):
                            c, half = i // 2, i % 2
                            s.wait_ge(sems["gld"], gld0 + 16 * (i + 1))
                            dma(LD, s, t["wff1bf"][:, c * 4096 + half * 2048:c * 4096 + (half + 1) * 2048],
                                WF1[:, half * 2048:(half + 1) * 2048])
                        for k in range(32):
                            half = k % 2
                            s.wait_ge(sems["gld"], gld0 + 16 * (16 + k + 1))
                            dma(LD, s, t["wff2bf"][:, k * 1024:(k + 1) * 1024],
                                WF2[:, half * 2048:half * 2048 + 1024])

                    @blk.gpsimd
                    def _(g):
                        for i in range(16):
                            c, half = i // 2, i % 2
                            if i >= 2:
                                g.wait_ge(sems["ld"], ld0 + 16 * i)
                            dma(GLD, g, WF1[:, half * 2048:(half + 1) * 2048],
                                t["w_ff1"][128 * c:128 * (c + 1), half * 2048:(half + 1) * 2048])
                        for k in range(32):
                            half = k % 2
                            if k >= 2:
                                g.wait_ge(sems["ld"], ld0 + 16 * (16 + k))
                            dma(GLD, g, WF2[:, half * 2048:half * 2048 + 1024],
                                t["w_ff2"][128 * k:128 * (k + 1), :])
                        g.wait_ge(sems["ld"], ld0 + 16 * 49)

                GA = 0.044715 * 1.5957691216057308
                GB = 1.5957691216057308
                TB3 = TB[:, :].rearrange("p (c f) -> p c f", c=8)
                W1BF = t["wff1bf"].rearrange("p (c f) -> p c f", c=8)

                with nc.Block() as blk:
                    sc0, ve0, pe0, ld0 = SC.n, VE.n, PE.n, LD.n
                    SCP6, VEP6, PEP6 = 108, 120, 544
                    f1d = lambda b, mg: pe0 + PEP6 * b + 32 + 16 * (mg + 1)
                    f1dm = lambda b, m: pe0 + PEP6 * b + 32 + 8 * (m + 1)
                    f2d = lambda b, g: pe0 + PEP6 * b + 288 + 16 * (g + 1)
                    trd = lambda b, tb: pe0 + PEP6 * b + 8 * (tb + 1)
                    s_ap = lambda b, tb: sc0 + SCP6 * b + tb + 1
                    s_tev = lambda b, tb: sc0 + SCP6 * b + 4 + 2 * (tb + 1)
                    s_gel = lambda b, m, j: sc0 + SCP6 * b + 12 + 3 * m + j
                    v_gb = lambda b, tb: ve0 + VEP6 * b + 2 * (tb + 1)
                    v_gel = lambda b, m, j: ve0 + VEP6 * b + 8 + 3 * m + j
                    v_ev = lambda b, k: ve0 + VEP6 * b + 104 + k

                    @blk.sync
                    def _(s):
                        for b in range(8):
                            for mg in range(16):
                                if mg >= 2:
                                    s.wait_ge(sems["pe"], f1d(b, mg - 2))
                                elif b >= 1:
                                    s.wait_ge(sems["pe"], pe0 + PEP6 * b)
                                dst = WF1[:, (mg % 2) * 2048:(mg % 2 + 1) * 2048].rearrange(
                                    "p (c f) -> p c f", c=8)
                                dma(LD, s, dst, W1BF[:, :, mg * 256:(mg + 1) * 256])
                            for g in range(16):
                                if g >= 2:
                                    s.wait_ge(sems["pe"], f2d(b, g - 2))
                                dma(LD, s, WF2[:, (g % 2) * 2048:(g % 2 + 1) * 2048],
                                    t["wff2bf"][:, g * 2048:(g + 1) * 2048])

                    @blk.scalar
                    def _(s):
                        SC.eng = s
                        for b in range(8):
                            for tb in range(4):
                                tt = 4 * b + tb
                                qq = tb % 2
                                if tb >= 2:
                                    SC.wait(VE, v_gb(b, tb - 2))
                                elif b >= 1:
                                    SC.wait(VE, ve0 + VEP6 * b)
                                SC.op(s.activation, SXN[:, qq * D:(qq + 1) * D],
                                      XA[:, tt * D:(tt + 1) * D], AF.Identity,
                                      scale=RSTD3[:, tt:tt + 1], bias=NB3[:, tt:tt + 1])
                            for tb in range(4):
                                SC.wait(PE, trd(b, tb))
                                SC.op(s.activation, TB3[:, 0:4, 128 * tb:128 * (tb + 1)],
                                      P[2 * (tb % 2)][:, :], AF.Identity)
                                SC.op(s.activation, TB3[:, 4:8, 128 * tb:128 * (tb + 1)],
                                      P[2 * (tb % 2) + 1][:, :], AF.Identity)
                            for m in range(32):
                                bank = P[m % 4]
                                SC.wait(PE, f1dm(b, m))
                                SC.op(s.activation, SF3H(m), bank[:, :], AF.Identity,
                                      bias=BF1C[:, m:m + 1])
                                SC.op(s.activation, SF1H(m), bank[:, :], AF.Square,
                                      bias=BF1C[:, m:m + 1])
                                SC.wait(VE, v_gel(b, m, 2))
                                SC.op(s.activation, SF1H(m), SF1H(m), AF.Sigmoid)

                    @blk.vector
                    def _(v):
                        VE.eng = v
                        for b in range(8):
                            for tb in range(4):
                                qq = tb % 2
                                VE.wait(SC, s_ap(b, tb))
                                if b >= 1 and tb == 0:
                                    VE.wait(PE, pe0 + PEP6 * b)
                                if tb >= 2:
                                    VE.wait(PE, trd(b, tb - 2))
                                VE.op(v.tensor_tensor, SXG[:, qq * D:(qq + 1) * D],
                                      SXN[:, qq * D:(qq + 1) * D], GGB[:, 0:D], ALU.mult)
                                VE.op(v.tensor_tensor, SXG[:, qq * D:(qq + 1) * D],
                                      SXG[:, qq * D:(qq + 1) * D], GGB[:, D:2 * D], ALU.add)
                            for m in range(32):
                                VE.wait(SC, s_gel(b, m, 2))
                                VE.op(v.tensor_scalar, SF1H(m), SF1H(m), GA, GB, ALU.mult, ALU.add)
                                VE.op(v.tensor_tensor, SF1H(m), SF3H(m), SF1H(m), ALU.mult)
                                VE.wait(SC, s_gel(b, m, 3))
                                VE.op(v.tensor_tensor, H1B[:, m * 512:(m + 1) * 512],
                                      SF3H(m), SF1H(m), ALU.mult)
                            VE.wait(PE, pe0 + PEP6 * (b + 1))
                            k = 0
                            for tb in range(4):
                                tt = 4 * b + tb
                                for j in range(2):
                                    xh = XA[:, tt * D + j * 512:tt * D + (j + 1) * 512]
                                    k += 1
                                    VE.op(v.tensor_tensor, xh, xh, P[2 * tb + j][:, :], ALU.add)
                                    k += 1
                                    VE.op(v.tensor_tensor, xh, xh,
                                          BCAST[:, j * 512:(j + 1) * 512], ALU.add)

                    @blk.tensor
                    def _(te):
                        PE.eng = te
                        for b in range(8):
                            for tb in range(4):
                                qq = tb % 2
                                PE.wait(VE, v_gb(b, tb))
                                if b >= 1 and tb <= 1:
                                    PE.wait(VE, ve0 + VEP6 * b)
                                if tb >= 2:
                                    PE.wait(SC, s_tev(b, tb - 2))
                                for j in range(8):
                                    bank = P[2 * qq + j // 4]
                                    PE.op(te.matmul, bank[:, (j % 4) * 128:(j % 4 + 1) * 128],
                                          SXG[:, qq * D + j * 128:qq * D + (j + 1) * 128],
                                          IDCS[:, :], start=True, stop=True)
                            for m in range(32):
                                mg = m // 2
                                bank = P[m % 4]
                                PE.wait(SC, sc0 + SCP6 * b + 12)
                                PE.wait(LD, ld0 + 16 * (32 * b + mg + 1))
                                if m >= 4:
                                    PE.wait(SC, s_gel(b, m - 4, 2))
                                for cc in range(8):
                                    lh = WF1[:, (mg % 2) * 2048 + cc * 256 + (m % 2) * 128:
                                             (mg % 2) * 2048 + cc * 256 + (m % 2) * 128 + 128]
                                    PE.op(te.matmul, bank[:, :], lh,
                                          TB[:, cc * 512:(cc + 1) * 512],
                                          start=(cc == 0), stop=(cc == 7))
                            for g in range(16):
                                PE.wait(LD, ld0 + 16 * (32 * b + 16 + g + 1))
                                PE.wait(VE, v_gel(b, 4 * g + 3 if 4 * g + 3 < 32 else 31, 3))
                                if g == 0:
                                    PE.wait(SC, s_gel(b, 31, 2))
                                for tb in range(4):
                                    for j in range(2):
                                        pb = P[2 * tb + j]
                                        for ck in range(2):
                                            kk = 2 * g + ck
                                            PE.op(te.matmul, pb[:, :],
                                                  H1B[:, kk * 512 + 128 * tb:kk * 512 + 128 * (tb + 1)],
                                                  WF2[:, (g % 2) * 2048 + ck * 1024 + j * 512:
                                                      (g % 2) * 2048 + ck * 1024 + (j + 1) * 512],
                                                  start=(g == 0 and ck == 0),
                                                  stop=(g == 15 and ck == 1))

            if debug:
                with nc.Block() as blk:
                    @blk.sync
                    def _(s):
                        for k in range(8):
                            dst = dbgs["dbg_xa6"][512 * k:512 * (k + 1), :].rearrange(
                                "(j p) f -> p j f", p=128)
                            src_ = XA[:, 4 * k * D:(4 * k + 4) * D].rearrange(
                                "p (j f) -> p j f", j=4)
                            dma(LD, s, dst, src_)
                        s.wait_ge(sems["ld"], LD.n)

        if upto >= 6:
            XINB = TB[:, :].bitcast(F32)
            B0RSTD = STB[:, 192:224]
            B0NB = STB[:, 224:256]
            with nc.Block() as blk:
                sc0, ve0, ld0 = SC.n, VE.n, LD.n

                @blk.scalar
                def _(s):
                    SC.eng = s
                    for tt in range(NT):
                        SC.op(s.activation, SSQ[:, :], XA[:, tt * D:(tt + 1) * D],
                              AF.Square, accum_out=STB[:, 32 + tt:33 + tt])
                    SC.wait(VE, ve0 + 35)
                    SC.op(s.activation, STB[:, 160:192], STB[:, 128:160], AF.Ln, bias=cp("eps"))
                    SC.op(s.activation, B0RSTD[:, :], STB[:, 160:192], AF.Exp, scale=cp("neg05"))
                    for tt in range(NT):
                        if tt == 0:
                            SC.wait(VE, ve0 + 36)
                        SC.op(s.activation, XA[:, tt * D:(tt + 1) * D],
                              XA[:, tt * D:(tt + 1) * D], AF.Identity,
                              scale=B0RSTD[:, tt:tt + 1], bias=B0NB[:, tt:tt + 1])

                @blk.vector
                def _(v):
                    VE.eng = v
                    for tt in range(NT):
                        VE.op(v.tensor_scalar, SXN[:, 0:1024], XA[:, tt * D:(tt + 1) * D],
                              1.0, 0.0, ALU.mult, ALU.add, accum_out=STB[:, tt:tt + 1])
                    VE.op(v.tensor_scalar, STB[:, 64:96], STB[:, 0:32], 1.0 / D, None, ALU.mult)
                    VE.op(v.tensor_tensor, STB[:, 96:128], STB[:, 64:96], STB[:, 64:96], ALU.mult)
                    VE.wait(SC, sc0 + 32)
                    VE.op(v.scalar_tensor_tensor, STB[:, 128:160], STB[:, 32:64], 1.0 / D,
                          STB[:, 96:128], ALU.mult, ALU.subtract)
                    VE.wait(SC, sc0 + 34)
                    VE.op(v.scalar_tensor_tensor, B0NB[:, :], STB[:, 64:96], -1.0,
                          B0RSTD[:, :], ALU.mult, ALU.mult)
                    for tt in range(NT):
                        VE.wait(SC, sc0 + 35 + tt)
                        VE.wait(LD, ld0 + 32 * tt + 16)
                        VE.op(v.tensor_tensor, XA[:, tt * D:(tt + 1) * D],
                              XA[:, tt * D:(tt + 1) * D],
                              XINB[:, (tt % 2) * D:(tt % 2 + 1) * D], ALU.add)

                @blk.sync
                def _(s):
                    for tt in range(NT):
                        if tt >= 2:
                            s.wait_ge(sems["ve"], ve0 + 37 + (tt - 2))
                        dma(LD, s, XINB[:, (tt % 2) * D:(tt % 2 + 1) * D],
                            xin[128 * tt:128 * (tt + 1), :])
                        s.wait_ge(sems["ve"], ve0 + 37 + tt)
                        dma(LD, s, t["x2s"][128 * tt:128 * (tt + 1), :],
                            XA[:, tt * D:(tt + 1) * D])

            if debug:
                with nc.Block() as blk:
                    @blk.sync
                    def _(s):
                        dma(LD, s, dbgs["dbg_stb"][:, :], STB[:, :])
                        for k in range(8):
                            dst = dbgs["dbg_x2"][512 * k:512 * (k + 1), :].rearrange(
                                "(j p) f -> p j f", p=128)
                            src_ = XA[:, 4 * k * D:(4 * k + 4) * D].rearrange(
                                "p (j f) -> p j f", j=4)
                            dma(LD, s, dst, src_)
                        s.wait_ge(sems["ld"], LD.n)

        if upto >= 7:
            XA_scope.close()
            OT = es.enter_context(nc.sbuf_tensor("OT", [128, 8192], BF16, side="right"))
            t["OT"] = OT
            esB = ExitStack()
            QselT = esB.enter_context(nc.sbuf_tensor("QselT", [128, 8192], BF16, side="right"))
            KselT = esB.enter_context(nc.sbuf_tensor("KselT", [128, 8192], BF16, side="right"))
            VB = esB.enter_context(nc.sbuf_tensor("VB", [128, 8320], BF16, side="right"))
            esXB = ExitStack()
            xbT = esXB.enter_context(nc.sbuf_tensor("xbT", [128, 32768], BF16, side="right"))
            X2SC = sbt("X2SC", [128, 2048], F32, esB)
            TACC = sbt("TACC", [128, 16], F32, esB)
            TACCB = sbt("TACCB", [128, 16], BF16, esB)
            WRS = sbt("WRS", [128, 128], BF16, esB)
            BRC = sbt("BRC", [16, 1], F32, esB)
            BRR = sbt("BRR", [1, 16], F32, esB)
            R16C = sbt("R16C", [16, 1], F32, esB)
            R16R = sbt("R16R", [1, 16], F32, esB)
            R16RB = sbt("R16RB", [1, 16], BF16, esB)
            MB16 = sbt("MB16", [16, 16], F32, esB)
            M16 = sbt("M16", [16, 1], F32, esB)
            M16B = sbt("M16B", [16, 1], BF16, esB)
            RANK = sbt("RANK", [16, 1], F32, esB)
            S16F = sbt("S16F", [16, 4], F32, esB)
            S16B = sbt("S16B", [16, 4], BF16, esB)
            U16S = sbt("U16S", [16, 16], BF16, esB)
            IOTA4S = sbt("IOTA4S", [16, 4], F32, esB)
            xbT3 = xbT[:, :].rearrange("p (c f) -> p c f", c=8)

            with nc.Block() as blk:
                @blk.gpsimd
                def _(g):
                    dma(GLD, g, WRS[:, :], t["Wr_c"][:, :])
                    g.wait_ge(sems["gld"], GLD.n)

                @blk.sync
                def _(s):
                    dma(LD, s, BRC[:, :], t["br_c"][:, :])
                    dma(LD, s, BRR[:, :], t["br_r"][:, :])
                    dma(LD, s, U16S[:, :], t["U16"][:, :])
                    dma(LD, s, IOTA4S[:, :], t["IOTA4"][:, :])
                    s.wait_ge(sems["ld"], LD.n)

                @blk.vector
                def _(v):
                    VE.eng = v
                    VE.op(v.memset, TACC[:, :], 0.0)

            with nc.Block() as blk:
                sc0, ve0, pe0, ld0 = SC.n, VE.n, PE.n, LD.n

                @blk.sync
                def _(s):
                    for tt in range(NT):
                        qq = tt % 2
                        if tt >= 2:
                            s.wait_ge(sems["sc"], sc0 + 6 * (tt - 2) + 4)
                        dma(LD, s, X2SC[:, qq * D:(qq + 1) * D],
                            t["x2s"][128 * tt:128 * (tt + 1), :])

                @blk.scalar
                def _(s):
                    SC.eng = s
                    for tt in range(NT):
                        qq = tt % 2
                        x2t = X2SC[:, qq * D:(qq + 1) * D]
                        SC.wait(LD, ld0 + 16 * (tt + 1))
                        if tt >= 2:
                            SC.wait(VE, ve0 + 6 * (tt - 2) + 1)
                        SC.op(s.activation, SSQ[:, :], x2t, AF.Square, accum_out=STAT[:, 24:25])
                        SC.wait(VE, ve0 + 6 * tt + 4)
                        SC.op(s.activation, STAT[:, 25:26], STAT[:, 23:24], AF.Ln, bias=cp("eps"))
                        SC.op(s.activation, STAT[:, 26:27], STAT[:, 25:26], AF.Exp, scale=cp("neg05"))
                        SC.wait(VE, ve0 + 6 * tt + 5)
                        if tt >= 2:
                            SC.wait(PE, pe0 + 16 * (tt - 2) + 8)
                        SC.op(s.activation, SXN[:, qq * D:(qq + 1) * D], x2t, AF.Identity,
                              scale=STAT[:, 26:27], bias=STAT[:, 27:28])
                        SC.wait(PE, pe0 + 16 * tt + 8)
                        SC.op(s.activation, xbT3[:, 0:4, tt * 128:(tt + 1) * 128],
                              P[2 * qq][:, :], AF.Identity)
                        SC.op(s.activation, xbT3[:, 4:8, tt * 128:(tt + 1) * 128],
                              P[2 * qq + 1][:, :], AF.Identity)

                @blk.vector
                def _(v):
                    VE.eng = v
                    for tt in range(NT):
                        qq = tt % 2
                        x2t = X2SC[:, qq * D:(qq + 1) * D]
                        VE.wait(SC, sc0 + 6 * tt + 1)
                        VE.op(v.tensor_scalar, SXG[:, qq * D:(qq + 1) * D], x2t,
                              1.0, 0.0, ALU.mult, ALU.add, accum_out=STAT[:, 20:21])
                        VE.op(v.tensor_scalar, STAT[:, 21:22], STAT[:, 20:21], 1.0 / D, None, ALU.mult)
                        VE.op(v.tensor_tensor, STAT[:, 22:23], STAT[:, 21:22], STAT[:, 21:22], ALU.mult)
                        VE.op(v.scalar_tensor_tensor, STAT[:, 23:24], STAT[:, 24:25], 1.0 / D,
                              STAT[:, 22:23], ALU.mult, ALU.subtract)
                        VE.wait(SC, sc0 + 6 * tt + 3)
                        VE.op(v.scalar_tensor_tensor, STAT[:, 27:28], STAT[:, 21:22], -1.0,
                              STAT[:, 26:27], ALU.mult, ALU.mult)
                        VE.wait(PE, pe0 + 16 * tt + 16)
                        VE.op(v.tensor_tensor, TACC[:, :], TACC[:, :], P[4 + qq][0:128, 0:16], ALU.add)

                @blk.tensor
                def _(te):
                    PE.eng = te
                    for tt in range(NT):
                        qq = tt % 2
                        PE.wait(SC, sc0 + 6 * tt + 4)
                        if tt >= 2:
                            PE.wait(SC, sc0 + 6 * (tt - 1))
                            PE.wait(VE, ve0 + 6 * (tt - 1))
                        for j in range(8):
                            bank = P[2 * qq + j // 4]
                            PE.op(te.matmul, bank[:, (j % 4) * 128:(j % 4 + 1) * 128],
                                  SXN[:, qq * D + j * 128:qq * D + (j + 1) * 128],
                                  IDCS[:, :], start=True, stop=True)
                        PE.wait(SC, sc0 + 6 * (tt + 1))
                        for c in range(8):
                            PE.op(te.matmul, P[4 + qq][0:128, 0:16],
                                  xbT3[:, c, tt * 128:(tt + 1) * 128],
                                  WRS[:, c * 16:(c + 1) * 16], start=(c == 0), stop=(c == 7))

            with nc.Block() as blk:
                sc0, ve0, pe0 = SC.n, VE.n, PE.n

                @blk.scalar
                def _(s):
                    SC.eng = s
                    SC.op(s.activation, TACCB[:, :], TACC[:, :], AF.Identity)
                    SC.wait(VE, ve0 + 7)
                    SC.op(s.activation, M16B[:, :], M16[:, :], AF.Identity)
                    SC.wait(VE, ve0 + 10)
                    SC.op(s.activation, S16B[:, :], S16F[:, :], AF.Identity)

                @blk.tensor
                def _(te):
                    PE.eng = te
                    PE.wait(SC, sc0 + 1)
                    PE.op(te.matmul, P[0][0:16, 0:1], TACCB[:, :], ONESCS[:, :],
                          start=True, stop=True)
                    PE.op(te.matmul, P[1][0:1, 0:16], ONESCS[:, :], TACCB[:, :],
                          start=True, stop=True)
                    PE.wait(VE, ve0 + 4)
                    PE.op(te.matmul, P[2][0:16, 0:16], ONESRF[0:1, 0:16], R16R[:, :],
                          start=True, stop=True)
                    PE.wait(SC, sc0 + 2)
                    PE.op(te.matmul, P[3][0:16, 0:1], U16S[:, :], M16B[:, :],
                          start=True, stop=True)

                @blk.vector
                def _(v):
                    VE.eng = v
                    VE.wait(PE, pe0 + 2)
                    VE.op(v.tensor_scalar, R16C[:, :], P[0][0:16, 0:1], 1.0 / N, None, ALU.mult)
                    VE.op(v.tensor_tensor, R16C[:, :], R16C[:, :], BRC[:, :], ALU.add)
                    VE.op(v.tensor_scalar, R16R[:, :], P[1][0:1, 0:16], 1.0 / N, None, ALU.mult)
                    VE.op(v.tensor_tensor, R16R[:, :], R16R[:, :], BRR[:, :], ALU.add)
                    VE.wait(PE, pe0 + 3)
                    VE.op(v.tensor_scalar, MB16[:, :], P[2][0:16, 0:16], R16C[:, 0:1], None,
                          ALU.is_gt)
                    VE.op(v.tensor_reduce, M16[:, :], MB16[:, :], AX.X, ALU.add)
                    VE.op(v.tensor_scalar, M16[:, :], M16[:, :], 4.0, None, ALU.is_lt)
                    VE.wait(PE, pe0 + 4)
                    VE.op(v.tensor_copy, RANK[:, :], P[3][0:16, 0:1])
                    VE.op(v.tensor_scalar, S16F[:, :], IOTA4S[:, :], RANK[:, 0:1], None,
                          ALU.is_equal)
                    VE.op(v.tensor_scalar, S16F[:, :], S16F[:, :], M16[:, 0:1], None, ALU.mult)

            if debug:
                with nc.Block() as blk:
                    @blk.sync
                    def _(s):
                        dma(LD, s, dbgs["dbg_r16"][:, :], R16C[:, :])
                        dma(LD, s, dbgs["dbg_S16"][:, :], S16F[:, :])
                        s.wait_ge(sems["ld"], LD.n)
            if upto >= 8:
                SBLK = sbt("SBLK", [128, 2048], BF16, esB)
                WSELQ = sbt("WSELQ", [128, 2048], BF16, esB)
                WSELK = sbt("WSELK", [128, 2048], BF16, esB)
                WSELV = sbt("WSELV", [128, 2048], BF16, esB)
                BQSEL = sbt("BQSEL", [128, 4], F32, esB)
                BVSEL = sbt("BVSEL", [128, 256], F32, esB)
                esC = ExitStack()
                S16TB = sbt("S16TB", [4, 16], BF16, esC)
                H16B = sbt("H16B", [16, 256], BF16, esC)
                E4S = sbt("E4S", [4, 256], BF16, esC)
                GHS = sbt("GHS", [16, 1024], BF16, esC)
                IKRONS = sbt("IKRONS", [128, 256], BF16, esC)
                BQH = sbt("BQH", [128, 8], BF16, esC)
                BKH = sbt("BKH", [128, 8], BF16, esC)
                BVH = sbt("BVH", [128, 8], BF16, esC)
                BSROW = sbt("BSROW", [1, 512], F32, esC)
                BVRB = sbt("BVRB", [1, 256], BF16, esC)
                VB3 = VB[:, :].rearrange("p (tl a f) -> p tl a f", tl=32, a=4)

                with nc.Block() as blk:
                    @blk.sync
                    def _(s):
                        dma(LD, s, E4S[:, :], t["E4"][:, :])
                        dma(LD, s, GHS[:, :], t["GH"][:, :])
                        dma(LD, s, IKRONS[:, :], t["IKRON"][:, :])
                        s.wait_ge(sems["ld"], LD.n)

                    @blk.gpsimd
                    def _(g):
                        dma(GLD, g, BQH[:, :], t["bq_c"][:, :])
                        dma(GLD, g, BKH[:, :], t["bk_c"][:, :])
                        dma(GLD, g, BVH[:, :], t["bv_c"][:, :])
                        g.wait_ge(sems["gld"], GLD.n)

                    @blk.vector
                    def _(v):
                        VE.eng = v
                        VE.op(v.memset, VB[:, 64:8320:65], 1.0)

                with nc.Block() as blk:
                    sc0, ve0, pe0 = SC.n, VE.n, PE.n

                    @blk.tensor
                    def _(te):
                        PE.eng = te
                        PE.op(te.matmul, P[4][0:4, 0:16], S16B[:, :], IDCS[0:16, 0:16],
                              start=True, stop=True)
                        PE.wait(SC, sc0 + 1)
                        PE.op(te.matmul, P[5][0:16, 0:256], S16TB[:, :], E4S[:, :],
                              start=True, stop=True)
                        PE.wait(SC, sc0 + 2)
                        for tch in range(8):
                            if tch >= 2:
                                PE.wait(VE, ve0 + tch - 1)
                            PE.op(te.matmul, P[tch % 2][0:128, 0:256],
                                  GHS[:, tch * 128:(tch + 1) * 128], H16B[:, :],
                                  start=True, stop=True)
                        PE.wait(VE, ve0 + 8)
                        for c in range(8):
                            PE.op(te.matmul, P[6][0:1, 0:256], BQH[:, c:c + 1],
                                  SBLK[:, c * 256:(c + 1) * 256], start=(c == 0), stop=(c == 7))
                        for c in range(8):
                            PE.op(te.matmul, P[6][0:1, 256:512], BKH[:, c:c + 1],
                                  SBLK[:, c * 256:(c + 1) * 256], start=(c == 0), stop=(c == 7))
                        for c in range(8):
                            PE.op(te.matmul, P[7][0:1, 0:256], BVH[:, c:c + 1],
                                  SBLK[:, c * 256:(c + 1) * 256], start=(c == 0), stop=(c == 7))
                        PE.wait(VE, ve0 + 11)
                        PE.op(te.matmul, P[7][0:128, 256:512], ONESRS[0:1, 0:128], BVRB[:, :],
                              start=True, stop=True)

                    @blk.scalar
                    def _(s):
                        SC.eng = s
                        SC.wait(PE, pe0 + 1)
                        SC.op(s.activation, S16TB[:, :], P[4][0:4, 0:16], AF.Identity)
                        SC.wait(PE, pe0 + 2)
                        SC.op(s.activation, H16B[:, :], P[5][0:16, 0:256], AF.Identity)


                    @blk.vector
                    def _(v):
                        VE.eng = v
                        for tch in range(8):
                            VE.wait(PE, pe0 + 2 + tch + 1)
                            VE.op(v.tensor_tensor, SBLK[:, tch * 256:(tch + 1) * 256],
                                  P[tch % 2][0:128, 0:256], IKRONS[:, :], ALU.mult)
                        VE.wait(PE, pe0 + 34)
                        VE.op(v.tensor_copy, BSROW[0:1, 0:256], P[6][0:1, 0:256])
                        VE.op(v.tensor_copy, BSROW[0:1, 256:512], P[6][0:1, 256:512])
                        VE.op(v.tensor_copy, BVRB[:, :], P[7][0:1, 0:256])
                        VE.wait(PE, pe0 + 35)
                        VE.op(v.tensor_copy, BVSEL[:, :], P[7][0:128, 256:512])

                with nc.Block() as blk:
                    ld0 = LD.n

                    @blk.sync
                    def _(s):
                        dma(LD, s, t["rowscr"][0:1, 0:512], BSROW[:, :])
                        s.wait_ge(sems["ld"], ld0 + 16)
                        LD.n += 16
                        s.dma_start(BQSEL[:, :],
                                    t["rowscr"][0:1, 0:512].rearrange("one (c p) -> (one p) c", p=128),
                                    allow_slow_non_contiguous=True).then_inc(LD.sem, 16)
                        s.wait_ge(sems["ld"], LD.n)

                esC.close()
                esW = ExitStack()
                WTMP = esW.enter_context(nc.sbuf_tensor("WTMP", [128, 2048], BF16))
                WTT = esW.enter_context(nc.sbuf_tensor("WTT", [128, 8192], BF16))
                WTT3 = WTT[:, :].rearrange("p (e f) -> p e f", e=8)

                with nc.Block() as blk:
                    sc0, ve0, pe0, gld0 = SC.n, VE.n, PE.n, GLD.n
                    WSRC = [t["Wq"], t["Wk"], t["Wv"]]
                    WDST = [WSELQ, WSELK, WSELV]
                    SCW, PEW = 24, 128

                    @blk.gpsimd
                    def _(g):
                        for wi in range(3):
                            for c in range(8):
                                if 8 * wi + c >= 2:
                                    g.wait_ge(sems["pe"], pe0 + PEW * wi + 8 * (c - 1)
                                              if c >= 2 else pe0 + PEW * (wi - 1) + 8 * (c + 7))
                                dma(GLD, g, WTMP[:, (c % 2) * 1024:(c % 2 + 1) * 1024],
                                    WSRC[wi][128 * c:128 * (c + 1), :])

                    @blk.tensor
                    def _(te):
                        PE.eng = te
                        for wi in range(3):
                            if wi >= 1:
                                PE.wait(SC, sc0 + SCW * (wi - 1) + 24)
                            for c in range(8):
                                PE.wait(GLD, gld0 + 16 * (8 * wi + c + 1))
                                if c >= 2:
                                    PE.wait(SC, sc0 + SCW * wi + 2 * (c - 1))
                                for e in range(8):
                                    bank = P[2 * (c % 2) + e // 4]
                                    PE.op(te.matmul, bank[:, (e % 4) * 128:(e % 4 + 1) * 128],
                                          WTMP[:, (c % 2) * 1024 + e * 128:(c % 2) * 1024 + (e + 1) * 128],
                                          IDCS[:, :], start=True, stop=True)
                            PE.wait(SC, sc0 + SCW * wi + 16)
                            for m in range(8):
                                if m >= 2:
                                    PE.wait(SC, sc0 + SCW * wi + 16 + (m - 1))
                                for e in range(8):
                                    PE.op(te.matmul, P[4 + m % 2][0:128, 0:256],
                                          WTT[:, e * 1024 + m * 128:e * 1024 + (m + 1) * 128],
                                          SBLK[:, e * 256:(e + 1) * 256],
                                          start=(e == 0), stop=(e == 7))

                    @blk.scalar
                    def _(s):
                        SC.eng = s
                        for wi in range(3):
                            for c in range(8):
                                SC.wait(PE, pe0 + PEW * wi + 8 * (c + 1))
                                SC.op(s.activation, WTT3[:, 0:4, c * 128:(c + 1) * 128],
                                      P[2 * (c % 2)][:, :], AF.Identity)
                                SC.op(s.activation, WTT3[:, 4:8, c * 128:(c + 1) * 128],
                                      P[2 * (c % 2) + 1][:, :], AF.Identity)
                            for m in range(8):
                                SC.wait(PE, pe0 + PEW * wi + 64 + 8 * (m + 1))
                                SC.op(s.activation, WDST[wi][:, m * 256:(m + 1) * 256],
                                      P[4 + m % 2][0:128, 0:256], AF.Identity)

                esW.close()

                with nc.Block() as blk:
                    sc0, ve0, pe0 = SC.n, VE.n, PE.n
                    PEB, SCB, VEB = 64, 4, 4

                    @blk.tensor
                    def _(te):
                        PE.eng = te
                        for T in range(8):
                            if T >= 1:
                                PE.wait(SC, sc0 + SCB * (T - 1) + 4)
                                PE.wait(VE, ve0 + VEB * (T - 1) + 4)
                            for rc in range(2):
                                for c in range(8):
                                    PE.op(te.matmul, P[rc][:, :],
                                          WSELQ[:, c * 256 + rc * 128:c * 256 + (rc + 1) * 128],
                                          xbT[:, c * 4096 + T * 512:c * 4096 + (T + 1) * 512],
                                          start=(c == 0), stop=(c == 7))
                            for rc in range(2):
                                for c in range(8):
                                    PE.op(te.matmul, P[2 + rc][:, :],
                                          WSELK[:, c * 256 + rc * 128:c * 256 + (rc + 1) * 128],
                                          xbT[:, c * 4096 + T * 512:c * 4096 + (T + 1) * 512],
                                          start=(c == 0), stop=(c == 7))
                            for tl in range(4):
                                for c in range(8):
                                    PE.op(te.matmul,
                                          P[4 + tl // 2][:, (tl % 2) * 256:(tl % 2) * 256 + 256],
                                          xbT[:, c * 4096 + (T * 4 + tl) * 128:
                                              c * 4096 + (T * 4 + tl + 1) * 128],
                                          WSELV[:, c * 256:(c + 1) * 256],
                                          start=(c == 0), stop=(c == 7))

                    @blk.scalar
                    def _(s):
                        SC.eng = s
                        for T in range(8):
                            SC.wait(PE, pe0 + PEB * T + 16)
                            for rc in range(2):
                                SC.op(s.activation, QselT[:, rc * 4096 + T * 512:rc * 4096 + (T + 1) * 512],
                                      P[rc][:, :], AF.Identity, bias=BQSEL[:, rc:rc + 1])
                            SC.wait(PE, pe0 + PEB * T + 32)
                            for rc in range(2):
                                SC.op(s.activation, KselT[:, rc * 4096 + T * 512:rc * 4096 + (T + 1) * 512],
                                      P[2 + rc][:, :], AF.Identity, bias=BQSEL[:, 2 + rc:3 + rc])

                    @blk.vector
                    def _(v):
                        VE.eng = v
                        for T in range(8):
                            VE.wait(PE, pe0 + PEB * T + 64)
                            for tl in range(4):
                                ps = P[4 + tl // 2][:, (tl % 2) * 256:(tl % 2) * 256 + 256]
                                VE.op(v.tensor_tensor,
                                      VB3[:, T * 4 + tl, :, 0:64],
                                      ps.rearrange("p (a f) -> p a f", a=4),
                                      BVSEL[:, :].rearrange("p (a f) -> p a f", a=4), ALU.add)

                if debug:
                    with nc.Block() as blk:
                        @blk.gpsimd
                        def _(g):
                            dma(GLD, g, dbgs["dbg_qselT"][0:128, :], QselT[:, 0:4096])
                            dma(GLD, g, dbgs["dbg_qselT"][128:256, :], QselT[:, 4096:8192])
                            g.wait_ge(sems["gld"], GLD.n)

            esXB.close()
            P47es.close()
            if upto >= 9:
                PQB = esB.enter_context(nc.psum_tensor("pqb", [128, 2048], F32))
                KHC = sbt("KHC", [64, 4096], BF16, esB)
                QHC = sbt("QHC", [64, 4096], BF16, esB)
                PSLOT = sbt("PSLOT", [128, 4096], BF16, esB)
                NEGM = sbt("NEGM", [128, 4], F32, esB)
                MCT = sbt("MCT", [128, 1], F32, esB)
                MXC = sbt("MXC", [128, 1], BF16, esB)
                MM1 = sbt("MM1", [1, 8], F32, esB)
                RRF = sbt("RRF", [1, 512], F32, esB)
                RRB = sbt("RRB", [1, 512], BF16, esB)
                RBC = sbt("RBC", [64, 512], BF16, esB)
                WOS = sbt("WOS", [128, 2048], BF16, esB)
                YSC = sbt("YSC", [128, 1024], F32, esB)

                with nc.Block() as blk:
                    @blk.gpsimd
                    def _(g):
                        for rc in range(2):
                            dma(GLD, g, WOS[:, rc * 1024:(rc + 1) * 1024],
                                t["Wo"][128 * rc:128 * (rc + 1), :])
                        g.wait_ge(sems["gld"], GLD.n)

                with nc.Block() as blk:
                    sc0, ve0, pe0 = SC.n, VE.n, PE.n
                    SCH, VEH, PEH = 156, 19, 587
                    peA = lambda a: pe0 + PEH * a
                    qbase = lambda a, qb: pe0 + PEH * a + 67 + 65 * qb

                    @blk.vector
                    def _(v):
                        VE.eng = v
                        for a in range(4):
                            rc, poff = a // 2, 64 * (a % 2)
                            vb = ve0 + VEH * a
                            sb0 = sc0 + SCH * a
                            if a >= 1:
                                VE.wait(PE, peA(a))
                            VE.op(v.tensor_copy, QHC[:, :],
                                  QselT[poff:poff + 64, rc * 4096:(rc + 1) * 4096])
                            VE.op(v.tensor_copy, KHC[:, :],
                                  KselT[poff:poff + 64, rc * 4096:(rc + 1) * 4096])
                            VE.wait(PE, peA(a) + 32)
                            VE.op(v.tensor_reduce, MCT[:, :], P[0][0:128, 0:32], AX.X, ALU.max)
                            VE.op(v.tensor_copy, MXC[:, :], MCT[:, :])
                            VE.wait(PE, peA(a) + 33)
                            VE.op(v.tensor_reduce, MM1[0:1, 0:1], P[1][0:1, 0:128], AX.X,
                                  ALU.max)
                            VE.wait(PE, peA(a) + 65)
                            VE.op(v.tensor_reduce, MCT[:, :], P[0][0:128, 0:32], AX.X, ALU.max)
                            VE.op(v.tensor_copy, MXC[:, :], MCT[:, :])
                            VE.wait(PE, peA(a) + 66)
                            VE.op(v.tensor_reduce, MM1[0:1, 1:2], P[1][0:1, 0:128], AX.X,
                                  ALU.max)
                            VE.op(v.tensor_tensor, MM1[0:1, 2:3], MM1[0:1, 0:1], MM1[0:1, 1:2],
                                  ALU.mult)
                            VE.wait(SC, sb0 + 4)
                            VE.op(v.tensor_scalar, MM1[0:1, 5:6], MM1[0:1, 4:5], -1.0, None,
                                  ALU.mult)
                            VE.wait(PE, peA(a) + 67)
                            VE.op(v.tensor_copy, NEGM[:, a:a + 1], P[2][0:128, 0:1])
                            for qb in range(8):
                                VE.wait(SC, sb0 + 4 + 19 * qb + 19)
                                VE.op(v.tensor_tensor,
                                      OT[poff:poff + 64,
                                         rc * 4096 + qb * 512:rc * 4096 + (qb + 1) * 512],
                                      P[3][0:64, 0:512], RBC[:, :], ALU.mult)

                    @blk.scalar
                    def _(s):
                        SC.eng = s
                        for a in range(4):
                            vb = ve0 + VEH * a
                            sb0 = sc0 + SCH * a
                            SC.wait(VE, vb + 1)
                            SC.op(s.activation, TB[0:64, :], QHC[:, :], AF.Square)
                            SC.wait(VE, vb + 2)
                            SC.wait(PE, peA(a) + 32)
                            SC.op(s.activation, TB[0:64, :], KHC[:, :], AF.Square)
                            SC.wait(VE, vb + 9)
                            SC.op(s.activation, MM1[0:1, 3:4], MM1[0:1, 2:3], AF.Ln)
                            SC.op(s.activation, MM1[0:1, 4:5], MM1[0:1, 3:4], AF.Exp,
                                  scale=CPO[0:1, CP["half"]:CP["half"] + 1],
                                  bias=CPO[0:1, CP["negln8"]:CP["negln8"] + 1])
                            SC.wait(VE, vb + 11)
                            for qb in range(8):
                                base = qbase(a, qb)
                                for p in range(16):
                                    SC.wait(PE, base + (2 if p == 0 else 4 if p == 1 else 4 * p))
                                    SC.op(s.activation,
                                          PSLOT[:, (p % 4) * 1024:(p % 4) * 1024 + 1024],
                                          PQB.ap()[:, (p % 2) * 1024:(p % 2) * 1024 + 1024],
                                          AF.Exp, bias=NEGM[:, a:a + 1],
                                          scale=cp("qkscale"))
                                SC.wait(PE, base + 64)
                                SC.op(s.activation, RRF[:, :], P[3][64:65, 0:512], AF.Ln)
                                SC.op(s.activation, RRB[:, :], RRF[:, :], AF.Exp,
                                      scale=CPO[0:1, CP["neg1"]:CP["neg1"] + 1])
                                SC.wait(PE, base + 65)
                                SC.op(s.activation, RBC[:, :], P[2][0:64, 0:512], AF.Identity)

                    @blk.tensor
                    def _(te):
                        PE.eng = te
                        for a in range(4):
                            vb = ve0 + VEH * a
                            sb0 = sc0 + SCH * a
                            PE.wait(SC, sb0 + 1)
                            for tl in range(32):
                                PE.op(te.matmul, P[0][0:128, tl:tl + 1],
                                      TB[0:64, tl * 128:(tl + 1) * 128], ONESCS[0:64, 0:1],
                                      start=True, stop=True)
                            PE.wait(VE, vb + 4)
                            PE.op(te.matmul, P[1][0:1, 0:128], MXC[:, :], IDCS[:, :],
                                  start=True, stop=True)
                            PE.wait(SC, sb0 + 2)
                            for tl in range(32):
                                PE.op(te.matmul, P[0][0:128, tl:tl + 1],
                                      TB[0:64, tl * 128:(tl + 1) * 128], ONESCS[0:64, 0:1],
                                      start=True, stop=True)
                            PE.wait(VE, vb + 7)
                            PE.op(te.matmul, P[1][0:1, 0:128], MXC[:, :], IDCS[:, :],
                                  start=True, stop=True)
                            PE.wait(VE, vb + 10)
                            PE.op(te.matmul, P[2][0:128, 0:1], ONESRF[0:1, :], MM1[0:1, 5:6],
                                  start=True, stop=True)
                            for qb in range(8):
                                base = qbase(a, qb)
                                sqb = sb0 + 4 + 19 * qb

                                def smm(kt):
                                    PE.op(te.matmul,
                                          PQB.ap()[:, (kt % 4) * 512:(kt % 4 + 1) * 512],
                                          KHC[0:64, kt * 128:(kt + 1) * 128],
                                          QHC[0:64, qb * 512:(qb + 1) * 512],
                                          start=True, stop=True)

                                def omm(kt):
                                    sl = ((kt // 2) % 4) * 1024 + (kt % 2) * 512
                                    PE.op(te.matmul, P[3][0:65, 0:512],
                                          VB[:, kt * 260 + 65 * a:kt * 260 + 65 * (a + 1)],
                                          PSLOT[:, sl:sl + 512],
                                          start=(kt == 0), stop=(kt == 31))

                                smm(0); smm(1); smm(2); smm(3)
                                for p in range(14):
                                    PE.wait(SC, sqb + p + 1)
                                    if p == 0 and qb >= 1:
                                        PE.wait(VE, vb + 11 + qb)
                                    omm(2 * p); omm(2 * p + 1)
                                    smm(2 * p + 4); smm(2 * p + 5)
                                PE.wait(SC, sqb + 15)
                                omm(28); omm(29)
                                PE.wait(SC, sqb + 16)
                                omm(30); omm(31)
                                PE.wait(SC, sqb + 18)
                                PE.op(te.matmul, P[2][0:64, 0:512], ONESRS[0:1, 0:64], RRB[:, :],
                                      start=True, stop=True)

            if upto >= 9:
                emit_row_broadcast("bo", BCAST)
                YG = sbt("YG", [128, 8192], BF16, esB)
                YSC2 = sbt("YSC2", [128, 2048], F32, esB)
                STB6 = sbt("STB6", [128, 288], F32, esB)
                B6SUM0 = STB6[:, 0:32]
                B6SUM1 = STB6[:, 32:64]
                B6SSA = STB6[:, 64:96]
                B6MEAN = STB6[:, 96:128]
                B6T = STB6[:, 128:160]
                B6RSTD = STB6[:, 160:192]
                B6NB = STB6[:, 192:224]
                B6T2 = STB6[:, 224:256]
                B6VAR = STB6[:, 256:288]
                LNY = [SSQ[:, 0:1024], SXG[:, 0:2048].bitcast(F32)]
                with nc.Block() as blk:
                    sc0, ve0, pe0, ld0 = SC.n, VE.n, PE.n, LD.n
                    NG = NT // 8

                    @blk.sync
                    def _(s):
                        for g in range(NG):
                            vg, lg = ve0 + 45 * g, ld0 + 256 * g
                            for i in range(8):
                                tt = 8 * g + i
                                if i >= 2:
                                    s.wait_ge(sems["ve"], vg + 36 + i)
                                elif g >= 1:
                                    s.wait_ge(sems["ve"], vg + i)
                                dma(LD, s, X2SC[:, (i % 2) * D:(i % 2 + 1) * D],
                                    t["x2s"][128 * tt:128 * (tt + 1), :])
                                s.wait_ge(sems["ve"], vg + 38 + i)
                                dma(LD, s, out[128 * tt:128 * (tt + 1), :],
                                    YSC2[:, (i % 2) * D:(i % 2 + 1) * D])
                        s.wait_ge(sems["ld"], LD.n)

                    @blk.tensor
                    def _(te):
                        PE.eng = te
                        for g in range(NG):
                            vg = ve0 + 45 * g
                            for i in range(8):
                                tt = 8 * g + i
                                if i >= 2:
                                    PE.wait(VE, vg + 4 * (i - 2) + 3)
                                elif g >= 1:
                                    PE.wait(VE, vg - 45 + 27 + 4 * i)
                                for j in range(2):
                                    for rc in range(2):
                                        PE.op(te.matmul, P[2 * (i % 2) + j][:, :],
                                              OT[:, rc * 4096 + tt * 128:rc * 4096 + (tt + 1) * 128],
                                              WOS[:, rc * 1024 + j * 512:rc * 1024 + (j + 1) * 512],
                                              start=(rc == 0), stop=(rc == 1))

                    @blk.scalar
                    def _(s):
                        SC.eng = s
                        for g in range(NG):
                            sg, vg = sc0 + 18 * g, ve0 + 45 * g
                            for i in range(8):
                                SC.wait(VE, vg + 4 * i + 4)
                                SC.op(s.activation, YSC[:, :], YG[:, i * 1024:(i + 1) * 1024],
                                      AF.Square, accum_out=B6SSA[:, i:i + 1])
                            SC.wait(VE, vg + 36)
                            SC.op(s.activation, B6T2[:, :], B6VAR[:, :], AF.Ln, bias=cp("eps"))
                            SC.op(s.activation, B6RSTD[:, :], B6T2[:, :], AF.Exp, scale=cp("neg05"))
                            for i in range(8):
                                SC.wait(VE, vg + 37 if i <= 1 else vg + 36 + i)
                                SC.op(s.activation, LNY[i % 2], YG[:, i * 1024:(i + 1) * 1024],
                                      AF.Identity, scale=B6RSTD[:, i:i + 1], bias=B6NB[:, i:i + 1])

                    @blk.vector
                    def _(v):
                        VE.eng = v
                        for g in range(NG):
                            sg, vg, pg, lg = sc0 + 18 * g, ve0 + 45 * g, pe0 + 32 * g, ld0 + 256 * g
                            for i in range(8):
                                VE.wait(PE, pg + 4 * i + 4)
                                if g >= 1:
                                    VE.wait(SC, sg - 18 + 11 + i)
                                for j in range(2):
                                    VE.op(v.tensor_tensor,
                                          YG[:, i * 1024 + j * 512:i * 1024 + (j + 1) * 512],
                                          P[2 * (i % 2) + j][:, :],
                                          BCAST[:, j * 512:(j + 1) * 512], ALU.add)
                                    VE.op(v.tensor_scalar, SXN[:, 0:512],
                                          YG[:, i * 1024 + j * 512:i * 1024 + (j + 1) * 512],
                                          1.0, 0.0, ALU.mult, ALU.add,
                                          accum_out=(B6SUM0 if j == 0 else B6SUM1)[:, i:i + 1])
                            VE.op(v.tensor_tensor, B6SUM0[:, :], B6SUM0[:, :], B6SUM1[:, :], ALU.add)
                            VE.op(v.tensor_scalar, B6MEAN[:, :], B6SUM0[:, :], 1.0 / D, None, ALU.mult)
                            VE.op(v.tensor_tensor, B6T[:, :], B6MEAN[:, :], B6MEAN[:, :], ALU.mult)
                            VE.wait(SC, sg + 8)
                            VE.op(v.scalar_tensor_tensor, B6VAR[:, :], B6SSA[:, :], 1.0 / D,
                                  B6T[:, :], ALU.mult, ALU.subtract)
                            VE.wait(SC, sg + 10)
                            VE.op(v.scalar_tensor_tensor, B6NB[:, :], B6MEAN[:, :], -1.0,
                                  B6RSTD[:, :], ALU.mult, ALU.mult)
                            for i in range(8):
                                VE.wait(SC, sg + 11 + i)
                                VE.wait(LD, lg + 32 * i + 16)
                                if i >= 2:
                                    VE.wait(LD, lg + 32 * (i - 2) + 32)
                                VE.op(v.tensor_tensor, YSC2[:, (i % 2) * D:(i % 2 + 1) * D],
                                      X2SC[:, (i % 2) * D:(i % 2 + 1) * D],
                                      LNY[i % 2], ALU.add)

            if debug and upto >= 9:
                with nc.Block() as blk:
                    @blk.gpsimd
                    def _(g):
                        dma(GLD, g, dbgs["dbg_oT"][0:128, :], OT[:, 0:4096])
                        dma(GLD, g, dbgs["dbg_oT"][128:256, :], OT[:, 4096:8192])
                        g.wait_ge(sems["gld"], GLD.n)

            esB.close()

        if upto < 9:
            with nc.Block() as blk:
                @blk.sync
                def _(s):
                    if upto >= 7:
                        dma(LD, s, out[:, :], t["x2s"][:, :])
                    else:
                        for k in range(8):
                            dst = out[512 * k:512 * (k + 1), :].rearrange("(j p) f -> p j f", p=128)
                            src_ = XA[:, 4 * k * D:(4 * k + 4) * D].rearrange("p (j f) -> p j f", j=4)
                            dma(LD, s, dst, src_)
                    s.wait_ge(sems["ld"], LD.n)
        if upto < 7:
            XA_scope.close()

    for nm, e in [("PE", _ENGS[0]), ("VE", _ENGS[1]), ("SC", _ENGS[2]), ("GP", _ENGS[3]),
                  ("LD", _ENGS[4]), ("GLD", _ENGS[5])]:
        if e.maxw > e.n:
            raise RuntimeError(f"DEADLOCK: {nm} waited {e.maxw} > final {e.n}")
    return nc, t, sems


def host_consts():
    c = {}
    c["IDC"] = np.eye(128).astype(bfd)
    c["MASKETA"] = (np.triu(np.ones((128, 128), np.float32)) * ETA)
    c["MASKETA4"] = np.tile(c["MASKETA"], (1, 4))
    c["U16"] = np.triu(np.ones((16, 16), np.float32), 1).astype(bfd)
    c["IOTA4"] = np.broadcast_to(np.arange(4, dtype=np.float32), (16, 4)).copy()
    e4 = np.zeros((4, 256), np.float32)
    for a in range(4):
        e4[a, 64 * a:64 * (a + 1)] = 1.0
    c["E4"] = e4.astype(bfd)
    gh = np.zeros((16, 1024), np.float32)
    for m_ in range(1024):
        gh[2 * (m_ // 128) + (m_ % 128) // 64, m_] = 1.0
    c["GH"] = gh.astype(bfd)
    ik = np.zeros((128, 256), np.float32)
    for r in range(128):
        for a in range(4):
            ik[r, 64 * a + (r % 64)] = 1.0
    c["IKRON"] = ik.astype(bfd)
    c["ONESC"] = np.ones((128, 1), np.float32).astype(bfd)
    c["ONESCF"] = np.ones((128, 1), np.float32)
    c["ONESRF"] = np.ones((1, 128), np.float32)
    c["ONESR"] = np.ones((1, 512), np.float32).astype(bfd)
    c["CPOOL"] = np.broadcast_to(np.array(CP_VALS, np.float32), (128, len(CP_VALS))).copy()
    return c


def host_weights(i):
    w = {}
    for nm, key in [("ln1g", "L0_attn_ln1_g"), ("ln1b", "L0_attn_ln1_b"),
                    ("ln2g", "L0_attn_ln2_g"), ("ln2b", "L0_attn_ln2_b"),
                    ("ln3g", "L0_attn_ln3_g"), ("ln3b", "L0_attn_ln3_b")]:
        w[nm] = np.ascontiguousarray(np.asarray(i[key], np.float32).reshape(1, D))
    wp = np.asarray(i["L0_attn_w_pot"], np.float32)
    w["w_potV"] = np.ascontiguousarray(wp[:, 0].reshape(8, 128).T)
    w["ln1g_c"] = np.ascontiguousarray(np.asarray(i["L0_attn_ln1_g"], np.float32).reshape(8, 128).T)
    w["ln1b_c"] = np.ascontiguousarray(np.asarray(i["L0_attn_ln1_b"], np.float32).reshape(8, 128).T)
    w["w_potP"] = np.ascontiguousarray(wp[:, 1].reshape(8, 128).T)
    w["b_pot"] = np.asarray(i["L0_attn_b_pot"], np.float32).reshape(1, 2)
    w["w_bk"] = np.asarray(i["L0_attn_w_bk"], np.float32)
    w["b_bk"] = np.asarray(i["L0_attn_b_bk"], np.float32).reshape(1, D)
    w["w_qkv"] = np.asarray(i["L0_attn_w_qkv"], np.float32)
    bq = np.asarray(i["L0_attn_b_qkv"], np.float32)
    w["b_qkv"] = np.ascontiguousarray(bq.reshape(12, 128).T)
    w["b_qkv_v"] = np.ascontiguousarray(bq[1024:1536].reshape(1, 512))
    w["b_qkv_k"] = np.ascontiguousarray(bq[512:1024].reshape(1, 512))
    w["w_out"] = np.asarray(i["L0_attn_w_out"], np.float32)
    w["b_out"] = np.asarray(i["L0_attn_b_out"], np.float32).reshape(1, D)
    w["w_ff1"] = np.asarray(i["L0_attn_w_ff1"], np.float32)
    w["b_ff1"] = np.ascontiguousarray(np.asarray(i["L0_attn_b_ff1"], np.float32).reshape(32, 128).T)
    w["w_ff2"] = np.asarray(i["L0_attn_w_ff2"], np.float32)
    w["b_ff2"] = np.asarray(i["L0_attn_b_ff2"], np.float32).reshape(1, D)
    w["Wq"] = np.asarray(i["L0_ffn_Wq"], np.float32)
    w["Wk"] = np.asarray(i["L0_ffn_Wk"], np.float32)
    w["Wv"] = np.asarray(i["L0_ffn_Wv"], np.float32)
    w["bq_c"] = np.ascontiguousarray(np.asarray(i["L0_ffn_bq"], np.float32).reshape(8, 128).T)
    w["bk_c"] = np.ascontiguousarray(np.asarray(i["L0_ffn_bk"], np.float32).reshape(8, 128).T)
    w["bv_r"] = np.asarray(i["L0_ffn_bv"], np.float32).reshape(1, D)
    w["Wr"] = np.asarray(i["L0_ffn_Wr"], np.float32)
    w["br_c"] = np.asarray(i["L0_ffn_br"], np.float32).reshape(16, 1)
    w["br_r"] = np.asarray(i["L0_ffn_br"], np.float32).reshape(1, 16)
    w["Wr_c"] = np.ascontiguousarray(
        np.asarray(i["L0_ffn_Wr"], np.float32).reshape(8, 128, 16).transpose(1, 0, 2).reshape(128, 128))
    w["bv_c"] = np.ascontiguousarray(np.asarray(i["L0_ffn_bv"], np.float32).reshape(8, 128).T)
    w["Wo"] = np.asarray(i["L0_ffn_Wo"], np.float32)
    w["bo"] = np.asarray(i["L0_ffn_bo"], np.float32).reshape(1, D)
    return w


def kernel(**inputs):
    nc, t, sems = build_program(debug=False)
    consts = host_consts()
    w = host_weights(inputs)
    x = np.ascontiguousarray(np.asarray(inputs["L0_attn_x"], np.float32))
    in_maps = []
    for core in range(8):
        m = {"xin": np.ascontiguousarray(x[core])}
        m.update(consts)
        m.update(w)
        in_maps.append(m)
    res = run_bass_kernel_spmd(nc, in_maps, core_ids=list(range(8)))
    return np.stack([np.asarray(r["out"], np.float32) for r in res.results])
